# version 19
# baseline (speedup 1.0000x reference)
import contextlib
import math

import numpy as np
import ml_dtypes

import concourse.bass as bass
import concourse.mybir as mybir
from concourse.bass_isa import ReduceOp

F32 = mybir.dt.float32
BF16 = mybir.dt.bfloat16
I32 = mybir.dt.int32
Act = mybir.ActivationFunctionType
Op = mybir.AluOpType
AX = mybir.AxisListType

LN_EPS = 1e-5
ALL_SEMS = ("sLD", "sST", "sGD", "sPE", "sDV", "sAC", "sGP", "sCC", "sWQ",
            "sWR", "sWS", "sWT")


class Builder:
    def __init__(self, L, NB, TT=512, n_cores=8):
        self.L, self.NB, self.TT, self.n_cores = L, NB, TT, n_cores
        self.D, self.DC, self.H, self.HC = 1024, 8, 4096, 32
        self.T = NB * L
        self.NT = self.T // 128
        self.NW = self.T // TT
        self.LC = L // 128
        self.LW = L // TT
        self.TC = TT // 128
        assert L % TT == 0 and TT % 128 == 0 and TT <= 512

    def sem_new(self, *names):
        for n in names:
            self.sems[n] = self.es.enter_context(self.nc.semaphore(n))
            self.cnt[n] = 0

    def inc(self, instr, s, k=1):
        instr.then_inc(self.sems[s], k)
        self.cnt[s] += k
        return self.cnt[s]

    def w(self, eng, s, val=None):
        if val is None:
            val = self.cnt[s]
        assert val <= self.cnt[s], (s, val, self.cnt[s])
        if val > 0:
            eng.wait_ge(self.sems[s], val)

    def sec(self, eng_name, fn):
        getattr(self.block, eng_name)(fn)

    def join_all(self):
        snap = dict(self.cnt)
        def mk():
            def f(e):
                for s in ALL_SEMS:
                    if snap[s] > 0:
                        e.wait_ge(self.sems[s], snap[s])
            return f
        for e in ("sync", "scalar", "vector", "tensor", "gpsimd"):
            self.sec(e, mk())

    def chain(self, v):
        def step(instr):
            self.inc(instr, "sDV")
            v.wait_ge(self.sems["sDV"], self.cnt["sDV"])
        return step

    def build(self):
        L, NB, TT, D, DC, H, HC, T, NT = (self.L, self.NB, self.TT, self.D,
                                          self.DC, self.H, self.HC, self.T,
                                          self.NT)
        nc = bass.Bass("TRN2", target_bir_lowering=False)
        self.nc = nc
        dt = nc.dram_tensor
        self.NWq = self.NW
        self.xT = dt("xT", [D, T], F32, kind="ExternalInput").ap()
        self.qkvwT = dt("qkvwT", [D, 3 * D], F32, kind="ExternalInput").ap()
        self.projwT = dt("projwT", [D, D], F32, kind="ExternalInput").ap()
        self.w1T = dt("w1T", [D, H], F32, kind="ExternalInput").ap()
        self.w2T = dt("w2T", [H, D], F32, kind="ExternalInput").ap()
        self.nwr_h = dt("nwr", [1, D], F32, kind="ExternalInput").ap()
        self.nbr_h = dt("nbr", [1, D], F32, kind="ExternalInput").ap()
        self.qkvbr_h = dt("qkvbr", [1, 3 * D], F32, kind="ExternalInput").ap()
        self.projbr_h = dt("projbr", [1, D], F32, kind="ExternalInput").ap()
        self.b1r_h = dt("b1r", [1, H], F32, kind="ExternalInput").ap()
        self.b2r_h = dt("b2r", [1, D], F32, kind="ExternalInput").ap()
        self.identb_h = dt("identb", [128, 128], BF16, kind="ExternalInput").ap()
        self.identf_h = dt("identf", [128, 128], F32, kind="ExternalInput").ap()
        self.outT = dt("outT", [D, T], F32, kind="ExternalOutput").ap()
        self.x1h = dt("x1h", [D, T], F32).ap()
        self.gh = dt("gh", [128, self.NW, self.HC, self.TT], F32).ap()
        self.vh = dt("vh", [D, L], BF16).ap()
        self.w1qh = dt("w1qh", [D, H], BF16).ap()
        self.w2qh = dt("w2qh", [H, D], BF16).ap()
        self.cc_in1 = dt("cc_in1", [1, 2], F32)
        self.cc_out1 = dt("cc_out1", [1, 2], F32)
        self.cc_in2 = dt("cc_in2", [1, 2], F32)
        self.cc_out2 = dt("cc_out2", [1, 2], F32)

        self.xTr = self.xT.rearrange("(c p) t -> p c t", p=128)
        self.x1r = self.x1h.rearrange("(c p) t -> p c t", p=128)
        self.vhr = self.vh.rearrange("(c p) t -> p c t", p=128)
        self.outr = self.outT.rearrange("(c p) t -> p c t", p=128)
        self.w1Tq = self.w1T.rearrange("(c p) (x h) -> p c x h", p=128, h=512)
        self.w2Tq = self.w2T.rearrange("(c p) (x d) -> p c x d", p=128, d=512)
        self.w1qhw = self.w1qh.rearrange("(c p) (x h) -> p c x h", p=128, h=512)
        self.w2qhw = self.w2qh.rearrange("(c p) (x d) -> p c x d", p=128, d=512)
        self.w1qhr = self.w1qh.rearrange("(c p) h -> p c h", p=128)

        es = contextlib.ExitStack()
        self.es = es
        with es:
            E = es.enter_context
            self.sems, self.cnt = {}, {}
            self.sem_new(*ALL_SEMS)

            self.identb = E(nc.sbuf_tensor("identb_s", [128, 128], BF16))
            self.identf = E(nc.sbuf_tensor("identf_s", [128, 128], F32))
            self.stage = E(nc.sbuf_tensor("stage", [128, 128], F32))
            self.stage2 = E(nc.sbuf_tensor("stage2", [128, 128], BF16))
            self.onesrowf = E(nc.sbuf_tensor("onesrowf", [1, 128], F32))
            self.e1row = E(nc.sbuf_tensor("e1row", [1, 2], F32))
            self.nwnb_colf = E(nc.sbuf_tensor("nwnb_colf", [128, 2 * DC], F32))
            self.b1cols = E(nc.sbuf_tensor("b1cols", [128, HC], F32))
            self.b2cols = E(nc.sbuf_tensor("b2cols", [128, DC], F32))
            tm = {}
            for name in ("sum1", "ssq1", "m1", "v1", "alpha", "invalpha",
                         "r1", "kscale", "tmp0", "tmp1", "tmp2",
                         "sum3", "ssq3", "m3", "r3", "xmin_tm", "xmax_tm",
                         "qmin_tm", "qmax_tm"):
                tm[name] = E(nc.sbuf_tensor("tm_" + name, [128, NT], F32))
            self.tm = tm
            self.onesNT = E(nc.sbuf_tensor("onesNT", [128, NT], F32))
            self.ones_col_b = E(nc.sbuf_tensor("ones_col_b", [128, 1], BF16))
            sc = {}
            for name in ("s1", "zp1", "lo1", "hi1", "rs1", "s1sw1",
                         "s2", "zp2", "lo2", "hi2", "rs2", "s2sw2",
                         "sw1", "rsw1", "sw2", "rsw2", "t0", "t1", "t2",
                         "onef"):
                sc[name] = E(nc.sbuf_tensor("sc_" + name, [1, 1], F32))
            sc["ti"] = E(nc.sbuf_tensor("sc_ti", [1, 1], I32))
            self.sc = sc
            bc = {}
            for name in ("lo1", "hi1", "lo2", "hi2", "s1sw1", "s2sw2",
                         "rsw1", "rsw2", "rs1", "rs2"):
                bc[name] = E(nc.sbuf_tensor("bc_" + name, [128, 1], F32))
            self.bc = bc
            self.wqrow = E(nc.sbuf_tensor("wqrow", [1, 2], F32))
            self.ccs = E(nc.sbuf_tensor("ccs", [1, 2], F32))
            self.cc2 = E(nc.sbuf_tensor("cc2", [2, 1], F32))

            self.block = E(nc.Block())

            self.phase0_prep()
            for b in range(NB):
                self.phase1_attn(b)
            self.phase2_stats3()
            self.phase3_ffn1()
            self.phase4_ffn2()
            self.finish()
        return nc

    def open_pool(self, stack):
        S = stack.enter_context
        self.ps_small = S(self.nc.psum_tensor(f"ps_sm{self.nc.next_id()}",
                                              [128, 128], F32))
        self.ps_bk = [S(self.nc.psum_tensor(f"ps_bk{self.nc.next_id()}",
                                            [128, 512], F32)) for _ in range(2)]

    def row_from_tm(self, name, dst, part, scale=None, f32=False):
        nc, sec, inc, w, NT = self.nc, self.sec, self.inc, self.w, self.NT
        src = self.tm[name]
        if scale is not None:
            def pre(v):
                w(v, "sDV")
                inc(v.tensor_scalar(out=self.tm["tmp0"][:, :], in0=src[:, :],
                                    scalar1=scale, scalar2=None, op0=Op.mult),
                    "sDV")
            sec("vector", pre)
            src = self.tm["tmp0"]
        def tr(t):
            w(t, "sDV")
            w(t, "sLD")
            inc(t.transpose(self.ps_small[0:NT, :], src[:, 0:NT],
                            self.identf[:, :]), "sPE")
        sec("tensor", tr)
        def ev(v):
            w(v, "sPE")
            w(v, "sLD")
            w(v, "sDV")
            inc(v.tensor_copy(out=self.stage[0:NT, :],
                              in_=self.ps_small[0:NT, :]), "sDV")
            if not f32:
                v.wait_ge(self.sems["sDV"], self.cnt["sDV"])
                inc(v.tensor_copy(out=self.stage2[0:NT, :],
                                  in_=self.stage[0:NT, :]), "sDV")
        sec("vector", ev)
        def pk(s):
            w(s, "sDV")
            srcb = self.stage[0:NT, :] if f32 else self.stage2[0:NT, :]
            w(s, "sLD")
            inc(s.dma_start(out=dst[part:part + 1, :], in_=srcb), "sLD", 16)
        sec("sync", pk)

    def bcast_from_row(self, row, dst):
        nc, sec, inc, w, T = self.nc, self.sec, self.inc, self.w, self.T
        for i in range(T // 512):
            def mm(t, i=i):
                w(t, "sLD")
                w(t, "sDV")
                inc(t.matmul(self.ps_bk[i % 2][:, :], self.onesrowf[:, :],
                             row[:, i * 512:(i + 1) * 512],
                             start=True, stop=True), "sPE")
            sec("tensor", mm)
            def ev(v, i=i):
                w(v, "sPE")
                inc(v.tensor_copy(out=dst[:, i * 512:(i + 1) * 512],
                                  in_=self.ps_bk[i % 2][:, :]), "sDV")
            sec("vector", ev)

    def bcast_scalar(self, name):
        nc, sec, inc, w = self.nc, self.sec, self.inc, self.w
        def mm(t):
            w(t, "sDV")
            inc(t.matmul(self.ps_small[:, 0:1], self.onesrowf[:, :],
                         self.sc[name][:, :], start=True, stop=True), "sPE")
        sec("tensor", mm)
        def ev(v):
            w(v, "sPE")
            inc(v.tensor_copy(out=self.bc[name][:, :],
                              in_=self.ps_small[:, 0:1]), "sDV")
        sec("vector", ev)

    def allreduce(self, cc_in, cc_out):
        sec, inc, w = self.sec, self.inc, self.w
        def cc(g):
            w(g, "sDV")
            w(g, "sLD")
            w(g, "sGD")
            inc(g.dma_start(out=cc_in.ap(), in_=self.ccs[:, :]), "sGD", 16)
            g.wait_ge(self.sems["sGD"], self.cnt["sGD"])
            inc(g.collective_compute(
                "AllReduce", Op.max,
                replica_groups=[list(range(self.n_cores))],
                ins=[cc_in.ap().opt()], outs=[cc_out.ap().opt()]), "sCC")
            g.wait_ge(self.sems["sCC"], self.cnt["sCC"])
            w(g, "sGD")
            inc(g.dma_start(out=self.ccs[:, :], in_=cc_out.ap()), "sGD", 16)
        sec("gpsimd", cc)

    def quant_scale_math(self, s_, zp_, lo_, hi_, rs_):
        sec, inc, w, sc = self.sec, self.inc, self.w, self.sc
        def vs(v):
            w(v, "sGD")
            w(v, "sDV")
            step = self.chain(v)
            step(v.tensor_tensor(out=sc["t0"][:, :], in0=self.ccs[:, 0:1],
                                 in1=self.ccs[:, 1:2], op=Op.add))
            step(v.tensor_scalar(out=sc[s_][:, :], in0=sc["t0"][:, :],
                                 scalar1=float(np.float32(1.0) / np.float32(255.0)),
                                 scalar2=1e-8, op0=Op.mult, op1=Op.max))
            step(v.reciprocal(out=sc[rs_][:, :], in_=sc[s_][:, :]))
            step(v.tensor_tensor(out=sc["t1"][:, :], in0=self.ccs[:, 0:1],
                                 in1=sc[rs_][:, :], op=Op.mult))
            step(v.tensor_copy(out=sc["ti"][:, :], in_=sc["t1"][:, :]))
            step(v.tensor_copy(out=sc["t1"][:, :], in_=sc["ti"][:, :]))
            step(v.tensor_scalar(out=sc[zp_][:, :], in0=sc["t1"][:, :],
                                 scalar1=-128.0, scalar2=-128.0,
                                 op0=Op.add, op1=Op.max))
            step(v.tensor_scalar(out=sc[zp_][:, :], in0=sc[zp_][:, :],
                                 scalar1=127.0, scalar2=None, op0=Op.min))
            step(v.tensor_scalar(out=sc[lo_][:, :], in0=sc[zp_][:, :],
                                 scalar1=-1.0, scalar2=-128.0,
                                 op0=Op.mult, op1=Op.add))
            step(v.tensor_scalar(out=sc[hi_][:, :], in0=sc[zp_][:, :],
                                 scalar1=-1.0, scalar2=127.0,
                                 op0=Op.mult, op1=Op.add))
        sec("vector", vs)
        for n in (lo_, hi_, rs_):
            self.bcast_scalar(n)

    def sqrt_newton(self, v_in, out_sqrt):
        sec, inc, w, tm = self.sec, self.inc, self.w, self.tm
        def sq(s):
            w(s, "sDV")
            inc(s.activation(out=tm["tmp2"][:, :], in_=tm[v_in][:, :],
                             func=Act.Sqrt), "sAC")
        sec("scalar", sq)
        def vm(v):
            w(v, "sAC")
            w(v, "sDV")
            step = self.chain(v)
            step(v.reciprocal(out=tm[out_sqrt][:, :], in_=tm["tmp2"][:, :]))
            step(v.tensor_tensor(out=tm[out_sqrt][:, :], in0=tm[out_sqrt][:, :],
                                 in1=tm[v_in][:, :], op=Op.mult))
            step(v.tensor_tensor(out=tm[out_sqrt][:, :], in0=tm[out_sqrt][:, :],
                                 in1=tm["tmp2"][:, :], op=Op.add))
            step(v.tensor_scalar(out=tm[out_sqrt][:, :], in0=tm[out_sqrt][:, :],
                                 scalar1=0.5, scalar2=None, op0=Op.mult))
        sec("vector", vm)

    def stats_pass(self, src_r, sum_name, ssq_name, minmax=None):
        nc, sec, inc, w = self.nc, self.sec, self.inc, self.w
        DC, TT, NW, TC, NT = self.DC, self.TT, self.NW, self.TC, self.NT
        with contextlib.ExitStack() as ses:
            S = ses.enter_context
            xf = [S(nc.sbuf_tensor(f"st_xf{nc.next_id()}", [128, DC, TT], F32))
                  for _ in range(2)]
            xb = [S(nc.sbuf_tensor(f"st_xb{nc.next_id()}", [128, DC, TT], BF16))
                  for _ in range(2)]
            sq = [S(nc.sbuf_tensor(f"st_sq{nc.next_id()}", [128, DC, TT], BF16))
                  for _ in range(2)]
            ps_sum = S(nc.psum_tensor(f"ps_su{nc.next_id()}", [128, NT], F32))
            ps_ssq = S(nc.psum_tensor(f"ps_sq{nc.next_id()}", [128, NT], F32))

            def ld0(s):
                w(s, "sPE")
                w(s, "sDV")
                w(s, "sGD")
                inc(s.dma_start(out=xf[0][:, :, :], in_=src_r[:, :, 0:TT]),
                    "sGD", 16)
            sec("gpsimd", ld0)
            ld_w = [self.cnt["sGD"]]
            dv_win, ac_win, pe_win, dv_cast = [], [], [], []
            for wi in range(NW):
                if wi + 1 < NW:
                    def ldn(s, wi=wi):
                        if wi >= 1:
                            w(s, "sDV", dv_win[wi - 1])
                            w(s, "sAC", ac_win[wi - 1])
                        w(s, "sGD")
                        inc(s.dma_start(out=xf[(wi + 1) % 2][:, :, :],
                                        in_=src_r[:, :, (wi + 1) * TT:(wi + 2) * TT]),
                            "sGD", 16)
                    sec("gpsimd", ldn)
                    ld_w.append(self.cnt["sGD"])
                bi = wi % 2
                def proc(v, wi=wi, bi=bi):
                    w(v, "sGD", ld_w[wi])
                    if wi >= 2:
                        w(v, "sPE", pe_win[wi - 2])
                    step = self.chain(v)
                    inc(v.tensor_copy(out=xb[bi][:, :, :], in_=xf[bi][:, :, :]),
                        "sDV")
                    dv_cast.append(self.cnt["sDV"])
                    if minmax is not None:
                        cmin, cmax = minmax
                        sl = slice(wi * TT, (wi + 1) * TT)
                        step(v.tensor_tensor(out=cmax[:, sl], in0=xf[bi][:, 0, :],
                                             in1=xf[bi][:, 1, :], op=Op.max))
                        for c in range(2, DC):
                            step(v.tensor_tensor(out=cmax[:, sl], in0=cmax[:, sl],
                                                 in1=xf[bi][:, c, :], op=Op.max))
                        step(v.tensor_tensor(out=cmin[:, sl], in0=xf[bi][:, 0, :],
                                             in1=xf[bi][:, 1, :], op=Op.min))
                        for c in range(2, DC):
                            step(v.tensor_tensor(out=cmin[:, sl], in0=cmin[:, sl],
                                                 in1=xf[bi][:, c, :], op=Op.min))
                sec("vector", proc)
                dv_win.append(self.cnt["sDV"])
                def sqr(s_, wi=wi, bi=bi):
                    w(s_, "sDV", dv_cast[wi])
                    inc(s_.activation(out=sq[bi][:, :, :], in_=xb[bi][:, :, :],
                                      func=Act.Square), "sAC")
                sec("scalar", sqr)
                ac_win.append(self.cnt["sAC"])
                def mm(t, wi=wi, bi=bi):
                    w(t, "sDV", dv_cast[wi])
                    w(t, "sAC", ac_win[wi])
                    for j in range(TC):
                        col = wi * TC + j
                        for c in range(DC):
                            inc(t.matmul(ps_sum[:, col:col + 1],
                                         xb[bi][:, c, j * 128:(j + 1) * 128],
                                         self.ones_col_b[:, :],
                                         start=(c == 0), stop=(c == DC - 1)),
                                "sPE")
                        for c in range(DC):
                            inc(t.matmul(ps_ssq[:, col:col + 1],
                                         sq[bi][:, c, j * 128:(j + 1) * 128],
                                         self.ones_col_b[:, :],
                                         start=(c == 0), stop=(c == DC - 1)),
                                "sPE")
                sec("tensor", mm)
                pe_win.append(self.cnt["sPE"])
            def evs(v):
                w(v, "sPE", pe_win[-1])
                inc(v.tensor_copy(out=self.tm[sum_name][:, :], in_=ps_sum[:, :]),
                    "sDV")
                inc(v.tensor_copy(out=self.tm[ssq_name][:, :], in_=ps_ssq[:, :]),
                    "sDV")
            sec("vector", evs)

    def finalize_minmax_cc(self, negmin_col, max_col):
        nc, sec, inc, w = self.nc, self.sec, self.inc, self.w
        def cp(v):
            w(v, "sDV")
            step = self.chain(v)
            step(v.tensor_copy(out=self.tm["tmp0"][:, 0:1], in_=negmin_col))
            step(v.tensor_copy(out=self.tm["tmp0"][:, 1:2], in_=max_col))
        sec("vector", cp)
        def tr(t):
            w(t, "sDV")
            inc(t.transpose(self.ps_small[0:2, :], self.tm["tmp0"][:, 0:2],
                            self.identf[:, :]), "sPE")
        sec("tensor", tr)
        def rd(v):
            w(v, "sPE")
            inc(v.tensor_reduce(out=self.cc2[:, 0:1], in_=self.ps_small[0:2, :],
                                axis=AX.X, op=Op.max), "sDV")
        sec("vector", rd)
        def pk(s):
            w(s, "sDV")
            w(s, "sLD")
            inc(s.dma_start(out=self.ccs[0:1, :], in_=self.cc2[0:2, 0:1]),
                "sLD", 16)
        sec("sync", pk)

    def wq_pump(self, k, quant=True):
        nc, sec, inc, w = self.nc, self.sec, self.inc, self.w
        k *= max(1, 2048 // self.L)
        for _ in range(k):
            lim = 256 if self.wq_finalized else 128
            i = self.wq_step
            if i >= lim:
                return
            if i >= 128 and not quant:
                return
            pi = i % 128
            src, dst, rsw = self._wq_order[pi]
            if i in (0, 128):
                def ld01(s, i=i):
                    for j in (i, i + 1):
                        sem = "sWQ" if j % 2 == 0 else "sWR"
                        if j >= 2:
                            w(s, "sDV", self._wq_dv.get(j - 2, 0))
                        inc(s.dma_start(out=self._wabs[j % 2][:, :],
                                        in_=self._wq_order[j % 128][0]),
                            sem, 16)
                        self._wq_ld[j] = (sem, self.cnt[sem])
                sec("sync", ld01)
            if i < 128:
                def vab(v, i=i, pi=pi):
                    w(v, self._wq_ld[i][0], self._wq_ld[i][1])
                    am = self._am[0 if pi < 64 else 1]
                    col = pi % 64
                    inc(v.reduce_max(out=am[:, col:col + 1],
                                     in_=self._wabs[i % 2][:, :], axis=AX.X,
                                     apply_absolute_value=True), "sDV")
                sec("vector", vab)
                self._wq_dv[i] = self.cnt["sDV"]
                gate = ("sDV", self._wq_dv[i])
            else:
                def vrnd(v, i=i, rsw=rsw):
                    w(v, self._wq_ld[i][0], self._wq_ld[i][1])
                    if i >= 130:
                        sem_p, cnt_p = self._wq_st[i - 2]
                        w(v, sem_p, cnt_p)
                    buf = self._wabs[i % 2]
                    step = self.chain(v)
                    step(v.tensor_scalar(out=buf[:, :], in0=buf[:, :],
                                         scalar1=self.bc[rsw][:, 0:1],
                                         scalar2=None, op0=Op.mult))
                    step(v.tensor_copy(out=self._wqI[:, :], in_=buf[:, :]))
                    step(v.tensor_copy(out=self._wqst[i % 2][:, :],
                                       in_=self._wqI[:, :]))
                sec("vector", vrnd)
                self._wq_dv[i] = self.cnt["sDV"]
                gate = ("sDV", self._wq_dv[i])
                st_sem = "sWS" if i % 2 == 0 else "sWT"
                def stp(s, i=i, dst=dst, st_sem=st_sem):
                    w(s, "sDV", self._wq_dv[i])
                    inc(s.dma_start(out=dst, in_=self._wqst[i % 2][:, :]),
                        st_sem, 16)
                sec("sync", stp)
                self._wq_st[i] = (st_sem, self.cnt[st_sem])
                if pi == 63:
                    self.w1q_ready = (self.cnt["sWS"], self.cnt["sWT"])
                if pi == 127:
                    self.w2q_ready = (self.cnt["sWS"], self.cnt["sWT"])
            if i + 2 < lim:
                def ldn(s, i=i, gate=gate):
                    sem = "sWQ" if i % 2 == 0 else "sWR"
                    w(s, gate[0], gate[1])
                    inc(s.dma_start(out=self._wabs[(i + 2) % 2][:, :],
                                    in_=self._wq_order[(i + 2) % 128][0]),
                        sem, 16)
                    self._wq_ld[i + 2] = (sem, self.cnt[sem])
                sec("sync", ldn)
            self.wq_step = i + 1

    def wq_finalize(self):
        nc, sec, inc, w = self.nc, self.sec, self.inc, self.w
        sc, tm = self.sc, self.tm
        assert self.wq_step == 128, self.wq_step
        with nc.psum_tensor("ps_wqf", [128, 512], F32) as ps_f:
            def vr(v):
                w(v, "sDV")
                step = self.chain(v)
                step(v.tensor_reduce(out=tm["tmp0"][:, 0:1],
                                     in_=self._am[0][:, :], axis=AX.X,
                                     op=Op.max))
                step(v.tensor_reduce(out=tm["tmp0"][:, 1:2],
                                     in_=self._am[1][:, :], axis=AX.X,
                                     op=Op.max))
            sec("vector", vr)
            def tr(t):
                w(t, "sDV")
                inc(t.transpose(ps_f[0:2, 0:128], tm["tmp0"][:, 0:2],
                                self.identf[:, :]), "sPE")
            sec("tensor", tr)
            def fin(v):
                w(v, "sPE")
                step = self.chain(v)
                step(v.tensor_reduce(out=self.cc2[:, 0:1], in_=ps_f[0:2, 0:128],
                                     axis=AX.X, op=Op.max))
            sec("vector", fin)
            def pk(s):
                w(s, "sDV")
                inc(s.dma_start(out=self.wqrow[0:1, :], in_=self.cc2[0:2, 0:1]),
                    "sGP", 16)
            sec("sync", pk)
            def fin2(v):
                w(v, "sGP")
                step = self.chain(v)
                for col, sw, rsw in ((0, "sw1", "rsw1"), (1, "sw2", "rsw2")):
                    step(v.tensor_scalar(
                        out=sc[sw][:, :], in0=self.wqrow[:, col:col + 1],
                        scalar1=float(np.float32(1.0) / np.float32(127.0)),
                        scalar2=1e-8, op0=Op.mult, op1=Op.max))
                    step(v.reciprocal(out=sc[rsw][:, :], in_=sc[sw][:, :]))
            sec("vector", fin2)
            for rsw in ("rsw1", "rsw2"):
                def mm(t, rsw=rsw):
                    w(t, "sDV")
                    inc(t.matmul(ps_f[:, 0:1], self.onesrowf[:, :],
                                 sc[rsw][:, :], start=True, stop=True), "sPE")
                sec("tensor", mm)
                def ev(v, rsw=rsw):
                    w(v, "sPE")
                    inc(v.tensor_copy(out=self.bc[rsw][:, :],
                                      in_=ps_f[:, 0:1]), "sDV")
                sec("vector", ev)
        self.wq_finalized = True

    def phase0_prep(self):
        nc, sec, inc, w = self.nc, self.sec, self.inc, self.w
        D, DC, T, NT, TT, NW, TC = (self.D, self.DC, self.T, self.NT,
                                    self.TT, self.NW, self.TC)
        tm = self.tm

        self.attn_es = contextlib.ExitStack()
        AE = self.attn_es.enter_context
        self._wq_order = (
            [(self.w1Tq[:, i // 8, i % 8, :],
              self.w1qhw[:, i // 8, i % 8, :], "rsw1") for i in range(64)] +
            [(self.w2Tq[:, i // 2, i % 2, :],
              self.w2qhw[:, i // 2, i % 2, :], "rsw2") for i in range(64)])
        self._wq_ld, self._wq_st, self._wq_dv, self._wq_ac = {}, {}, {}, {}
        self.wq_step = 0
        self.wq_finalized = False
        self.Wqkv = AE(nc.sbuf_tensor("Wqkv", [128, DC, 3 * D], BF16))
        self.PW = AE(nc.sbuf_tensor("PW", [128, DC, D], BF16))
        self.alphab = AE(nc.sbuf_tensor("alphab", [128, T], BF16))
        self.r1pb = AE(nc.sbuf_tensor("r1pb", [128, T], BF16))
        self.crow_b = AE(nc.sbuf_tensor("crow_b", [2, 3 * D], BF16))
        self.prow_b = AE(nc.sbuf_tensor("prow_b", [2, D], BF16))
        self.tokqk_b = AE(nc.sbuf_tensor("tokqk_b", [2, T], BF16))
        self.tokpj_b = AE(nc.sbuf_tensor("tokpj_b", [2, T], BF16))

        with contextlib.ExitStack() as p0a:
            self.open_pool(p0a)
            stg = contextlib.ExitStack()
            PA = stg.enter_context
            self.nwnb = PA(nc.sbuf_tensor("nwnb", [2, D], F32))
            self.qkvbr = PA(nc.sbuf_tensor("qkvbr_s", [1, 3 * D], F32))
            self.projbr = PA(nc.sbuf_tensor("projbr_s", [1, D], F32))
            self.b1row = PA(nc.sbuf_tensor("b1row", [1, self.H], F32))
            self.b2row = PA(nc.sbuf_tensor("b2row", [1, D], F32))
            self.crow_f = PA(nc.sbuf_tensor("crow_f", [2, 3 * D], F32))
            self.onesrowb = PA(nc.sbuf_tensor("onesrowb", [1, D], BF16))

            def ld(s):
                for out, in_ in ((self.identb[:, :], self.identb_h),
                                 (self.identf[:, :], self.identf_h),
                                 (self.nwnb[0:1, :], self.nwr_h),
                                 (self.nwnb[1:2, :], self.nbr_h),
                                 (self.qkvbr[:, :], self.qkvbr_h),
                                 (self.projbr[:, :], self.projbr_h),
                                 (self.b1row[:, :], self.b1r_h),
                                 (self.b2row[:, :], self.b2r_h)):
                    w(s, "sLD")
                    inc(s.dma_start(out=out, in_=in_), "sLD", 16)
            sec("sync", ld)
            n_small = self.cnt["sLD"]

            def ms(v):
                w(v, "sLD", n_small)
                inc(v.memset(self.onesNT[:, :], 1.0), "sDV")
                inc(v.memset(self.ones_col_b[:, :], 1.0), "sDV")
                inc(v.memset(self.sc["onef"][:, :], 1.0), "sDV")
                inc(v.memset(self.onesrowf[:, :], 1.0), "sDV")
                inc(v.memset(self.e1row[:, 0:1], 0.0), "sDV")
                inc(v.memset(self.e1row[:, 1:2], 1.0), "sDV")
                inc(v.memset(self.onesrowb[:, :], 1.0), "sDV")
            sec("vector", ms)

            tr_jobs = (
                [(self.nwnb[:, c * 128:(c + 1) * 128], 2,
                  self.nwnb_colf[:, 2 * c:2 * c + 2]) for c in range(DC)] +
                [(self.b1row[:, c * 128:(c + 1) * 128], 1,
                  self.b1cols[:, c:c + 1]) for c in range(self.HC)] +
                [(self.b2row[:, c * 128:(c + 1) * 128], 1,
                  self.b2cols[:, c:c + 1]) for c in range(DC)])
            dv_tr = {}
            for ji, (src_sl, nr, dst_sl) in enumerate(tr_jobs):
                def trc(t, ji=ji, src_sl=src_sl, nr=nr):
                    if ji == 0:
                        w(t, "sLD", n_small)
                        w(t, "sDV")
                    if ji >= 2:
                        w(t, "sDV", dv_tr[ji - 2])
                    inc(t.transpose(self.ps_bk[ji % 2][0:128, 0:nr],
                                    src_sl, self.identf[0:nr, 0:nr]), "sPE")
                sec("tensor", trc)
                pe_tr_ = self.cnt["sPE"]
                def evc(v, ji=ji, nr=nr, dst_sl=dst_sl, pe_tr_=pe_tr_):
                    w(v, "sPE", pe_tr_)
                    inc(v.tensor_copy(out=dst_sl,
                                      in_=self.ps_bk[ji % 2][0:128, 0:nr]),
                        "sDV")
                sec("vector", evc)
                dv_tr[ji] = self.cnt["sDV"]
            dv_nwnb = dv_tr[DC - 1]

            def prow(v):
                w(v, "sLD", n_small)
                inc(v.tensor_copy(out=self.prow_b[0:1, :], in_=self.projbr[:, :]),
                    "sDV")
            sec("vector", prow)
            def prow2(s):
                w(s, "sDV")
                w(s, "sLD")
                inc(s.dma_start(out=self.prow_b[1:2, :], in_=self.onesrowb[:, :]),
                    "sLD", 16)
            sec("sync", prow2)

            with contextlib.ExitStack() as wes:
                WS = wes.enter_context
                wraw = [WS(nc.sbuf_tensor(f"wraw{i}", [128, 3 * D], F32))
                        for i in range(2)]
                praw = [WS(nc.sbuf_tensor(f"praw{i}", [128, D], F32))
                        for i in range(2)]
                ps_c12 = [WS(nc.psum_tensor(f"ps_c12_{i}", [2, 512], F32))
                          for i in range(5)]
                dv_done, pe_done, ld_done = {}, {}, {}
                for half in range(2):
                    sdv_half = self.cnt["sDV"]
                    for c in range(DC):
                        ci = half * DC + c
                        def ldc(s, c=c, ci=ci, half=half):
                            if ci >= 2:
                                w(s, "sDV", dv_done[ci - 2])
                                w(s, "sPE", pe_done[ci - 2])
                            w(s, "sLD")
                            inc(s.dma_start(out=wraw[ci % 2][:, :],
                                            in_=self.qkvwT[c * 128:(c + 1) * 128, :]),
                                "sLD", 16)
                            if half == 0:
                                w(s, "sLD")
                                inc(s.dma_start(out=praw[ci % 2][:, :],
                                                in_=self.projwT[c * 128:(c + 1) * 128, :]),
                                    "sLD", 16)
                        sec("sync", ldc)
                        ld_done[ci] = self.cnt["sLD"]
                        def mmc(t, c=c, ci=ci, half=half, sdv_half=sdv_half):
                            w(t, "sLD", ld_done[ci])
                            if c == 0:
                                w(t, "sDV", sdv_half)
                            for ot in range(3):
                                inc(t.matmul(ps_c12[ot][:, :],
                                             self.nwnb_colf[:, 2 * c:2 * c + 2],
                                             wraw[ci % 2][:, (half * 3 + ot) * 512:
                                                          (half * 3 + ot + 1) * 512],
                                             start=(c == 0), stop=False), "sPE")
                        sec("tensor", mmc)
                        pe_done[ci] = self.cnt["sPE"]
                        if half == 0:
                            def dvc(v, c=c, ci=ci):
                                w(v, "sLD", ld_done[ci])
                                w(v, "sDV", dv_nwnb)
                                inc(v.tensor_scalar(
                                    out=self.Wqkv[:, c, :], in0=wraw[ci % 2][:, :],
                                    scalar1=self.nwnb_colf[:, 2 * c:2 * c + 1],
                                    scalar2=None, op0=Op.mult), "sDV")
                                inc(v.tensor_copy(out=self.PW[:, c, :],
                                                  in_=praw[ci % 2][:, :]), "sDV")
                            sec("vector", dvc)
                        dv_done[ci] = self.cnt["sDV"]
                    def mmb2(t, half=half):
                        for ot in range(3):
                            o0 = (half * 3 + ot) * 512
                            inc(t.matmul(ps_c12[ot][:, :], self.e1row[:, :],
                                         self.qkvbr[:, o0:o0 + 512],
                                         start=False, stop=True), "sPE")
                    sec("tensor", mmb2)
                    pe_c12 = self.cnt["sPE"]
                    def evc(v, half=half):
                        w(v, "sPE", pe_c12)
                        w(v, "sLD", n_small)
                        for ot in range(3):
                            o0 = (half * 3 + ot) * 512
                            inc(v.tensor_copy(out=self.crow_f[:, o0:o0 + 512],
                                              in_=ps_c12[ot][:, :]), "sDV")
                    sec("vector", evc)
                def evb(v):
                    w(v, "sDV")
                    inc(v.tensor_copy(out=self.crow_b[:, :],
                                      in_=self.crow_f[:, :]), "sDV")
                sec("vector", evb)

            stg.close()
            AE = self.attn_es.enter_context
            self._wabs = [AE(nc.sbuf_tensor(f"wabs{i}", [128, 512], F32))
                          for i in range(2)]
            self._am = [AE(nc.sbuf_tensor(f"wam{i}", [128, 64], F32))
                        for i in range(2)]
            self._wqst = [AE(nc.sbuf_tensor(f"wqst{i}", [128, 512], BF16))
                          for i in range(2)]
            self._wqI = AE(nc.sbuf_tensor("wqI", [128, 512], I32))
            self.join_all()

            self.stats_pass(self.xTr, "sum1", "ssq1")
            self.rowA = p0a.enter_context(nc.sbuf_tensor("rowA", [1, T], F32))
            self.rowB = p0a.enter_context(nc.sbuf_tensor("rowB", [1, T], F32))

            rD = 1.0 / self.D
            def vm(v):
                w(v, "sDV")
                w(v, "sAC")
                step = self.chain(v)
                step(v.tensor_scalar(out=tm["m1"][:, :], in0=tm["sum1"][:, :],
                                     scalar1=rD, scalar2=None, op0=Op.mult))
                step(v.tensor_scalar(out=tm["tmp0"][:, :], in0=tm["ssq1"][:, :],
                                     scalar1=rD, scalar2=None, op0=Op.mult))
                step(v.tensor_tensor(out=tm["tmp1"][:, :], in0=tm["m1"][:, :],
                                     in1=tm["m1"][:, :], op=Op.mult))
                step(v.tensor_tensor(out=tm["v1"][:, :], in0=tm["tmp0"][:, :],
                                     in1=tm["tmp1"][:, :], op=Op.subtract))
                step(v.tensor_scalar(out=tm["tmp1"][:, :], in0=tm["v1"][:, :],
                                     scalar1=1.0 + LN_EPS, scalar2=LN_EPS * LN_EPS,
                                     op0=Op.mult, op1=Op.add))
            sec("vector", vm)
            self.sqrt_newton("tmp1", "invalpha")
            def vm2(v):
                w(v, "sDV")
                step = self.chain(v)
                step(v.reciprocal(out=tm["alpha"][:, :], in_=tm["invalpha"][:, :]))
                step(v.tensor_scalar(out=tm["tmp1"][:, :], in0=tm["v1"][:, :],
                                     scalar1=LN_EPS, scalar2=None, op0=Op.add))
            sec("vector", vm2)
            self.sqrt_newton("tmp1", "tmp0")
            def vm3(v):
                w(v, "sDV")
                step = self.chain(v)
                step(v.reciprocal(out=tm["r1"][:, :], in_=tm["tmp0"][:, :]))
                step(v.tensor_scalar(out=tm["kscale"][:, :], in0=tm["alpha"][:, :],
                                     scalar1=1.0 / math.sqrt(self.L),
                                     scalar2=None, op0=Op.mult))
                step(v.tensor_scalar(out=tm["tmp1"][:, :], in0=tm["r1"][:, :],
                                     scalar1=1.0, scalar2=None, op0=Op.add))
                step(v.tensor_tensor(out=tm["tmp2"][:, :], in0=tm["m1"][:, :],
                                     in1=tm["r1"][:, :], op=Op.mult))
                step(v.tensor_tensor(out=tm["tmp2"][:, :], in0=tm["tmp2"][:, :],
                                     in1=tm["invalpha"][:, :], op=Op.mult))
                step(v.tensor_scalar(out=tm["tmp2"][:, :], in0=tm["tmp2"][:, :],
                                     scalar1=-1.0, scalar2=None, op0=Op.mult))
            sec("vector", vm3)

            self.row_from_tm("invalpha", self.tokqk_b, 1)
            self.row_from_tm("invalpha", self.tokpj_b, 0)
            self.row_from_tm("tmp2", self.tokpj_b, 1)
            self.row_from_tm("alpha", self.rowA, 0, f32=True)
            self.row_from_tm("tmp1", self.rowB, 0, f32=True)
            self.row_from_tm("m1", self.tokqk_b, 0, scale=-1.0)
            self.bcast_from_row(self.rowA, self.alphab)
            self.bcast_from_row(self.rowB, self.r1pb)
        self.join_all()

    def phase1_attn(self, b):
        nc, sec, inc, w = self.nc, self.sec, self.inc, self.w
        D, DC, L, TT, LC, LW, TC = (self.D, self.DC, self.L, self.TT,
                                    self.LC, self.LW, self.TC)
        tm = self.tm
        t0 = b * L
        c0 = t0 // 128
        xTr, x1r, vhr = self.xTr, self.x1r, self.vhr

        with contextlib.ExitStack() as aes:
            A = aes.enter_context
            TTq = min(256, TT)
            LWq = L // TTq
            TCq = TTq // 128
            wT = A(nc.sbuf_tensor(f"wT{b}", [128, DC, D], BF16))
            recip = A(nc.sbuf_tensor(f"recip{b}", [128, DC], F32))
            qks = contextlib.ExitStack()
            QK = qks.enter_context
            q_s = QK(nc.sbuf_tensor(f"q_s{b}", [128, LC, D], BF16))
            k_s = QK(nc.sbuf_tensor(f"k_s{b}", [128, LC, D], BF16))
            xes = contextlib.ExitStack()
            XE = xes.enter_context
            xb = [XE(nc.sbuf_tensor(f"a_xb{b}{i}", [128, DC, TTq], BF16))
                  for i in range(2)]

            with contextlib.ExitStack() as qes:
                Q = qes.enter_context
                ps_qk = [Q(nc.psum_tensor(f"ps_qk{b}{i}", [128, 512], F32))
                         for i in range(8)]
                def ld0(g):
                    w(g, "sPE")
                    w(g, "sDV")
                    w(g, "sGD")
                    inc(g.dma_start(out=xb[0][:, :, :], in_=xTr[:, :, t0:t0 + TTq]),
                        "sGD", 16)
                sec("gpsimd", ld0)
                ld_w = [self.cnt["sGD"]]
                evac_dv, evac_ac, pe_qk = {}, {}, {}
                for wi in range(LWq):
                    if wi + 1 < LWq:
                        def ldn(g, wi=wi):
                            if wi >= 1:
                                w(g, "sPE", pe_qk[(wi - 1, TCq - 1)])
                            w(g, "sGD")
                            inc(g.dma_start(
                                out=xb[(wi + 1) % 2][:, :, :],
                                in_=xTr[:, :, t0 + (wi + 1) * TTq:t0 + (wi + 2) * TTq]),
                                "sGD", 16)
                        sec("gpsimd", ldn)
                        ld_w.append(self.cnt["sGD"])
                    self.wq_pump(8, quant=False)
                    bi = wi % 2
                    for j in range(TCq):
                        tau = wi * TCq + j
                        bank0 = 4 * (tau % 2)
                        def mm(t, wi=wi, bi=bi, j=j, tau=tau, bank0=bank0):
                            w(t, "sGD", ld_w[wi])
                            if tau >= 2:
                                w(t, "sDV", evac_dv[tau - 2])
                            for ot in range(4):
                                for c in range(DC):
                                    inc(t.matmul(
                                        ps_qk[bank0 + ot][:, :],
                                        xb[bi][:, c, j * 128:(j + 1) * 128],
                                        self.Wqkv[:, c, ot * 512:(ot + 1) * 512],
                                        start=(c == 0), stop=False), "sPE")
                                inc(t.matmul(
                                    ps_qk[bank0 + ot][:, :],
                                    self.tokqk_b[:, t0 + tau * 128:t0 + (tau + 1) * 128],
                                    self.crow_b[:, ot * 512:(ot + 1) * 512],
                                    start=False, stop=True), "sPE")
                        sec("tensor", mm)
                        pe_qk[(wi, j)] = self.cnt["sPE"]
                        def evq(v, wi=wi, j=j, tau=tau, bank0=bank0):
                            w(v, "sPE", pe_qk[(wi, j)])
                            for ot in range(2):
                                inc(v.tensor_scalar(
                                    out=q_s[:, tau, ot * 512:(ot + 1) * 512],
                                    in0=ps_qk[bank0 + ot][:, :],
                                    scalar1=tm["alpha"][:, c0 + tau:c0 + tau + 1],
                                    scalar2=None, op0=Op.mult), "sDV")
                            for ot in range(2):
                                inc(v.tensor_scalar(
                                    out=k_s[:, tau, ot * 512:(ot + 1) * 512],
                                    in0=ps_qk[bank0 + 2 + ot][:, :],
                                    scalar1=tm["kscale"][:, c0 + tau:c0 + tau + 1],
                                    scalar2=None, op0=Op.mult), "sDV")
                        sec("vector", evq)
                        evac_dv[tau] = self.cnt["sDV"]
                qk_dv_all = evac_dv[LC - 1]
                qk_ac_all = self.cnt["sAC"]
            self.wq_pump(2)

            with contextlib.ExitStack() as ves:
                V = ves.enter_context
                vwin = [V(nc.sbuf_tensor(f"vwin{b}", [128, TTq], BF16))] * 2
                ps_v = [V(nc.psum_tensor(f"ps_v{b}{i}", [128, TTq], F32))
                        for i in range(2)]
                def ldv0(g):
                    w(g, "sPE")
                    w(g, "sDV")
                    w(g, "sGD")
                    inc(g.dma_start(out=xb[0][:, :, :], in_=xTr[:, :, t0:t0 + TTq]),
                        "sGD", 16)
                sec("gpsimd", ldv0)
                ldv_w = [self.cnt["sGD"]]
                pev, dvv, stv = {}, {}, {}
                for wi in range(LWq):
                    if wi + 1 < LWq:
                        def ldn(g, wi=wi):
                            if wi >= 1:
                                w(g, "sPE", pev[(wi - 1, DC - 1)])
                            w(g, "sGD")
                            inc(g.dma_start(
                                out=xb[(wi + 1) % 2][:, :, :],
                                in_=xTr[:, :, t0 + (wi + 1) * TTq:t0 + (wi + 2) * TTq]),
                                "sGD", 16)
                        sec("gpsimd", ldn)
                        ldv_w.append(self.cnt["sGD"])
                    self.wq_pump(6)
                    bi = wi % 2
                    for oc in range(DC):
                        vidx = wi * DC + oc
                        bank = vidx % 2
                        def mmv(t, wi=wi, bi=bi, oc=oc, bank=bank, vidx=vidx):
                            w(t, "sGD", ldv_w[wi])
                            if vidx >= 2:
                                w(t, "sDV", dvv[vidx - 2])
                            for c in range(DC):
                                inc(t.matmul(
                                    ps_v[bank][:, :],
                                    self.Wqkv[:, c, 2048 + oc * 128:2048 + (oc + 1) * 128],
                                    xb[bi][:, c, :],
                                    start=(c == 0), stop=False), "sPE")
                            inc(t.matmul(
                                ps_v[bank][:, :],
                                self.crow_b[:, 2048 + oc * 128:2048 + (oc + 1) * 128],
                                self.tokqk_b[:, t0 + wi * TTq:t0 + (wi + 1) * TTq],
                                start=False, stop=True), "sPE")
                        sec("tensor", mmv)
                        pev[(wi, oc)] = self.cnt["sPE"]
                        def evv(v, wi=wi, oc=oc, bank=bank, vidx=vidx):
                            w(v, "sPE", pev[(wi, oc)])
                            if vidx >= 1:
                                w(v, "sST", stv[vidx - 1])
                            inc(v.tensor_copy(out=vwin[vidx % 2][:, :],
                                              in_=ps_v[bank][:, :]), "sDV")
                        sec("vector", evv)
                        dvv[vidx] = self.cnt["sDV"]
                        def sv(s_, wi=wi, oc=oc, vidx=vidx):
                            w(s_, "sDV", dvv[vidx])
                            w(s_, "sST")
                            inc(s_.dma_start(out=vhr[:, oc, wi * TTq:(wi + 1) * TTq],
                                             in_=vwin[vidx % 2][:, :]), "sST", 16)
                        sec("scalar", sv)
                        stv[vidx] = self.cnt["sST"]
                v_st_all = stv[LWq * DC - 1]
                v_dv_all = dvv[LWq * DC - 1]
            xes.close()
            self.wq_pump(2)
            with contextlib.ExitStack() as ses:
                S = ses.enter_context
                w_s = [S(nc.sbuf_tensor(f"w_s{b}{i}", [128, D], BF16))
                       for i in range(2)]
                rmax = S(nc.sbuf_tensor(f"rmax{b}", [128, 4], F32))
                rsum = S(nc.sbuf_tensor(f"rsum{b}", [128, 4], F32))
                ps_S = [S(nc.psum_tensor(f"ps_S{b}{i}", [128, 512], F32))
                        for i in range(4)]
                ps_w = [S(nc.psum_tensor(f"ps_w{b}{i}", [128, 128], BF16))
                        for i in range(2)]
                pe_S, dv_soft, ac_soft, pe_tr, dv_wT = {}, {}, {}, {}, {}

                def emit_tr(d1c):
                    for d2c in range(DC):
                        if d2c >= 2:
                            prev = (d1c, d2c - 2)
                        elif d1c >= 1:
                            prev = (d1c - 1, DC - 2 + d2c)
                        else:
                            prev = None
                        def trw(t, d1c=d1c, d2c=d2c, prev=prev):
                            w(t, "sAC", ac_soft[d1c])
                            if prev is not None:
                                w(t, "sDV", dv_wT[prev])
                            inc(t.transpose(ps_w[d2c % 2][:, :],
                                            w_s[d1c % 2][:, d2c * 128:(d2c + 1) * 128],
                                            self.identb[:, :]), "sPE")
                        sec("tensor", trw)
                        pe_tr[(d1c, d2c)] = self.cnt["sPE"]
                        def evw(v, d1c=d1c, d2c=d2c):
                            w(v, "sPE", pe_tr[(d1c, d2c)])
                            inc(v.tensor_copy(
                                out=wT[:, d2c, d1c * 128:(d1c + 1) * 128],
                                in_=ps_w[d2c % 2][:, :]), "sDV")
                        sec("vector", evw)
                        dv_wT[(d1c, d2c)] = self.cnt["sDV"]

                for d1c in range(DC):
                    pair = 2 * (d1c % 2)
                    def mmS(t, d1c=d1c, pair=pair):
                        w(t, "sAC", qk_ac_all)
                        w(t, "sDV", qk_dv_all)
                        if d1c == 0:
                            w(t, "sDV", v_dv_all)
                        if d1c >= 2:
                            w(t, "sAC", ac_soft[d1c - 2])
                        for half in range(2):
                            for tau in range(LC):
                                inc(t.matmul(
                                    ps_S[pair + half][:, :],
                                    q_s[:, tau, d1c * 128:(d1c + 1) * 128],
                                    k_s[:, tau, half * 512:(half + 1) * 512],
                                    start=(tau == 0), stop=(tau == LC - 1)), "sPE")
                    sec("tensor", mmS)
                    pe_S[d1c] = self.cnt["sPE"]
                    def soft(v, d1c=d1c, pair=pair):
                        w(v, "sPE", pe_S[d1c])
                        if d1c == 0:
                            w(v, "sST", v_st_all)
                        w(v, "sDV")
                        step = self.chain(v)
                        step(v.reduce_max(out=rmax[:, 0:1], in_=ps_S[pair][:, :],
                                          axis=AX.X))
                        step(v.reduce_max(out=rmax[:, 1:2], in_=ps_S[pair + 1][:, :],
                                          axis=AX.X))
                        step(v.tensor_tensor(out=rmax[:, 2:3], in0=rmax[:, 0:1],
                                             in1=rmax[:, 1:2], op=Op.max))
                        step(v.tensor_scalar(out=rmax[:, 3:4], in0=rmax[:, 2:3],
                                             scalar1=-1.0, scalar2=None,
                                             op0=Op.mult))
                    sec("vector", soft)
                    dv_max = self.cnt["sDV"]
                    def expg(s_, d1c=d1c, pair=pair):
                        w(s_, "sDV", dv_max)
                        if d1c >= 2:
                            w(s_, "sPE", pe_tr[(d1c - 2, DC - 1)])
                        for half in range(2):
                            inc(s_.activation(
                                out=w_s[d1c % 2][:, half * 512:(half + 1) * 512],
                                in_=ps_S[pair + half][:, :], func=Act.Exp,
                                bias=rmax[:, 3:4], scale=1.0,
                                accum_out=rsum[:, half:half + 1]), "sAC")
                    sec("scalar", expg)
                    ac_soft[d1c] = self.cnt["sAC"]
                    def sm2(v, d1c=d1c):
                        w(v, "sAC", ac_soft[d1c])
                        w(v, "sDV")
                        step = self.chain(v)
                        step(v.tensor_tensor(out=rsum[:, 2:3], in0=rsum[:, 0:1],
                                             in1=rsum[:, 1:2], op=Op.add))
                        step(v.reciprocal(out=recip[:, d1c:d1c + 1],
                                          in_=rsum[:, 2:3]))
                    sec("vector", sm2)
                    dv_soft[d1c] = self.cnt["sDV"]
                    if d1c >= 1:
                        emit_tr(d1c - 1)
                emit_tr(DC - 1)
                soft_dv_all = max(dv_soft.values())
                wT_dv_all = dv_wT[(DC - 1, DC - 1)]
                pe_S_all = self.cnt["sPE"]
            self.wq_pump(2)
            qks.close()

            TTa = min(256, TT)
            LWa = L // TTa
            with contextlib.ExitStack() as pes:
                PS = pes.enter_context
                vload = [PS(nc.sbuf_tensor(f"vload{b}{i}", [128, DC, TTa], BF16))
                         for i in range(2)]
                aT = [PS(nc.sbuf_tensor(f"aT{b}{i}", [128, DC, TTa], BF16))
                      for i in range(2)]
                xres = [PS(nc.sbuf_tensor(f"xres{b}{i}", [128, DC, TTa], F32))
                        for i in range(2)]
                xout = [PS(nc.sbuf_tensor(f"xout{b}{i}", [128, DC, TTa], F32))
                        for i in range(2)]
                tproj = PS(nc.sbuf_tensor(f"tproj{b}", [128, TTa], F32))
                tln = PS(nc.sbuf_tensor(f"tln{b}", [128, TTa], F32))
                ps_a = [PS(nc.psum_tensor(f"ps_a{b}{i}", [128, TTa], F32))
                        for i in range(2)]
                ps_p = [PS(nc.psum_tensor(f"ps_p{b}{i}", [128, TTa], F32))
                        for i in range(2)]
                xb3 = PS(nc.sbuf_tensor(f"xb3{b}", [128, DC, TTa], BF16))
                sq3 = PS(nc.sbuf_tensor(f"sq3{b}", [128, DC, TTa], BF16))
                cminW = [PS(nc.sbuf_tensor(f"cminW{b}{i}", [128, TTa], F32))
                         for i in range(2)]
                cmaxW = [PS(nc.sbuf_tensor(f"cmaxW{b}{i}", [128, TTa], F32))
                         for i in range(2)]
                ps_s3 = PS(nc.psum_tensor(f"ps_s3{b}", [128, 2 * LWa], F32))
                ps_q3 = PS(nc.psum_tensor(f"ps_q3{b}", [128, 2 * LWa], F32))
                ps_mmx = [PS(nc.psum_tensor(f"ps_mx{b}{i}", [128, 128], F32))
                          for i in range(2)]
                (ac_s3, pe_s3mm, pe_mmx, dv_mmx, dv_w_list,
                 dv_mnx) = {}, {}, {}, {}, {}, {}
                def ldt0(s):
                    w(s, "sST", v_st_all)
                    w(s, "sDV", v_dv_all)
                    w(s, "sPE", pe_S_all)
                    w(s, "sLD")
                    inc(s.dma_start(out=vload[0][:, :, :], in_=vhr[:, :, 0:TTa]),
                        "sLD", 16)
                    w(s, "sLD")
                    inc(s.dma_start(out=xres[0][:, :, :], in_=xTr[:, :, t0:t0 + TTa]),
                        "sLD", 16)
                sec("sync", ldt0)
                ldt = [self.cnt["sLD"]]
                dv_a, pe_a, dv_p, pe_p, st_x = [], [], [], [], []
                for ti in range(LWa):
                    if ti + 1 < LWa:
                        def ldn(s, ti=ti):
                            if ti >= 1:
                                w(s, "sPE", pe_p[(ti - 1) * DC + DC - 1])
                                w(s, "sDV", dv_p[(ti - 1) * DC + DC - 1])
                            w(s, "sLD")
                            inc(s.dma_start(out=vload[(ti + 1) % 2][:, :, :],
                                            in_=vhr[:, :, (ti + 1) * TTa:(ti + 2) * TTa]),
                                "sLD", 16)
                            w(s, "sLD")
                            inc(s.dma_start(
                                out=xres[(ti + 1) % 2][:, :, :],
                                in_=xTr[:, :, t0 + (ti + 1) * TTa:t0 + (ti + 2) * TTa]),
                                "sLD", 16)
                        sec("sync", ldn)
                        ldt.append(self.cnt["sLD"])
                    self.wq_pump(10)
                    bi = ti % 2
                    for d1c in range(DC):
                        ai = ti * DC + d1c
                        def mma(t, ti=ti, bi=bi, d1c=d1c, ai=ai):
                            w(t, "sLD", ldt[ti])
                            w(t, "sDV", wT_dv_all)
                            if ai >= 2:
                                w(t, "sDV", dv_a[ai - 2])
                            for d2c in range(DC):
                                inc(t.matmul(ps_a[d1c % 2][:, :],
                                             wT[:, d2c, d1c * 128:(d1c + 1) * 128],
                                             vload[bi][:, d2c, :],
                                             start=(d2c == 0),
                                             stop=(d2c == DC - 1)), "sPE")
                        sec("tensor", mma)
                        pe_a.append(self.cnt["sPE"])
                        def eva(v, ti=ti, d1c=d1c, ai=ai):
                            w(v, "sPE", pe_a[ai])
                            w(v, "sDV", soft_dv_all)
                            inc(v.tensor_scalar(out=aT[ti % 2][:, d1c, :],
                                                in0=ps_a[d1c % 2][:, :],
                                                scalar1=recip[:, d1c:d1c + 1],
                                                scalar2=None, op0=Op.mult), "sDV")
                        sec("vector", eva)
                        dv_a.append(self.cnt["sDV"])
                    dv_a_all = dv_a[ti * DC + DC - 1]
                    for oc in range(DC):
                        pi = ti * DC + oc
                        def mmp(t, ti=ti, oc=oc, pi=pi):
                            w(t, "sDV", dv_a_all)
                            if pi >= 2:
                                w(t, "sDV", dv_p[pi - 2])
                            for d1c in range(DC):
                                inc(t.matmul(ps_p[oc % 2][:, :],
                                             self.PW[:, d1c, oc * 128:(oc + 1) * 128],
                                             aT[ti % 2][:, d1c, :],
                                             start=(d1c == 0), stop=False), "sPE")
                            inc(t.matmul(ps_p[oc % 2][:, :],
                                         self.prow_b[:, oc * 128:(oc + 1) * 128],
                                         self.tokpj_b[:, t0 + ti * TTa:t0 + (ti + 1) * TTa],
                                         start=False, stop=True), "sPE")
                        sec("tensor", mmp)
                        pe_p.append(self.cnt["sPE"])
                        def evp(v, ti=ti, bi=bi, oc=oc, pi=pi):
                            w(v, "sPE", pe_p[pi])
                            w(v, "sLD", ldt[ti])
                            if pi >= 2:
                                w(v, "sST", st_x[pi - 2])
                            if ti >= 2 and oc == 0:
                                w(v, "sAC", ac_s3[ti - 2])
                            cb = t0 + ti * TTa
                            w(v, "sDV")
                            step = self.chain(v)
                            step(v.tensor_tensor(out=tproj[:, :],
                                                 in0=ps_p[oc % 2][:, :],
                                                 in1=self.alphab[:, cb:cb + TTa],
                                                 op=Op.mult))
                            step(v.tensor_tensor(out=tln[:, :],
                                                 in0=xres[bi][:, oc, :],
                                                 in1=self.r1pb[:, cb:cb + TTa],
                                                 op=Op.mult))
                            step(v.tensor_tensor(out=xout[ti % 2][:, oc, :],
                                                 in0=tproj[:, :], in1=tln[:, :],
                                                 op=Op.add))
                        sec("vector", evp)
                        dv_p.append(self.cnt["sDV"])
                        def sx(s_, ti=ti, oc=oc, pi=pi):
                            w(s_, "sDV", dv_p[pi])
                            w(s_, "sST")
                            inc(s_.dma_start(
                                out=x1r[:, oc, t0 + ti * TTa:t0 + (ti + 1) * TTa],
                                in_=xout[ti % 2][:, oc, :]), "sST", 16)
                        sec("scalar", sx)
                        st_x.append(self.cnt["sST"])

                    dv_w_list[ti] = dv_p[ti * DC + DC - 1]
                    def mnx3(v, ti=ti):
                        if ti >= 2:
                            w(v, "sPE", pe_mmx[ti - 2])
                        step = self.chain(v)
                        xv = xout[ti % 2][:, :, :].rearrange("p a b -> p b a")
                        step(v.tensor_reduce(out=cminW[ti % 2][:, :], in_=xv,
                                             axis=AX.X, op=Op.min))
                        step(v.tensor_reduce(out=cmaxW[ti % 2][:, :], in_=xv,
                                             axis=AX.X, op=Op.max))
                    sec("vector", mnx3)
                    dv_mnx[ti] = self.cnt["sDV"]
                    def stats_tail(k):
                        def mm3(t, k=k):
                            w(t, "sAC", ac_s3[k])
                            for j in range(TTa // 128):
                                col = k * (TTa // 128) + j
                                for c in range(DC):
                                    inc(t.matmul(ps_s3[:, col:col + 1],
                                                 xb3[:, c, j * 128:(j + 1) * 128],
                                                 self.ones_col_b[:, :],
                                                 start=(c == 0),
                                                 stop=(c == DC - 1)), "sPE")
                                for c in range(DC):
                                    inc(t.matmul(ps_q3[:, col:col + 1],
                                                 sq3[:, c, j * 128:(j + 1) * 128],
                                                 self.ones_col_b[:, :],
                                                 start=(c == 0),
                                                 stop=(c == DC - 1)), "sPE")
                        sec("tensor", mm3)
                        pe_s3mm[k] = self.cnt["sPE"]
                        for which, srcW in enumerate((cminW, cmaxW)):
                            for j in range(TTa // 128):
                                mx_id = 2 * (k * (TTa // 128) + j) + which
                                def trm(t, srcW=srcW, j=j, k=k, mx_id=mx_id):
                                    w(t, "sDV", dv_mnx[k])
                                    if mx_id >= 2:
                                        w(t, "sDV", dv_mmx[mx_id - 2])
                                    inc(t.transpose(
                                        ps_mmx[mx_id % 2][:, :],
                                        srcW[k % 2][:, j * 128:(j + 1) * 128],
                                        self.identf[:, :]), "sPE")
                                sec("tensor", trm)
                                pe_here = self.cnt["sPE"]
                                def rdm(v, which=which, j=j, k=k, mx_id=mx_id,
                                        pe_here=pe_here, b=b):
                                    w(v, "sPE", pe_here)
                                    col = b * (L // 128) + k * (TTa // 128) + j
                                    dst = (self.tm["xmin_tm"] if which == 0
                                           else self.tm["xmax_tm"])
                                    inc(v.tensor_reduce(
                                        out=dst[:, col:col + 1],
                                        in_=ps_mmx[mx_id % 2][:, :], axis=AX.X,
                                        op=(Op.min if which == 0 else Op.max)),
                                        "sDV")
                                sec("vector", rdm)
                                dv_mmx[mx_id] = self.cnt["sDV"]
                        pe_mmx[k] = self.cnt["sPE"]
                    if ti >= 1:
                        stats_tail(ti - 1)
                    def sq3f(s_, ti=ti, bi=bi):
                        w(s_, "sDV", dv_w_list[ti])
                        if ti >= 1:
                            w(s_, "sPE", pe_s3mm[ti - 1])
                        inc(s_.activation(out=xb3[:, :, :],
                                          in_=xout[ti % 2][:, :, :],
                                          func=Act.Copy), "sAC")
                        inc(s_.activation(out=sq3[:, :, :],
                                          in_=xout[ti % 2][:, :, :],
                                          func=Act.Square), "sAC")
                    sec("scalar", sq3f)
                    ac_s3[ti] = self.cnt["sAC"]

                stats_tail(LWa - 1)
                def ev3(v, b=b):
                    w(v, "sPE", pe_s3mm[LWa - 1])
                    c0s = b * (L // 128)
                    inc(v.tensor_copy(out=self.tm["sum3"][:, c0s:c0s + L // 128],
                                      in_=ps_s3[:, :]), "sDV")
                    inc(v.tensor_copy(out=self.tm["ssq3"][:, c0s:c0s + L // 128],
                                      in_=ps_q3[:, :]), "sDV")
                sec("vector", ev3)
            if b + 1 < self.NB:
                snapb = dict(self.cnt)
                def gj(g):
                    for s_ in ALL_SEMS:
                        if snapb[s_] > 0:
                            g.wait_ge(self.sems[s_], snapb[s_])
                sec("gpsimd", gj)
            else:
                self.join_all()
            if b == 0:
                self.wq_finalize()

    def phase2_stats3(self):
        nc, sec, inc, w = self.nc, self.sec, self.inc, self.w
        D, DC, T, NT, TT, NW = self.D, self.DC, self.T, self.NT, self.TT, self.NW
        tm = self.tm
        self.attn_es.close()

        self.ffn_es = contextlib.ExitStack()
        FE = self.ffn_es.enter_context
        self.open_pool(self.ffn_es)
        self.gmaxc = FE(nc.sbuf_tensor("gmaxc", [128, 4 * self.NW], F32))
        self.gminc = FE(nc.sbuf_tensor("gminc", [128, 4 * self.NW], F32))
        self.ab_es = contextlib.ExitStack()
        self.Ab = self.ab_es.enter_context(nc.sbuf_tensor("Ab", [128, T], F32))
        self.Bb = self.ab_es.enter_context(nc.sbuf_tensor("Bb", [128, T], F32))

        self.p3es = contextlib.ExitStack()
        self.w1q = self.p3es.enter_context(nc.sbuf_tensor("w1q", [128, DC, self.H],
                                                          BF16))
        def ldw1(s):
            w(s, "sWS", self.w1q_ready[0])
            w(s, "sWT", self.w1q_ready[1])
            for c in range(0, DC, 2):
                w(s, "sLD")
                inc(s.dma_start(out=self.w1q[:, c:c + 2, :],
                                in_=self.w1qhr[:, c:c + 2, :]), "sLD", 16)
        sec("sync", ldw1)
        self.ld_w1q = self.cnt["sLD"]
        self.x1f = [self.p3es.enter_context(
            nc.sbuf_tensor(f"p3x{i}", [128, DC, TT], F32)) for i in range(2)]

        with contextlib.ExitStack() as ses:
            S = ses.enter_context
            rowA = S(nc.sbuf_tensor("rowA2", [1, T], F32))
            rowB = S(nc.sbuf_tensor("rowB2", [1, T], F32))

            if True:
                rD = 1.0 / self.D
                def vm(v):
                    w(v, "sDV")
                    step = self.chain(v)
                    step(v.tensor_scalar(out=tm["m3"][:, :], in0=tm["sum3"][:, :],
                                         scalar1=rD, scalar2=None, op0=Op.mult))
                    step(v.tensor_scalar(out=tm["tmp1"][:, :], in0=tm["ssq3"][:, :],
                                         scalar1=rD, scalar2=None, op0=Op.mult))
                    step(v.tensor_tensor(out=tm["tmp0"][:, :], in0=tm["m3"][:, :],
                                         in1=tm["m3"][:, :], op=Op.mult))
                    step(v.tensor_tensor(out=tm["tmp1"][:, :], in0=tm["tmp1"][:, :],
                                         in1=tm["tmp0"][:, :], op=Op.subtract))
                    step(v.tensor_scalar(out=tm["tmp1"][:, :], in0=tm["tmp1"][:, :],
                                         scalar1=LN_EPS, scalar2=None, op0=Op.add))
                sec("vector", vm)
                self.sqrt_newton("tmp1", "tmp0")
                def vm2(v):
                    w(v, "sDV")
                    step = self.chain(v)
                    step(v.reciprocal(out=tm["r3"][:, :], in_=tm["tmp0"][:, :]))
                sec("vector", vm2)

                def vq(v):
                    w(v, "sDV")
                    step = self.chain(v)
                    step(v.tensor_tensor(out=tm["qmin_tm"][:, :], in0=tm["xmin_tm"][:, :],
                                         in1=tm["m3"][:, :], op=Op.subtract))
                    step(v.tensor_tensor(out=tm["qmin_tm"][:, :], in0=tm["qmin_tm"][:, :],
                                         in1=tm["r3"][:, :], op=Op.mult))
                    step(v.tensor_tensor(out=tm["qmax_tm"][:, :], in0=tm["xmax_tm"][:, :],
                                         in1=tm["m3"][:, :], op=Op.subtract))
                    step(v.tensor_tensor(out=tm["qmax_tm"][:, :], in0=tm["qmax_tm"][:, :],
                                         in1=tm["r3"][:, :], op=Op.mult))
                    step(v.tensor_reduce(out=tm["tmp1"][:, 0:1], in_=tm["qmin_tm"][:, :],
                                         axis=AX.X, op=Op.min))
                    step(v.tensor_scalar(out=tm["tmp1"][:, 0:1], in0=tm["tmp1"][:, 0:1],
                                         scalar1=-1.0, scalar2=None, op0=Op.mult))
                    step(v.tensor_reduce(out=tm["tmp1"][:, 1:2], in_=tm["qmax_tm"][:, :],
                                         axis=AX.X, op=Op.max))
                sec("vector", vq)
                self.finalize_minmax_cc(tm["tmp1"][:, 0:1], tm["tmp1"][:, 1:2])
                self.allreduce(self.cc_in1, self.cc_out1)
                self.quant_scale_math("s1", "zp1", "lo1", "hi1", "rs1")

            def vab(v):
                w(v, "sDV")
                step = self.chain(v)
                step(v.tensor_scalar(out=tm["tmp1"][:, :], in0=tm["r3"][:, :],
                                     scalar1=self.bc["rs1"][:, 0:1], scalar2=None,
                                     op0=Op.mult))
                step(v.tensor_tensor(out=tm["tmp2"][:, :], in0=tm["m3"][:, :],
                                     in1=tm["tmp1"][:, :], op=Op.mult))
                step(v.tensor_scalar(out=tm["tmp2"][:, :], in0=tm["tmp2"][:, :],
                                     scalar1=-1.0, scalar2=None, op0=Op.mult))
            sec("vector", vab)
            self.row_from_tm("tmp1", rowA, 0, f32=True)
            self.row_from_tm("tmp2", rowB, 0, f32=True)
            self.bcast_from_row(rowA, self.Ab)
            self.bcast_from_row(rowB, self.Bb)
            self.dv_ab = self.cnt["sDV"]

    def phase3_ffn1(self):
        nc, sec, inc, w = self.nc, self.sec, self.inc, self.w
        D, DC, H, HC, T, TT, NW = (self.D, self.DC, self.H, self.HC,
                                   self.T, self.TT, self.NW)
        sc = self.sc
        fes = self.p3es
        if True:
            FS = fes.enter_context
            def vv(v):
                w(v, "sDV")
                w(v, "sGD")
                inc(v.tensor_tensor(out=sc["s1sw1"][:, :], in0=sc["s1"][:, :],
                                    in1=sc["sw1"][:, :], op=Op.mult),
                    "sDV")
            sec("vector", vv)
            self.bcast_scalar("s1sw1")

            NBLK = 4 * NW
            gmaxc = self.gmaxc
            gminc = self.gminc
            x1f = self.x1f
            xqw = [FS(nc.sbuf_tensor(f"p3q{i}", [128, DC, TT], BF16))
                   for i in range(2)]
            qbuf = [FS(nc.sbuf_tensor(f"p3g{i}", [128, 8, TT], F32))
                    for i in range(2)]
            gsc = [FS(nc.sbuf_tensor(f"p3s{i}", [128, TT], F32))
                   for i in range(2)]
            uq = [FS(nc.sbuf_tensor(f"p3u{i}", [128, TT], F32))
                  for i in range(2)]
            uqI = FS(nc.sbuf_tensor("p3uI", [128, TT], I32))
            ps_h = [FS(nc.psum_tensor(f"ps_h{i}", [128, TT], F32))
                    for i in range(4)]
            (ld_w, dv_q, pe_h, pe_w, ac_h, gp_q, st_q,
             dv_gs) = {}, {}, {}, {}, {}, {}, {}, {}
            def ld0(s):
                w(s, "sST")
                w(s, "sLD")
                inc(s.dma_start(out=x1f[0][:, :, :], in_=self.x1r[:, :, 0:TT]),
                    "sLD", 16)
            sec("sync", ld0)
            ld_w[0] = self.cnt["sLD"]
            def emit_qz(tt):
                bi = tt % 2
                cb = tt * TT
                def qz(v, tt=tt, bi=bi, cb=cb):
                    w(v, "sLD", ld_w[tt])
                    if tt >= 2:
                        w(v, "sPE", pe_w[tt - 2])
                    step = self.chain(v)
                    for c in range(DC):
                        step(v.tensor_tensor(out=uq[0][:, :], in0=x1f[bi][:, c, :],
                                             in1=self.Ab[:, cb:cb + TT],
                                             op=Op.mult))
                        step(v.tensor_tensor(out=uq[1][:, :], in0=uq[0][:, :],
                                             in1=self.Bb[:, cb:cb + TT],
                                             op=Op.add))
                        step(v.tensor_copy(out=uqI[:, :], in_=uq[1][:, :]))
                        step(v.tensor_scalar(out=xqw[bi][:, c, :], in0=uqI[:, :],
                                             scalar1=self.bc["lo1"][:, 0:1],
                                             scalar2=self.bc["hi1"][:, 0:1],
                                             op0=Op.max, op1=Op.min))
                sec("vector", qz)
                dv_q[tt] = self.cnt["sDV"]

            emit_qz(0)
            for tt in range(NW):
                if tt + 1 < NW:
                    def ldn(s, tt=tt):
                        if tt >= 1:
                            w(s, "sDV", dv_q[tt - 1])
                        w(s, "sLD")
                        inc(s.dma_start(
                            out=x1f[(tt + 1) % 2][:, :, :],
                            in_=self.x1r[:, :, (tt + 1) * TT:(tt + 2) * TT]),
                            "sLD", 16)
                    sec("sync", ldn)
                    ld_w[tt + 1] = self.cnt["sLD"]
                    emit_qz(tt + 1)
                bi = tt % 2
                for hc in range(HC):
                    gidx = tt * HC + hc
                    bank = gidx % 4
                    q = tt * 4 + hc // 8
                    def mmh(t, tt=tt, bi=bi, hc=hc, bank=bank, gidx=gidx):
                        if gidx == 0:
                            w(t, "sLD", self.ld_w1q)
                        w(t, "sDV", dv_q[tt])
                        if gidx >= 4:
                            w(t, "sDV", dv_gs[gidx - 4])
                        for c in range(DC):
                            inc(t.matmul(ps_h[bank][:, :],
                                         self.w1q[:, c, hc * 128:(hc + 1) * 128],
                                         xqw[bi][:, c, :],
                                         start=(c == 0), stop=(c == DC - 1)),
                                "sPE")
                    sec("tensor", mmh)
                    pe_h[gidx] = self.cnt["sPE"]
                    def evg(v, hc=hc, bank=bank, gidx=gidx):
                        w(v, "sPE", pe_h[gidx])
                        if gidx >= 2:
                            w(v, "sAC", ac_h[gidx - 2])
                        inc(v.tensor_scalar(out=gsc[gidx % 2][:, :],
                                            in0=ps_h[bank][:, :],
                                            scalar1=self.bc["s1sw1"][:, 0:1],
                                            scalar2=self.b1cols[:, hc:hc + 1],
                                            op0=Op.mult, op1=Op.add), "sDV")
                    sec("vector", evg)
                    dv_gs[gidx] = self.cnt["sDV"]
                    def evh(s_, hc=hc, gidx=gidx, q=q):
                        w(s_, "sDV", dv_gs[gidx])
                        if q >= 2 and hc % 8 == 0:
                            w(s_, "sGD", st_q[q - 2])
                            w(s_, "sDV", gp_q[q - 2])
                        inc(s_.activation(out=qbuf[q % 2][:, hc % 8, :],
                                          in_=gsc[gidx % 2][:, :],
                                          func=Act.Gelu), "sAC")
                    sec("scalar", evh)
                    ac_h[gidx] = self.cnt["sAC"]
                    if hc % 8 == 7:
                        def mnx(v, gidx=gidx, q=q):
                            w(v, "sAC", ac_h[gidx])
                            sl = qbuf[q % 2][:, :, :]
                            inc(v.tensor_reduce(out=gmaxc[:, q:q + 1],
                                                in_=sl, axis=AX.XY, op=Op.max),
                                "sDV")
                            v.wait_ge(self.sems["sDV"], self.cnt["sDV"])
                            inc(v.tensor_reduce(out=gminc[:, q:q + 1],
                                                in_=sl, axis=AX.XY, op=Op.min),
                                "sDV")
                        sec("vector", mnx)
                        gp_q[q] = self.cnt["sDV"]
                        def sth(g, tt=tt, hc=hc, gidx=gidx, q=q):
                            w(g, "sAC", ac_h[gidx])
                            w(g, "sGD")
                            inc(g.dma_start(
                                out=self.gh[:, tt, hc - 7:hc + 1, :],
                                in_=qbuf[q % 2][:, :, :]), "sGD", 16)
                        sec("gpsimd", sth)
                        st_q[q] = self.cnt["sGD"]
                pe_w[tt] = self.cnt["sPE"]

            def gred(v):
                w(v, "sDV")
                step = self.chain(v)
                step(v.tensor_reduce(out=self.tm["tmp1"][:, 0:1],
                                     in_=gminc[:, 0:NBLK], axis=AX.X, op=Op.min))
                step(v.tensor_scalar(out=self.tm["tmp1"][:, 0:1],
                                     in0=self.tm["tmp1"][:, 0:1],
                                     scalar1=-1.0, scalar2=None, op0=Op.mult))
                step(v.tensor_reduce(out=self.tm["tmp1"][:, 1:2],
                                     in_=gmaxc[:, 0:NBLK], axis=AX.X, op=Op.max))
            sec("vector", gred)
            self.finalize_minmax_cc(self.tm["tmp1"][:, 0:1], self.tm["tmp1"][:, 1:2])
            snap = dict(self.cnt)
            self.allreduce(self.cc_in2, self.cc_out2)
            def mkj():
                def f(e):
                    for s_ in ("sLD", "sST", "sPE", "sDV", "sAC", "sGD", "sGP"):
                        if snap[s_] > 0:
                            e.wait_ge(self.sems[s_], snap[s_])
                return f
            for e in ("sync", "scalar", "vector", "tensor"):
                sec(e, mkj())
        fes.close()
        self.ab_es.close()

    def phase4_ffn2(self):
        nc, sec, inc, w = self.nc, self.sec, self.inc, self.w
        D, DC, H, HC, T, TT, NW = (self.D, self.DC, self.H, self.HC,
                                   self.T, self.TT, self.NW)
        sc = self.sc

        with contextlib.ExitStack() as fes:
            FS = fes.enter_context
            w2q = FS(nc.sbuf_tensor("w2q", [128, HC, D], BF16))
            b2row = FS(nc.sbuf_tensor("b2row4", [1, D], F32))
            b2p_rb = FS(nc.sbuf_tensor("b2p_rb", [1, D], BF16))
            orow = FS(nc.sbuf_tensor("orowb", [1, 512], BF16))

            def ldw2(s):
                w(s, "sWS", self.w2q_ready[0])
                w(s, "sWT", self.w2q_ready[1])
                for c in range(0, HC, 8):
                    w(s, "sLD")
                    inc(s.dma_start(out=w2q[:, c:c + 8, :],
                                    in_=self.w2qhw[:, c:c + 8, :]), "sLD", 16)
            sec("sync", ldw2)
            ld_w2q = self.cnt["sLD"]

            self.quant_scale_math("s2", "zp2", "lo2", "hi2", "rs2")

            def ldb2(s):
                w(s, "sLD")
                inc(s.dma_start(out=b2row[:, :], in_=self.b2r_h), "sLD", 16)
            sec("sync", ldb2)
            ldb = self.cnt["sLD"]
            def vv(v):
                w(v, "sDV")
                w(v, "sGD")
                w(v, "sLD", ldb)
                step = self.chain(v)
                step(v.tensor_tensor(out=sc["s2sw2"][:, :], in0=sc["s2"][:, :],
                                     in1=sc["sw2"][:, :], op=Op.mult))
                step(v.tensor_tensor(out=sc["t0"][:, :], in0=sc["rs2"][:, :],
                                     in1=sc["rsw2"][:, :], op=Op.mult))
                step(v.tensor_scalar(out=b2row[:, :], in0=b2row[:, :],
                                     scalar1=sc["t0"][:, 0:1], scalar2=None,
                                     op0=Op.mult))
                step(v.tensor_copy(out=b2p_rb[:, :], in_=b2row[:, :]))
                inc(v.memset(orow[:, :], 1.0), "sDV")
            sec("vector", vv)
            self.bcast_scalar("s2sw2")

            HH = HC // 2
            TTs = 256
            gf = [FS(nc.sbuf_tensor(f"gf{i}", [128, HH, TT], F32))
                  for i in range(2)]
            hq = [FS(nc.sbuf_tensor(f"hq{i}", [128, HC, TTs], BF16))
                  for i in range(2)]
            xres = [FS(nc.sbuf_tensor(f"x2res{i}", [128, DC, TTs], F32))
                    for i in range(2)]
            oout = FS(nc.sbuf_tensor("oout", [128, DC, TTs], F32))
            gfI = FS(nc.sbuf_tensor("gfI", [128, 8, TTs], I32))
            ps_o = [FS(nc.psum_tensor(f"ps_o{i}", [128, TTs], F32))
                    for i in range(2)]

            ld_half, ac_half, dv_clast, dv_csub = {}, {}, {}, {}
            ld_x, pe_o, dv_o, st_o, pe_sub = {}, [], [], [], {}
            dv_rs2 = self.cnt["sDV"]

            def emit_loads(tt):
                for half in range(2):
                    hid = tt * 2 + half
                    def ldh(s, tt=tt, half=half, hid=hid):
                        w(s, "sLD")
                        if hid >= 2:
                            w(s, "sDV", dv_clast[hid - 2])
                        inc(s.dma_start(
                            out=gf[hid % 2][:, :, :],
                            in_=self.gh[:, tt, half * HH:(half + 1) * HH, :]),
                            "sLD", 16)
                    sec("sync", ldh)
                    ld_half[hid] = self.cnt["sLD"]
                for half in range(2):
                    hid = tt * 2 + half
                    def rnd(s_, hid=hid):
                        w(s_, "sLD", ld_half[hid])
                        w(s_, "sDV", dv_rs2)
                        inc(s_.activation(out=gf[hid % 2][:, :, :],
                                          in_=gf[hid % 2][:, :, :], func=Act.Copy,
                                          bias=0.0,
                                          scale=self.bc["rs2"][:, 0:1]), "sAC")
                    sec("scalar", rnd)
                    ac_half[hid] = self.cnt["sAC"]

            def emit_clamp(tt, sub):
                s_id = tt * 2 + sub
                for half in range(2):
                    hid = tt * 2 + half
                    def clp(v, sub=sub, half=half, hid=hid, s_id=s_id):
                        w(v, "sAC", ac_half[hid])
                        if s_id >= 2 and half == 0:
                            w(v, "sPE", pe_sub[s_id - 2])
                        step = self.chain(v)
                        for p in range(2):
                            sl = gf[hid % 2][:, p * 8:(p + 1) * 8,
                                             sub * TTs:(sub + 1) * TTs]
                            hsl = hq[sub][:, half * HH + p * 8:
                                          half * HH + (p + 1) * 8, :]
                            step(v.tensor_copy(out=gfI[:, :, :], in_=sl))
                            step(v.tensor_scalar(out=hsl, in0=gfI[:, :, :],
                                                 scalar1=self.bc["lo2"][:, 0:1],
                                                 scalar2=self.bc["hi2"][:, 0:1],
                                                 op0=Op.max, op1=Op.min))
                    sec("vector", clp)
                    dv_csub[(tt, sub)] = self.cnt["sDV"]
                    if sub == 1:
                        dv_clast[tt * 2 + half] = self.cnt["sDV"]

            emit_loads(0)
            emit_clamp(0, 0)
            emit_clamp(0, 1)
            for tt in range(NW):
                for sub in range(2):
                    s_id = tt * 2 + sub
                    def ldx(s, s_id=s_id):
                        w(s, "sLD")
                        if s_id >= 2:
                            w(s, "sDV", dv_o[(s_id - 2) * DC + DC - 1])
                        inc(s.dma_start(
                            out=xres[s_id % 2][:, :, :],
                            in_=self.x1r[:, :, s_id * TTs:(s_id + 1) * TTs]),
                            "sLD", 16)
                    sec("sync", ldx)
                    ld_x[s_id] = self.cnt["sLD"]
                    for dc in range(DC):
                        oi = s_id * DC + dc
                        def mmo(t, tt=tt, sub=sub, dc=dc, oi=oi):
                            if oi == 0:
                                w(t, "sLD", ld_w2q)
                            w(t, "sDV", dv_csub[(tt, sub)])
                            if oi >= 2:
                                w(t, "sDV", dv_o[oi - 2])
                            for hc in range(HC):
                                inc(t.matmul(ps_o[dc % 2][:, :],
                                             w2q[:, hc, dc * 128:(dc + 1) * 128],
                                             hq[sub][:, hc, :],
                                             start=(hc == 0), stop=False), "sPE")
                            inc(t.matmul(ps_o[dc % 2][:, :],
                                         b2p_rb[:, dc * 128:(dc + 1) * 128],
                                         orow[:, 0:TTs],
                                         start=False, stop=True), "sPE")
                        sec("tensor", mmo)
                        pe_o.append(self.cnt["sPE"])
                        def evo(v, s_id=s_id, dc=dc, oi=oi):
                            w(v, "sPE", pe_o[oi])
                            w(v, "sLD", ld_x[s_id])
                            if oi >= 8:
                                w(v, "sGD", st_o[oi - 8])
                            inc(v.scalar_tensor_tensor(
                                out=oout[:, dc, :],
                                in0=ps_o[dc % 2][:, :],
                                scalar=self.bc["s2sw2"][:, 0:1],
                                in1=xres[s_id % 2][:, dc, :],
                                op0=Op.mult, op1=Op.add), "sDV")
                        sec("vector", evo)
                        dv_o.append(self.cnt["sDV"])
                        def sto(g, s_id=s_id, dc=dc, oi=oi):
                            w(g, "sDV", dv_o[oi])
                            w(g, "sGD")
                            inc(g.dma_start(
                                out=self.outr[:, dc, s_id * TTs:(s_id + 1) * TTs],
                                in_=oout[:, dc, :]), "sGD", 16)
                        sec("gpsimd", sto)
                        st_o.append(self.cnt["sGD"])
                    pe_sub[s_id] = self.cnt["sPE"]
                    if sub == 0 and tt + 1 < NW:
                        emit_loads(tt + 1)
                        emit_clamp(tt + 1, 0)
                    if sub == 1 and tt + 1 < NW:
                        emit_clamp(tt + 1, 1)
            self.join_all()
        self.ffn_es.close()

    def finish(self):
        sec, w = self.sec, self.w
        def fw(s_):
            w(s_, "sST")
            w(s_, "sLD")
            w(s_, "sGD")
        sec("scalar", fw)


def host_pack(inputs, L, NB, n_cores):
    x = np.asarray(inputs["L0_attn_x"], dtype=np.float32)
    B = x.shape[0]
    assert B == NB * n_cores
    ident_b = np.eye(128, dtype=ml_dtypes.bfloat16)
    ident_f = np.eye(128, dtype=np.float32)
    shared = {
        "qkvwT": np.ascontiguousarray(np.asarray(inputs["L0_attn_qkv_w"], np.float32).T),
        "projwT": np.ascontiguousarray(np.asarray(inputs["L0_attn_proj_w"], np.float32).T),
        "w1T": np.ascontiguousarray(np.asarray(inputs["L0_ffn_W1"], np.float32).T),
        "w2T": np.ascontiguousarray(np.asarray(inputs["L0_ffn_W2"], np.float32).T),
        "nwr": np.asarray(inputs["L0_attn_norm_w"], np.float32).reshape(1, -1),
        "nbr": np.asarray(inputs["L0_attn_norm_b"], np.float32).reshape(1, -1),
        "qkvbr": np.asarray(inputs["L0_attn_qkv_b"], np.float32).reshape(1, -1),
        "projbr": np.asarray(inputs["L0_attn_proj_b"], np.float32).reshape(1, -1),
        "b1r": np.asarray(inputs["L0_ffn_b1"], np.float32).reshape(1, -1),
        "b2r": np.asarray(inputs["L0_ffn_b2"], np.float32).reshape(1, -1),
        "identb": ident_b,
        "identf": ident_f,
    }
    in_maps = []
    for c in range(n_cores):
        xc = x[c * NB:(c + 1) * NB].reshape(NB * L, x.shape[2])
        in_maps.append(dict(shared, xT=np.ascontiguousarray(xc.T)))
    return in_maps


def host_unpack(results, L, NB, n_cores, D=1024):
    outs = []
    for c in range(n_cores):
        o = np.asarray(results[c]["outT"])
        outs.append(np.ascontiguousarray(o.T).reshape(NB, L, D))
    return np.concatenate(outs, axis=0)


def kernel(**inputs):
    from concourse.bass_utils import run_bass_kernel_spmd
    x = np.asarray(inputs["L0_attn_x"])
    B, L, D = x.shape
    assert (B, L, D) == (16, 2048, 1024), (B, L, D)
    n_cores = 8
    NB = B // n_cores
    bld = Builder(L, NB, TT=512, n_cores=n_cores)
    nc = bld.build()
    in_maps = host_pack(inputs, L, NB, n_cores)
    res = run_bass_kernel_spmd(nc, in_maps, core_ids=list(range(n_cores)))
    return host_unpack(res.results, L, NB, n_cores, D)


# version 20
# speedup vs baseline: 1.0743x; 1.0743x over previous
import contextlib
import math

import numpy as np
import ml_dtypes

import concourse.bass as bass
import concourse.mybir as mybir
from concourse.bass_isa import ReduceOp

F32 = mybir.dt.float32
BF16 = mybir.dt.bfloat16
I32 = mybir.dt.int32
Act = mybir.ActivationFunctionType
Op = mybir.AluOpType
AX = mybir.AxisListType

LN_EPS = 1e-5
ALL_SEMS = ("sLD", "sST", "sGD", "sPE", "sDV", "sAC", "sGP", "sCC", "sWQ",
            "sWR", "sWS", "sWT")


class Builder:
    def __init__(self, L, NB, TT=512, n_cores=8):
        self.L, self.NB, self.TT, self.n_cores = L, NB, TT, n_cores
        self.D, self.DC, self.H, self.HC = 1024, 8, 4096, 32
        self.T = NB * L
        self.NT = self.T // 128
        self.NW = self.T // TT
        self.LC = L // 128
        self.LW = L // TT
        self.TC = TT // 128
        assert L % TT == 0 and TT % 128 == 0 and TT <= 512

    def sem_new(self, *names):
        for n in names:
            self.sems[n] = self.es.enter_context(self.nc.semaphore(n))
            self.cnt[n] = 0

    def inc(self, instr, s, k=1):
        instr.then_inc(self.sems[s], k)
        self.cnt[s] += k
        return self.cnt[s]

    def w(self, eng, s, val=None):
        if val is None:
            val = self.cnt[s]
        assert val <= self.cnt[s], (s, val, self.cnt[s])
        if val > 0:
            eng.wait_ge(self.sems[s], val)

    def sec(self, eng_name, fn):
        getattr(self.block, eng_name)(fn)

    def join_all(self):
        snap = dict(self.cnt)
        def mk():
            def f(e):
                for s in ALL_SEMS:
                    if snap[s] > 0:
                        e.wait_ge(self.sems[s], snap[s])
            return f
        for e in ("sync", "scalar", "vector", "tensor", "gpsimd"):
            self.sec(e, mk())

    def chain(self, v):
        def step(instr):
            self.inc(instr, "sDV")
            v.wait_ge(self.sems["sDV"], self.cnt["sDV"])
        return step

    def build(self):
        L, NB, TT, D, DC, H, HC, T, NT = (self.L, self.NB, self.TT, self.D,
                                          self.DC, self.H, self.HC, self.T,
                                          self.NT)
        nc = bass.Bass("TRN2", target_bir_lowering=False)
        self.nc = nc
        dt = nc.dram_tensor
        self.NWq = self.NW
        self.xT = dt("xT", [D, T], F32, kind="ExternalInput").ap()
        self.qkvwT = dt("qkvwT", [D, 3 * D], F32, kind="ExternalInput").ap()
        self.projwT = dt("projwT", [D, D], F32, kind="ExternalInput").ap()
        self.w1T = dt("w1T", [D, H], F32, kind="ExternalInput").ap()
        self.w2T = dt("w2T", [H, D], F32, kind="ExternalInput").ap()
        self.nwr_h = dt("nwr", [1, D], F32, kind="ExternalInput").ap()
        self.nbr_h = dt("nbr", [1, D], F32, kind="ExternalInput").ap()
        self.qkvbr_h = dt("qkvbr", [1, 3 * D], F32, kind="ExternalInput").ap()
        self.projbr_h = dt("projbr", [1, D], F32, kind="ExternalInput").ap()
        self.b1r_h = dt("b1r", [1, H], F32, kind="ExternalInput").ap()
        self.b2r_h = dt("b2r", [1, D], F32, kind="ExternalInput").ap()
        self.identb_h = dt("identb", [128, 128], BF16, kind="ExternalInput").ap()
        self.identf_h = dt("identf", [128, 128], F32, kind="ExternalInput").ap()
        self.outT = dt("outT", [D, T], F32, kind="ExternalOutput").ap()
        self.x1h = dt("x1h", [D, T], F32).ap()
        self.gh = dt("gh", [128, self.NW, self.HC, self.TT], F32).ap()
        self.vh = dt("vh", [D, L], BF16).ap()
        self.w1qh = dt("w1qh", [D, H], BF16).ap()
        self.w2qh = dt("w2qh", [H, D], BF16).ap()
        self.cc_in1 = dt("cc_in1", [1, 2], F32)
        self.cc_out1 = dt("cc_out1", [1, 2], F32)
        self.cc_in2 = dt("cc_in2", [1, 2], F32)
        self.cc_out2 = dt("cc_out2", [1, 2], F32)

        self.xTr = self.xT.rearrange("(c p) t -> p c t", p=128)
        self.x1r = self.x1h.rearrange("(c p) t -> p c t", p=128)
        self.vhr = self.vh.rearrange("(c p) t -> p c t", p=128)
        self.outr = self.outT.rearrange("(c p) t -> p c t", p=128)
        self.w1Tq = self.w1T.rearrange("(c p) (x h) -> p c x h", p=128, h=512)
        self.w2Tq = self.w2T.rearrange("(c p) (x d) -> p c x d", p=128, d=512)
        self.w1qhw = self.w1qh.rearrange("(c p) (x h) -> p c x h", p=128, h=512)
        self.w2qhw = self.w2qh.rearrange("(c p) (x d) -> p c x d", p=128, d=512)
        self.w1qhr = self.w1qh.rearrange("(c p) h -> p c h", p=128)

        es = contextlib.ExitStack()
        self.es = es
        with es:
            E = es.enter_context
            self.sems, self.cnt = {}, {}
            self.sem_new(*ALL_SEMS)

            self.identb = E(nc.sbuf_tensor("identb_s", [128, 128], BF16))
            self.identf = E(nc.sbuf_tensor("identf_s", [128, 128], F32))
            self.stage = E(nc.sbuf_tensor("stage", [128, 128], F32))
            self.stage2 = E(nc.sbuf_tensor("stage2", [128, 128], BF16))
            self.onesrowf = E(nc.sbuf_tensor("onesrowf", [1, 128], F32))
            self.e1row = E(nc.sbuf_tensor("e1row", [1, 2], F32))
            self.nwnb_colf = E(nc.sbuf_tensor("nwnb_colf", [128, 2 * DC], F32))
            self.b1cols = E(nc.sbuf_tensor("b1cols", [128, HC], F32))
            self.b2cols = E(nc.sbuf_tensor("b2cols", [128, DC], F32))
            tm = {}
            for name in ("sum1", "ssq1", "m1", "v1", "alpha", "invalpha",
                         "r1", "kscale", "tmp0", "tmp1", "tmp2",
                         "sum3", "ssq3", "m3", "r3", "xmin_tm", "xmax_tm",
                         "qmin_tm", "qmax_tm"):
                tm[name] = E(nc.sbuf_tensor("tm_" + name, [128, NT], F32))
            self.tm = tm
            self.onesNT = E(nc.sbuf_tensor("onesNT", [128, NT], F32))
            self.ones_col_b = E(nc.sbuf_tensor("ones_col_b", [128, 1], BF16))
            sc = {}
            for name in ("s1", "zp1", "lo1", "hi1", "rs1", "s1sw1",
                         "s2", "zp2", "lo2", "hi2", "rs2", "s2sw2",
                         "sw1", "rsw1", "sw2", "rsw2", "t0", "t1", "t2",
                         "onef"):
                sc[name] = E(nc.sbuf_tensor("sc_" + name, [1, 1], F32))
            sc["ti"] = E(nc.sbuf_tensor("sc_ti", [1, 1], I32))
            self.sc = sc
            bc = {}
            for name in ("lo1", "hi1", "lo2", "hi2", "s1sw1", "s2sw2",
                         "rsw1", "rsw2", "rs1", "rs2"):
                bc[name] = E(nc.sbuf_tensor("bc_" + name, [128, 1], F32))
            self.bc = bc
            self.wqrow = E(nc.sbuf_tensor("wqrow", [1, 2], F32))
            self.ccs = E(nc.sbuf_tensor("ccs", [1, 2], F32))
            self.cc2 = E(nc.sbuf_tensor("cc2", [2, 1], F32))

            self.block = E(nc.Block())

            self.phase0_prep()
            for b in range(NB):
                self.phase1_attn(b)
            self.phase2_stats3()
            self.phase3_ffn1()
            self.phase4_ffn2()
            self.finish()
        return nc

    def open_pool(self, stack):
        S = stack.enter_context
        self.ps_small = S(self.nc.psum_tensor(f"ps_sm{self.nc.next_id()}",
                                              [128, 128], F32))
        self.ps_bk = [S(self.nc.psum_tensor(f"ps_bk{self.nc.next_id()}",
                                            [128, 512], F32)) for _ in range(2)]

    def row_from_tm(self, name, dst, part, scale=None, f32=False):
        nc, sec, inc, w, NT = self.nc, self.sec, self.inc, self.w, self.NT
        src = self.tm[name]
        if scale is not None:
            def pre(v):
                w(v, "sDV")
                inc(v.tensor_scalar(out=self.tm["tmp0"][:, :], in0=src[:, :],
                                    scalar1=scale, scalar2=None, op0=Op.mult),
                    "sDV")
            sec("vector", pre)
            src = self.tm["tmp0"]
        def tr(t):
            w(t, "sDV")
            w(t, "sLD")
            inc(t.transpose(self.ps_small[0:NT, :], src[:, 0:NT],
                            self.identf[:, :]), "sPE")
        sec("tensor", tr)
        def ev(v):
            w(v, "sPE")
            w(v, "sLD")
            w(v, "sDV")
            inc(v.tensor_copy(out=self.stage[0:NT, :],
                              in_=self.ps_small[0:NT, :]), "sDV")
            if not f32:
                v.wait_ge(self.sems["sDV"], self.cnt["sDV"])
                inc(v.tensor_copy(out=self.stage2[0:NT, :],
                                  in_=self.stage[0:NT, :]), "sDV")
        sec("vector", ev)
        def pk(s):
            w(s, "sDV")
            srcb = self.stage[0:NT, :] if f32 else self.stage2[0:NT, :]
            w(s, "sLD")
            inc(s.dma_start(out=dst[part:part + 1, :], in_=srcb), "sLD", 16)
        sec("sync", pk)

    def bcast_from_row(self, row, dst):
        nc, sec, inc, w, T = self.nc, self.sec, self.inc, self.w, self.T
        for i in range(T // 512):
            def mm(t, i=i):
                w(t, "sLD")
                w(t, "sDV")
                inc(t.matmul(self.ps_bk[i % 2][:, :], self.onesrowf[:, :],
                             row[:, i * 512:(i + 1) * 512],
                             start=True, stop=True), "sPE")
            sec("tensor", mm)
            def ev(v, i=i):
                w(v, "sPE")
                inc(v.tensor_copy(out=dst[:, i * 512:(i + 1) * 512],
                                  in_=self.ps_bk[i % 2][:, :]), "sDV")
            sec("vector", ev)

    def bcast_scalar(self, name):
        nc, sec, inc, w = self.nc, self.sec, self.inc, self.w
        def mm(t):
            w(t, "sDV")
            inc(t.matmul(self.ps_small[:, 0:1], self.onesrowf[:, :],
                         self.sc[name][:, :], start=True, stop=True), "sPE")
        sec("tensor", mm)
        def ev(v):
            w(v, "sPE")
            inc(v.tensor_copy(out=self.bc[name][:, :],
                              in_=self.ps_small[:, 0:1]), "sDV")
        sec("vector", ev)

    def allreduce(self, cc_in, cc_out):
        sec, inc, w = self.sec, self.inc, self.w
        def cc(g):
            w(g, "sDV")
            w(g, "sLD")
            w(g, "sGD")
            inc(g.dma_start(out=cc_in.ap(), in_=self.ccs[:, :]), "sGD", 16)
            g.wait_ge(self.sems["sGD"], self.cnt["sGD"])
            inc(g.collective_compute(
                "AllReduce", Op.max,
                replica_groups=[list(range(self.n_cores))],
                ins=[cc_in.ap().opt()], outs=[cc_out.ap().opt()]), "sCC")
            g.wait_ge(self.sems["sCC"], self.cnt["sCC"])
            w(g, "sGD")
            inc(g.dma_start(out=self.ccs[:, :], in_=cc_out.ap()), "sGD", 16)
        sec("gpsimd", cc)

    def quant_scale_math(self, s_, zp_, lo_, hi_, rs_):
        sec, inc, w, sc = self.sec, self.inc, self.w, self.sc
        def vs(v):
            w(v, "sGD")
            w(v, "sDV")
            step = self.chain(v)
            step(v.tensor_tensor(out=sc["t0"][:, :], in0=self.ccs[:, 0:1],
                                 in1=self.ccs[:, 1:2], op=Op.add))
            step(v.tensor_scalar(out=sc[s_][:, :], in0=sc["t0"][:, :],
                                 scalar1=float(np.float32(1.0) / np.float32(255.0)),
                                 scalar2=1e-8, op0=Op.mult, op1=Op.max))
            step(v.reciprocal(out=sc[rs_][:, :], in_=sc[s_][:, :]))
            step(v.tensor_tensor(out=sc["t1"][:, :], in0=self.ccs[:, 0:1],
                                 in1=sc[rs_][:, :], op=Op.mult))
            step(v.tensor_copy(out=sc["ti"][:, :], in_=sc["t1"][:, :]))
            step(v.tensor_copy(out=sc["t1"][:, :], in_=sc["ti"][:, :]))
            step(v.tensor_scalar(out=sc[zp_][:, :], in0=sc["t1"][:, :],
                                 scalar1=-128.0, scalar2=-128.0,
                                 op0=Op.add, op1=Op.max))
            step(v.tensor_scalar(out=sc[zp_][:, :], in0=sc[zp_][:, :],
                                 scalar1=127.0, scalar2=None, op0=Op.min))
            step(v.tensor_scalar(out=sc[lo_][:, :], in0=sc[zp_][:, :],
                                 scalar1=-1.0, scalar2=-128.0,
                                 op0=Op.mult, op1=Op.add))
            step(v.tensor_scalar(out=sc[hi_][:, :], in0=sc[zp_][:, :],
                                 scalar1=-1.0, scalar2=127.0,
                                 op0=Op.mult, op1=Op.add))
        sec("vector", vs)
        for n in (lo_, hi_, rs_):
            self.bcast_scalar(n)

    def sqrt_newton(self, v_in, out_sqrt):
        sec, inc, w, tm = self.sec, self.inc, self.w, self.tm
        def sq(s):
            w(s, "sDV")
            inc(s.activation(out=tm["tmp2"][:, :], in_=tm[v_in][:, :],
                             func=Act.Sqrt), "sAC")
        sec("scalar", sq)
        def vm(v):
            w(v, "sAC")
            w(v, "sDV")
            step = self.chain(v)
            step(v.reciprocal(out=tm[out_sqrt][:, :], in_=tm["tmp2"][:, :]))
            step(v.tensor_tensor(out=tm[out_sqrt][:, :], in0=tm[out_sqrt][:, :],
                                 in1=tm[v_in][:, :], op=Op.mult))
            step(v.tensor_tensor(out=tm[out_sqrt][:, :], in0=tm[out_sqrt][:, :],
                                 in1=tm["tmp2"][:, :], op=Op.add))
            step(v.tensor_scalar(out=tm[out_sqrt][:, :], in0=tm[out_sqrt][:, :],
                                 scalar1=0.5, scalar2=None, op0=Op.mult))
        sec("vector", vm)

    def stats_pass(self, src_r, sum_name, ssq_name, minmax=None):
        nc, sec, inc, w = self.nc, self.sec, self.inc, self.w
        DC, TT, NW, TC, NT = self.DC, self.TT, self.NW, self.TC, self.NT
        with contextlib.ExitStack() as ses:
            S = ses.enter_context
            xf = [S(nc.sbuf_tensor(f"st_xf{nc.next_id()}", [128, DC, TT], F32))
                  for _ in range(2)]
            xb = [S(nc.sbuf_tensor(f"st_xb{nc.next_id()}", [128, DC, TT], BF16))
                  for _ in range(2)]
            sq = [S(nc.sbuf_tensor(f"st_sq{nc.next_id()}", [128, DC, TT], BF16))
                  for _ in range(2)]
            ps_sum = S(nc.psum_tensor(f"ps_su{nc.next_id()}", [128, NT], F32))
            ps_ssq = S(nc.psum_tensor(f"ps_sq{nc.next_id()}", [128, NT], F32))

            def ld0(s):
                w(s, "sPE")
                w(s, "sDV")
                w(s, "sGD")
                inc(s.dma_start(out=xf[0][:, :, :], in_=src_r[:, :, 0:TT]),
                    "sGD", 16)
            sec("gpsimd", ld0)
            ld_w = [self.cnt["sGD"]]
            dv_win, ac_win, pe_win, dv_cast = [], [], [], []
            for wi in range(NW):
                if wi + 1 < NW:
                    def ldn(s, wi=wi):
                        if wi >= 1:
                            w(s, "sDV", dv_win[wi - 1])
                            w(s, "sAC", ac_win[wi - 1])
                        w(s, "sGD")
                        inc(s.dma_start(out=xf[(wi + 1) % 2][:, :, :],
                                        in_=src_r[:, :, (wi + 1) * TT:(wi + 2) * TT]),
                            "sGD", 16)
                    sec("gpsimd", ldn)
                    ld_w.append(self.cnt["sGD"])
                bi = wi % 2
                def proc(v, wi=wi, bi=bi):
                    w(v, "sGD", ld_w[wi])
                    if wi >= 2:
                        w(v, "sPE", pe_win[wi - 2])
                    step = self.chain(v)
                    inc(v.tensor_copy(out=xb[bi][:, :, :], in_=xf[bi][:, :, :]),
                        "sDV")
                    dv_cast.append(self.cnt["sDV"])
                    if minmax is not None:
                        cmin, cmax = minmax
                        sl = slice(wi * TT, (wi + 1) * TT)
                        step(v.tensor_tensor(out=cmax[:, sl], in0=xf[bi][:, 0, :],
                                             in1=xf[bi][:, 1, :], op=Op.max))
                        for c in range(2, DC):
                            step(v.tensor_tensor(out=cmax[:, sl], in0=cmax[:, sl],
                                                 in1=xf[bi][:, c, :], op=Op.max))
                        step(v.tensor_tensor(out=cmin[:, sl], in0=xf[bi][:, 0, :],
                                             in1=xf[bi][:, 1, :], op=Op.min))
                        for c in range(2, DC):
                            step(v.tensor_tensor(out=cmin[:, sl], in0=cmin[:, sl],
                                                 in1=xf[bi][:, c, :], op=Op.min))
                sec("vector", proc)
                dv_win.append(self.cnt["sDV"])
                def sqr(s_, wi=wi, bi=bi):
                    w(s_, "sDV", dv_cast[wi])
                    inc(s_.activation(out=sq[bi][:, :, :], in_=xb[bi][:, :, :],
                                      func=Act.Square), "sAC")
                sec("scalar", sqr)
                ac_win.append(self.cnt["sAC"])
                def mm(t, wi=wi, bi=bi):
                    w(t, "sDV", dv_cast[wi])
                    w(t, "sAC", ac_win[wi])
                    for j in range(TC):
                        col = wi * TC + j
                        for c in range(DC):
                            inc(t.matmul(ps_sum[:, col:col + 1],
                                         xb[bi][:, c, j * 128:(j + 1) * 128],
                                         self.ones_col_b[:, :],
                                         start=(c == 0), stop=(c == DC - 1)),
                                "sPE")
                        for c in range(DC):
                            inc(t.matmul(ps_ssq[:, col:col + 1],
                                         sq[bi][:, c, j * 128:(j + 1) * 128],
                                         self.ones_col_b[:, :],
                                         start=(c == 0), stop=(c == DC - 1)),
                                "sPE")
                sec("tensor", mm)
                pe_win.append(self.cnt["sPE"])
            def evs(v):
                w(v, "sPE", pe_win[-1])
                inc(v.tensor_copy(out=self.tm[sum_name][:, :], in_=ps_sum[:, :]),
                    "sDV")
                inc(v.tensor_copy(out=self.tm[ssq_name][:, :], in_=ps_ssq[:, :]),
                    "sDV")
            sec("vector", evs)

    def finalize_minmax_cc(self, negmin_col, max_col):
        nc, sec, inc, w = self.nc, self.sec, self.inc, self.w
        def cp(v):
            w(v, "sDV")
            step = self.chain(v)
            step(v.tensor_copy(out=self.tm["tmp0"][:, 0:1], in_=negmin_col))
            step(v.tensor_copy(out=self.tm["tmp0"][:, 1:2], in_=max_col))
        sec("vector", cp)
        def tr(t):
            w(t, "sDV")
            inc(t.transpose(self.ps_small[0:2, :], self.tm["tmp0"][:, 0:2],
                            self.identf[:, :]), "sPE")
        sec("tensor", tr)
        def rd(v):
            w(v, "sPE")
            inc(v.tensor_reduce(out=self.cc2[:, 0:1], in_=self.ps_small[0:2, :],
                                axis=AX.X, op=Op.max), "sDV")
        sec("vector", rd)
        def pk(s):
            w(s, "sDV")
            w(s, "sLD")
            inc(s.dma_start(out=self.ccs[0:1, :], in_=self.cc2[0:2, 0:1]),
                "sLD", 16)
        sec("sync", pk)

    def wq_pump(self, k, quant=True):
        nc, sec, inc, w = self.nc, self.sec, self.inc, self.w
        k *= max(1, 2048 // self.L)
        for _ in range(k):
            lim = 256 if self.wq_finalized else 128
            i = self.wq_step
            if i >= lim:
                return
            if i >= 128 and not quant:
                return
            pi = i % 128
            src, dst, rsw = self._wq_order[pi]
            if i in (0, 128):
                def ld01(s, i=i):
                    for j in (i, i + 1):
                        sem = "sWQ" if j % 2 == 0 else "sWR"
                        if j >= 2:
                            w(s, "sDV", self._wq_dv.get(j - 2, 0))
                        inc(s.dma_start(out=self._wabs[j % 2][:, :],
                                        in_=self._wq_order[j % 128][0]),
                            sem, 16)
                        self._wq_ld[j] = (sem, self.cnt[sem])
                sec("sync", ld01)
            if i < 128:
                def vab(v, i=i, pi=pi):
                    w(v, self._wq_ld[i][0], self._wq_ld[i][1])
                    am = self._am[0 if pi < 64 else 1]
                    col = pi % 64
                    inc(v.reduce_max(out=am[:, col:col + 1],
                                     in_=self._wabs[i % 2][:, :], axis=AX.X,
                                     apply_absolute_value=True), "sDV")
                sec("vector", vab)
                self._wq_dv[i] = self.cnt["sDV"]
                gate = ("sDV", self._wq_dv[i])
            else:
                def vrnd(v, i=i, rsw=rsw):
                    w(v, self._wq_ld[i][0], self._wq_ld[i][1])
                    if i >= 130:
                        sem_p, cnt_p = self._wq_st[i - 2]
                        w(v, sem_p, cnt_p)
                    buf = self._wabs[i % 2]
                    step = self.chain(v)
                    step(v.tensor_scalar(out=buf[:, :], in0=buf[:, :],
                                         scalar1=self.bc[rsw][:, 0:1],
                                         scalar2=None, op0=Op.mult))
                    step(v.tensor_copy(out=self._wqI[:, :], in_=buf[:, :]))
                    step(v.tensor_copy(out=self._wqst[i % 2][:, :],
                                       in_=self._wqI[:, :]))
                sec("vector", vrnd)
                self._wq_dv[i] = self.cnt["sDV"]
                gate = ("sDV", self._wq_dv[i])
                st_sem = "sWS" if i % 2 == 0 else "sWT"
                def stp(s, i=i, dst=dst, st_sem=st_sem):
                    w(s, "sDV", self._wq_dv[i])
                    inc(s.dma_start(out=dst, in_=self._wqst[i % 2][:, :]),
                        st_sem, 16)
                sec("sync", stp)
                self._wq_st[i] = (st_sem, self.cnt[st_sem])
                if pi == 63:
                    self.w1q_ready = (self.cnt["sWS"], self.cnt["sWT"])
                if pi == 127:
                    self.w2q_ready = (self.cnt["sWS"], self.cnt["sWT"])
            if i + 2 < lim:
                def ldn(s, i=i, gate=gate):
                    sem = "sWQ" if i % 2 == 0 else "sWR"
                    w(s, gate[0], gate[1])
                    inc(s.dma_start(out=self._wabs[(i + 2) % 2][:, :],
                                    in_=self._wq_order[(i + 2) % 128][0]),
                        sem, 16)
                    self._wq_ld[i + 2] = (sem, self.cnt[sem])
                sec("sync", ldn)
            self.wq_step = i + 1

    def wq_finalize(self):
        nc, sec, inc, w = self.nc, self.sec, self.inc, self.w
        sc, tm = self.sc, self.tm
        assert self.wq_step == 128, self.wq_step
        with nc.psum_tensor("ps_wqf", [128, 512], F32) as ps_f:
            def vr(v):
                w(v, "sDV")
                step = self.chain(v)
                step(v.tensor_reduce(out=tm["tmp0"][:, 0:1],
                                     in_=self._am[0][:, :], axis=AX.X,
                                     op=Op.max))
                step(v.tensor_reduce(out=tm["tmp0"][:, 1:2],
                                     in_=self._am[1][:, :], axis=AX.X,
                                     op=Op.max))
            sec("vector", vr)
            def tr(t):
                w(t, "sDV")
                inc(t.transpose(ps_f[0:2, 0:128], tm["tmp0"][:, 0:2],
                                self.identf[:, :]), "sPE")
            sec("tensor", tr)
            def fin(v):
                w(v, "sPE")
                step = self.chain(v)
                step(v.tensor_reduce(out=self.cc2[:, 0:1], in_=ps_f[0:2, 0:128],
                                     axis=AX.X, op=Op.max))
            sec("vector", fin)
            def pk(s):
                w(s, "sDV")
                inc(s.dma_start(out=self.wqrow[0:1, :], in_=self.cc2[0:2, 0:1]),
                    "sGP", 16)
            sec("sync", pk)
            def fin2(v):
                w(v, "sGP")
                step = self.chain(v)
                for col, sw, rsw in ((0, "sw1", "rsw1"), (1, "sw2", "rsw2")):
                    step(v.tensor_scalar(
                        out=sc[sw][:, :], in0=self.wqrow[:, col:col + 1],
                        scalar1=float(np.float32(1.0) / np.float32(127.0)),
                        scalar2=1e-8, op0=Op.mult, op1=Op.max))
                    step(v.reciprocal(out=sc[rsw][:, :], in_=sc[sw][:, :]))
            sec("vector", fin2)
            for rsw in ("rsw1", "rsw2"):
                def mm(t, rsw=rsw):
                    w(t, "sDV")
                    inc(t.matmul(ps_f[:, 0:1], self.onesrowf[:, :],
                                 sc[rsw][:, :], start=True, stop=True), "sPE")
                sec("tensor", mm)
                def ev(v, rsw=rsw):
                    w(v, "sPE")
                    inc(v.tensor_copy(out=self.bc[rsw][:, :],
                                      in_=ps_f[:, 0:1]), "sDV")
                sec("vector", ev)
        self.wq_finalized = True

    def phase0_prep(self):
        nc, sec, inc, w = self.nc, self.sec, self.inc, self.w
        D, DC, T, NT, TT, NW, TC = (self.D, self.DC, self.T, self.NT,
                                    self.TT, self.NW, self.TC)
        tm = self.tm

        self.attn_es = contextlib.ExitStack()
        AE = self.attn_es.enter_context
        self._wq_order = (
            [(self.w1Tq[:, i // 8, i % 8, :],
              self.w1qhw[:, i // 8, i % 8, :], "rsw1") for i in range(64)] +
            [(self.w2Tq[:, i // 2, i % 2, :],
              self.w2qhw[:, i // 2, i % 2, :], "rsw2") for i in range(64)])
        self._wq_ld, self._wq_st, self._wq_dv, self._wq_ac = {}, {}, {}, {}
        self.wq_step = 0
        self.wq_finalized = False
        self.Wqkv = AE(nc.sbuf_tensor("Wqkv", [128, DC, 3 * D], BF16))
        self.PW = AE(nc.sbuf_tensor("PW", [128, DC, D], BF16))
        self.alphab = AE(nc.sbuf_tensor("alphab", [128, T], BF16))
        self.r1pb = AE(nc.sbuf_tensor("r1pb", [128, T], BF16))
        self.crow_b = AE(nc.sbuf_tensor("crow_b", [2, 3 * D], BF16))
        self.prow_b = AE(nc.sbuf_tensor("prow_b", [2, D], BF16))
        self.tokqk_b = AE(nc.sbuf_tensor("tokqk_b", [2, T], BF16))
        self.tokpj_b = AE(nc.sbuf_tensor("tokpj_b", [2, T], BF16))

        with contextlib.ExitStack() as p0a:
            self.open_pool(p0a)
            stg = contextlib.ExitStack()
            PA = stg.enter_context
            self.nwnb = PA(nc.sbuf_tensor("nwnb", [2, D], F32))
            self.qkvbr = PA(nc.sbuf_tensor("qkvbr_s", [1, 3 * D], F32))
            self.projbr = PA(nc.sbuf_tensor("projbr_s", [1, D], F32))
            self.b1row = PA(nc.sbuf_tensor("b1row", [1, self.H], F32))
            self.b2row = PA(nc.sbuf_tensor("b2row", [1, D], F32))
            self.crow_f = PA(nc.sbuf_tensor("crow_f", [2, 3 * D], F32))
            self.onesrowb = PA(nc.sbuf_tensor("onesrowb", [1, D], BF16))

            def ld(s):
                for out, in_ in ((self.identb[:, :], self.identb_h),
                                 (self.identf[:, :], self.identf_h),
                                 (self.nwnb[0:1, :], self.nwr_h),
                                 (self.nwnb[1:2, :], self.nbr_h),
                                 (self.qkvbr[:, :], self.qkvbr_h),
                                 (self.projbr[:, :], self.projbr_h),
                                 (self.b1row[:, :], self.b1r_h),
                                 (self.b2row[:, :], self.b2r_h)):
                    w(s, "sLD")
                    inc(s.dma_start(out=out, in_=in_), "sLD", 16)
            sec("sync", ld)
            n_small = self.cnt["sLD"]

            def ms(v):
                w(v, "sLD", n_small)
                inc(v.memset(self.onesNT[:, :], 1.0), "sDV")
                inc(v.memset(self.ones_col_b[:, :], 1.0), "sDV")
                inc(v.memset(self.sc["onef"][:, :], 1.0), "sDV")
                inc(v.memset(self.onesrowf[:, :], 1.0), "sDV")
                inc(v.memset(self.e1row[:, 0:1], 0.0), "sDV")
                inc(v.memset(self.e1row[:, 1:2], 1.0), "sDV")
                inc(v.memset(self.onesrowb[:, :], 1.0), "sDV")
            sec("vector", ms)

            tr_jobs = (
                [(self.nwnb[:, c * 128:(c + 1) * 128], 2,
                  self.nwnb_colf[:, 2 * c:2 * c + 2]) for c in range(DC)] +
                [(self.b1row[:, c * 128:(c + 1) * 128], 1,
                  self.b1cols[:, c:c + 1]) for c in range(self.HC)] +
                [(self.b2row[:, c * 128:(c + 1) * 128], 1,
                  self.b2cols[:, c:c + 1]) for c in range(DC)])
            dv_tr = {}
            for ji, (src_sl, nr, dst_sl) in enumerate(tr_jobs):
                def trc(t, ji=ji, src_sl=src_sl, nr=nr):
                    if ji == 0:
                        w(t, "sLD", n_small)
                        w(t, "sDV")
                    if ji >= 2:
                        w(t, "sDV", dv_tr[ji - 2])
                    inc(t.transpose(self.ps_bk[ji % 2][0:128, 0:nr],
                                    src_sl, self.identf[0:nr, 0:nr]), "sPE")
                sec("tensor", trc)
                pe_tr_ = self.cnt["sPE"]
                def evc(v, ji=ji, nr=nr, dst_sl=dst_sl, pe_tr_=pe_tr_):
                    w(v, "sPE", pe_tr_)
                    inc(v.tensor_copy(out=dst_sl,
                                      in_=self.ps_bk[ji % 2][0:128, 0:nr]),
                        "sDV")
                sec("vector", evc)
                dv_tr[ji] = self.cnt["sDV"]
            dv_nwnb = dv_tr[DC - 1]

            def prow(v):
                w(v, "sLD", n_small)
                inc(v.tensor_copy(out=self.prow_b[0:1, :], in_=self.projbr[:, :]),
                    "sDV")
            sec("vector", prow)
            def prow2(s):
                w(s, "sDV")
                w(s, "sLD")
                inc(s.dma_start(out=self.prow_b[1:2, :], in_=self.onesrowb[:, :]),
                    "sLD", 16)
            sec("sync", prow2)

            with contextlib.ExitStack() as wes:
                WS = wes.enter_context
                wraw = [WS(nc.sbuf_tensor(f"wraw{i}", [128, 3 * D], F32))
                        for i in range(2)]
                praw = [WS(nc.sbuf_tensor(f"praw{i}", [128, D], F32))
                        for i in range(2)]
                ps_c12 = [WS(nc.psum_tensor(f"ps_c12_{i}", [2, 512], F32))
                          for i in range(5)]
                dv_done, pe_done, ld_done = {}, {}, {}
                for half in range(2):
                    sdv_half = self.cnt["sDV"]
                    for c in range(DC):
                        ci = half * DC + c
                        def ldc(s, c=c, ci=ci, half=half):
                            if ci >= 2:
                                w(s, "sDV", dv_done[ci - 2])
                                w(s, "sPE", pe_done[ci - 2])
                            w(s, "sLD")
                            inc(s.dma_start(out=wraw[ci % 2][:, :],
                                            in_=self.qkvwT[c * 128:(c + 1) * 128, :]),
                                "sLD", 16)
                            if half == 0:
                                w(s, "sLD")
                                inc(s.dma_start(out=praw[ci % 2][:, :],
                                                in_=self.projwT[c * 128:(c + 1) * 128, :]),
                                    "sLD", 16)
                        sec("sync", ldc)
                        ld_done[ci] = self.cnt["sLD"]
                        def mmc(t, c=c, ci=ci, half=half, sdv_half=sdv_half):
                            w(t, "sLD", ld_done[ci])
                            if c == 0:
                                w(t, "sDV", sdv_half)
                            for ot in range(3):
                                inc(t.matmul(ps_c12[ot][:, :],
                                             self.nwnb_colf[:, 2 * c:2 * c + 2],
                                             wraw[ci % 2][:, (half * 3 + ot) * 512:
                                                          (half * 3 + ot + 1) * 512],
                                             start=(c == 0), stop=False), "sPE")
                        sec("tensor", mmc)
                        pe_done[ci] = self.cnt["sPE"]
                        if half == 0:
                            def dvc(v, c=c, ci=ci):
                                w(v, "sLD", ld_done[ci])
                                w(v, "sDV", dv_nwnb)
                                inc(v.tensor_scalar(
                                    out=self.Wqkv[:, c, :], in0=wraw[ci % 2][:, :],
                                    scalar1=self.nwnb_colf[:, 2 * c:2 * c + 1],
                                    scalar2=None, op0=Op.mult), "sDV")
                                inc(v.tensor_copy(out=self.PW[:, c, :],
                                                  in_=praw[ci % 2][:, :]), "sDV")
                            sec("vector", dvc)
                        dv_done[ci] = self.cnt["sDV"]
                    def mmb2(t, half=half):
                        for ot in range(3):
                            o0 = (half * 3 + ot) * 512
                            inc(t.matmul(ps_c12[ot][:, :], self.e1row[:, :],
                                         self.qkvbr[:, o0:o0 + 512],
                                         start=False, stop=True), "sPE")
                    sec("tensor", mmb2)
                    pe_c12 = self.cnt["sPE"]
                    def evc(v, half=half):
                        w(v, "sPE", pe_c12)
                        w(v, "sLD", n_small)
                        for ot in range(3):
                            o0 = (half * 3 + ot) * 512
                            inc(v.tensor_copy(out=self.crow_f[:, o0:o0 + 512],
                                              in_=ps_c12[ot][:, :]), "sDV")
                    sec("vector", evc)
                def evb(v):
                    w(v, "sDV")
                    inc(v.tensor_copy(out=self.crow_b[:, :],
                                      in_=self.crow_f[:, :]), "sDV")
                sec("vector", evb)

            stg.close()
            AE = self.attn_es.enter_context
            self._wabs = [AE(nc.sbuf_tensor(f"wabs{i}", [128, 512], F32))
                          for i in range(2)]
            self._am = [AE(nc.sbuf_tensor(f"wam{i}", [128, 64], F32))
                        for i in range(2)]
            self._wqst = [AE(nc.sbuf_tensor(f"wqst{i}", [128, 512], BF16))
                          for i in range(2)]
            self._wqI = AE(nc.sbuf_tensor("wqI", [128, 512], I32))
            self.join_all()

            self.stats_pass(self.xTr, "sum1", "ssq1")
            self.rowA = p0a.enter_context(nc.sbuf_tensor("rowA", [1, T], F32))
            self.rowB = p0a.enter_context(nc.sbuf_tensor("rowB", [1, T], F32))

            rD = 1.0 / self.D
            def vm(v):
                w(v, "sDV")
                w(v, "sAC")
                step = self.chain(v)
                step(v.tensor_scalar(out=tm["m1"][:, :], in0=tm["sum1"][:, :],
                                     scalar1=rD, scalar2=None, op0=Op.mult))
                step(v.tensor_scalar(out=tm["tmp0"][:, :], in0=tm["ssq1"][:, :],
                                     scalar1=rD, scalar2=None, op0=Op.mult))
                step(v.tensor_tensor(out=tm["tmp1"][:, :], in0=tm["m1"][:, :],
                                     in1=tm["m1"][:, :], op=Op.mult))
                step(v.tensor_tensor(out=tm["v1"][:, :], in0=tm["tmp0"][:, :],
                                     in1=tm["tmp1"][:, :], op=Op.subtract))
                step(v.tensor_scalar(out=tm["tmp1"][:, :], in0=tm["v1"][:, :],
                                     scalar1=1.0 + LN_EPS, scalar2=LN_EPS * LN_EPS,
                                     op0=Op.mult, op1=Op.add))
            sec("vector", vm)
            self.sqrt_newton("tmp1", "invalpha")
            def vm2(v):
                w(v, "sDV")
                step = self.chain(v)
                step(v.reciprocal(out=tm["alpha"][:, :], in_=tm["invalpha"][:, :]))
                step(v.tensor_scalar(out=tm["tmp1"][:, :], in0=tm["v1"][:, :],
                                     scalar1=LN_EPS, scalar2=None, op0=Op.add))
            sec("vector", vm2)
            self.sqrt_newton("tmp1", "tmp0")
            def vm3(v):
                w(v, "sDV")
                step = self.chain(v)
                step(v.reciprocal(out=tm["r1"][:, :], in_=tm["tmp0"][:, :]))
                step(v.tensor_scalar(out=tm["kscale"][:, :], in0=tm["alpha"][:, :],
                                     scalar1=1.0 / math.sqrt(self.L),
                                     scalar2=None, op0=Op.mult))
                step(v.tensor_scalar(out=tm["tmp1"][:, :], in0=tm["r1"][:, :],
                                     scalar1=1.0, scalar2=None, op0=Op.add))
                step(v.tensor_tensor(out=tm["tmp2"][:, :], in0=tm["m1"][:, :],
                                     in1=tm["r1"][:, :], op=Op.mult))
                step(v.tensor_tensor(out=tm["tmp2"][:, :], in0=tm["tmp2"][:, :],
                                     in1=tm["invalpha"][:, :], op=Op.mult))
                step(v.tensor_scalar(out=tm["tmp2"][:, :], in0=tm["tmp2"][:, :],
                                     scalar1=-1.0, scalar2=None, op0=Op.mult))
            sec("vector", vm3)

            self.row_from_tm("invalpha", self.tokqk_b, 1)
            self.row_from_tm("invalpha", self.tokpj_b, 0)
            self.row_from_tm("tmp2", self.tokpj_b, 1)
            self.row_from_tm("alpha", self.rowA, 0, f32=True)
            self.row_from_tm("tmp1", self.rowB, 0, f32=True)
            self.row_from_tm("m1", self.tokqk_b, 0, scale=-1.0)
            self.bcast_from_row(self.rowA, self.alphab)
            self.bcast_from_row(self.rowB, self.r1pb)
        self.join_all()

    def phase1_attn(self, b):
        nc, sec, inc, w = self.nc, self.sec, self.inc, self.w
        D, DC, L, TT, LC, LW, TC = (self.D, self.DC, self.L, self.TT,
                                    self.LC, self.LW, self.TC)
        tm = self.tm
        t0 = b * L
        c0 = t0 // 128
        xTr, x1r, vhr = self.xTr, self.x1r, self.vhr

        with contextlib.ExitStack() as aes:
            A = aes.enter_context
            TTq = min(256, TT)
            LWq = L // TTq
            TCq = TTq // 128
            wT = A(nc.sbuf_tensor(f"wT{b}", [128, DC, D], BF16))
            recip = A(nc.sbuf_tensor(f"recip{b}", [128, DC], F32))
            qks = contextlib.ExitStack()
            QK = qks.enter_context
            q_s = QK(nc.sbuf_tensor(f"q_s{b}", [128, LC, D], BF16))
            k_s = QK(nc.sbuf_tensor(f"k_s{b}", [128, LC, D], BF16))
            xes = contextlib.ExitStack()
            XE = xes.enter_context
            xb = [XE(nc.sbuf_tensor(f"a_xb{b}{i}", [128, DC, TTq], BF16))
                  for i in range(2)]

            with contextlib.ExitStack() as qes:
                Q = qes.enter_context
                ps_qk = [Q(nc.psum_tensor(f"ps_qk{b}{i}", [128, 512], F32))
                         for i in range(8)]
                def ld0(g):
                    w(g, "sPE")
                    w(g, "sDV")
                    w(g, "sGD")
                    inc(g.dma_start(out=xb[0][:, :, :], in_=xTr[:, :, t0:t0 + TTq]),
                        "sGD", 16)
                sec("gpsimd", ld0)
                ld_w = [self.cnt["sGD"]]
                evac_dv, evac_ac, pe_qk = {}, {}, {}
                for wi in range(LWq):
                    if wi + 1 < LWq:
                        def ldn(g, wi=wi):
                            if wi >= 1:
                                w(g, "sPE", pe_qk[(wi - 1, TCq - 1)])
                            w(g, "sGD")
                            inc(g.dma_start(
                                out=xb[(wi + 1) % 2][:, :, :],
                                in_=xTr[:, :, t0 + (wi + 1) * TTq:t0 + (wi + 2) * TTq]),
                                "sGD", 16)
                        sec("gpsimd", ldn)
                        ld_w.append(self.cnt["sGD"])
                    self.wq_pump(8, quant=False)
                    bi = wi % 2
                    for j in range(TCq):
                        tau = wi * TCq + j
                        bank0 = 4 * (tau % 2)
                        def mm(t, wi=wi, bi=bi, j=j, tau=tau, bank0=bank0):
                            w(t, "sGD", ld_w[wi])
                            if tau >= 2:
                                w(t, "sDV", evac_dv[tau - 2])
                                w(t, "sAC", evac_ac[tau - 2])
                            for ot in range(4):
                                for c in range(DC):
                                    inc(t.matmul(
                                        ps_qk[bank0 + ot][:, :],
                                        xb[bi][:, c, j * 128:(j + 1) * 128],
                                        self.Wqkv[:, c, ot * 512:(ot + 1) * 512],
                                        start=(c == 0), stop=False), "sPE")
                                inc(t.matmul(
                                    ps_qk[bank0 + ot][:, :],
                                    self.tokqk_b[:, t0 + tau * 128:t0 + (tau + 1) * 128],
                                    self.crow_b[:, ot * 512:(ot + 1) * 512],
                                    start=False, stop=True), "sPE")
                        sec("tensor", mm)
                        pe_qk[(wi, j)] = self.cnt["sPE"]
                        def evq(v, wi=wi, j=j, tau=tau, bank0=bank0):
                            w(v, "sPE", pe_qk[(wi, j)])
                            for ot in range(2):
                                inc(v.tensor_scalar(
                                    out=q_s[:, tau, ot * 512:(ot + 1) * 512],
                                    in0=ps_qk[bank0 + ot][:, :],
                                    scalar1=tm["alpha"][:, c0 + tau:c0 + tau + 1],
                                    scalar2=None, op0=Op.mult), "sDV")
                        sec("vector", evq)
                        evac_dv[tau] = self.cnt["sDV"]
                        def evk(s_, wi=wi, j=j, tau=tau, bank0=bank0):
                            w(s_, "sPE", pe_qk[(wi, j)])
                            for ot in range(2):
                                inc(s_.activation(
                                    out=k_s[:, tau, ot * 512:(ot + 1) * 512],
                                    in_=ps_qk[bank0 + 2 + ot][:, :],
                                    func=Act.Copy,
                                    scale=tm["kscale"][:, c0 + tau:c0 + tau + 1]),
                                    "sAC")
                        sec("scalar", evk)
                        evac_ac[tau] = self.cnt["sAC"]
                qk_dv_all = evac_dv[LC - 1]
                qk_ac_all = evac_ac[LC - 1]
            self.wq_pump(2)

            with contextlib.ExitStack() as ves:
                V = ves.enter_context
                vwin = [V(nc.sbuf_tensor(f"vwin{b}", [128, TTq], BF16))] * 2
                ps_v = [V(nc.psum_tensor(f"ps_v{b}{i}", [128, TTq], F32))
                        for i in range(2)]
                def ldv0(g):
                    w(g, "sPE")
                    w(g, "sDV")
                    w(g, "sGD")
                    inc(g.dma_start(out=xb[0][:, :, :], in_=xTr[:, :, t0:t0 + TTq]),
                        "sGD", 16)
                sec("gpsimd", ldv0)
                ldv_w = [self.cnt["sGD"]]
                pev, dvv, stv = {}, {}, {}
                for wi in range(LWq):
                    if wi + 1 < LWq:
                        def ldn(g, wi=wi):
                            if wi >= 1:
                                w(g, "sPE", pev[(wi - 1, DC - 1)])
                            w(g, "sGD")
                            inc(g.dma_start(
                                out=xb[(wi + 1) % 2][:, :, :],
                                in_=xTr[:, :, t0 + (wi + 1) * TTq:t0 + (wi + 2) * TTq]),
                                "sGD", 16)
                        sec("gpsimd", ldn)
                        ldv_w.append(self.cnt["sGD"])
                    self.wq_pump(6)
                    bi = wi % 2
                    for oc in range(DC):
                        vidx = wi * DC + oc
                        bank = vidx % 2
                        def mmv(t, wi=wi, bi=bi, oc=oc, bank=bank, vidx=vidx):
                            w(t, "sGD", ldv_w[wi])
                            if vidx >= 2:
                                w(t, "sDV", dvv[vidx - 2])
                            for c in range(DC):
                                inc(t.matmul(
                                    ps_v[bank][:, :],
                                    self.Wqkv[:, c, 2048 + oc * 128:2048 + (oc + 1) * 128],
                                    xb[bi][:, c, :],
                                    start=(c == 0), stop=False), "sPE")
                            inc(t.matmul(
                                ps_v[bank][:, :],
                                self.crow_b[:, 2048 + oc * 128:2048 + (oc + 1) * 128],
                                self.tokqk_b[:, t0 + wi * TTq:t0 + (wi + 1) * TTq],
                                start=False, stop=True), "sPE")
                        sec("tensor", mmv)
                        pev[(wi, oc)] = self.cnt["sPE"]
                        def evv(v, wi=wi, oc=oc, bank=bank, vidx=vidx):
                            w(v, "sPE", pev[(wi, oc)])
                            if vidx >= 1:
                                w(v, "sST", stv[vidx - 1])
                            inc(v.tensor_copy(out=vwin[vidx % 2][:, :],
                                              in_=ps_v[bank][:, :]), "sDV")
                        sec("vector", evv)
                        dvv[vidx] = self.cnt["sDV"]
                        def sv(s_, wi=wi, oc=oc, vidx=vidx):
                            w(s_, "sDV", dvv[vidx])
                            w(s_, "sST")
                            inc(s_.dma_start(out=vhr[:, oc, wi * TTq:(wi + 1) * TTq],
                                             in_=vwin[vidx % 2][:, :]), "sST", 16)
                        sec("scalar", sv)
                        stv[vidx] = self.cnt["sST"]
                v_st_all = stv[LWq * DC - 1]
                v_dv_all = dvv[LWq * DC - 1]
            xes.close()
            self.wq_pump(2)
            with contextlib.ExitStack() as ses:
                S = ses.enter_context
                w_s = [S(nc.sbuf_tensor(f"w_s{b}{i}", [128, D], BF16))
                       for i in range(2)]
                rmax = S(nc.sbuf_tensor(f"rmax{b}", [128, 4], F32))
                rsum = S(nc.sbuf_tensor(f"rsum{b}", [128, 4], F32))
                ps_S = [S(nc.psum_tensor(f"ps_S{b}{i}", [128, 512], F32))
                        for i in range(4)]
                ps_w = [S(nc.psum_tensor(f"ps_w{b}{i}", [128, 128], BF16))
                        for i in range(2)]
                pe_S, dv_soft, ac_soft, pe_tr, dv_wT = {}, {}, {}, {}, {}

                def emit_tr(d1c):
                    for d2c in range(DC):
                        if d2c >= 2:
                            prev = (d1c, d2c - 2)
                        elif d1c >= 1:
                            prev = (d1c - 1, DC - 2 + d2c)
                        else:
                            prev = None
                        def trw(t, d1c=d1c, d2c=d2c, prev=prev):
                            w(t, "sAC", ac_soft[d1c])
                            if prev is not None:
                                w(t, "sDV", dv_wT[prev])
                            inc(t.transpose(ps_w[d2c % 2][:, :],
                                            w_s[d1c % 2][:, d2c * 128:(d2c + 1) * 128],
                                            self.identb[:, :]), "sPE")
                        sec("tensor", trw)
                        pe_tr[(d1c, d2c)] = self.cnt["sPE"]
                        def evw(v, d1c=d1c, d2c=d2c):
                            w(v, "sPE", pe_tr[(d1c, d2c)])
                            inc(v.tensor_copy(
                                out=wT[:, d2c, d1c * 128:(d1c + 1) * 128],
                                in_=ps_w[d2c % 2][:, :]), "sDV")
                        sec("vector", evw)
                        dv_wT[(d1c, d2c)] = self.cnt["sDV"]

                for d1c in range(DC):
                    pair = 2 * (d1c % 2)
                    def mmS(t, d1c=d1c, pair=pair):
                        w(t, "sAC", qk_ac_all)
                        w(t, "sDV", qk_dv_all)
                        if d1c == 0:
                            w(t, "sDV", v_dv_all)
                        if d1c >= 2:
                            w(t, "sAC", ac_soft[d1c - 2])
                        for half in range(2):
                            for tau in range(LC):
                                inc(t.matmul(
                                    ps_S[pair + half][:, :],
                                    q_s[:, tau, d1c * 128:(d1c + 1) * 128],
                                    k_s[:, tau, half * 512:(half + 1) * 512],
                                    start=(tau == 0), stop=(tau == LC - 1)), "sPE")
                    sec("tensor", mmS)
                    pe_S[d1c] = self.cnt["sPE"]
                    def soft(v, d1c=d1c, pair=pair):
                        w(v, "sPE", pe_S[d1c])
                        if d1c == 0:
                            w(v, "sST", v_st_all)
                        w(v, "sDV")
                        step = self.chain(v)
                        step(v.reduce_max(out=rmax[:, 0:1], in_=ps_S[pair][:, :],
                                          axis=AX.X))
                        step(v.reduce_max(out=rmax[:, 1:2], in_=ps_S[pair + 1][:, :],
                                          axis=AX.X))
                        step(v.tensor_tensor(out=rmax[:, 2:3], in0=rmax[:, 0:1],
                                             in1=rmax[:, 1:2], op=Op.max))
                        step(v.tensor_scalar(out=rmax[:, 3:4], in0=rmax[:, 2:3],
                                             scalar1=-1.0, scalar2=None,
                                             op0=Op.mult))
                    sec("vector", soft)
                    dv_max = self.cnt["sDV"]
                    def expg(s_, d1c=d1c, pair=pair):
                        w(s_, "sDV", dv_max)
                        if d1c >= 2:
                            w(s_, "sPE", pe_tr[(d1c - 2, DC - 1)])
                        for half in range(2):
                            inc(s_.activation(
                                out=w_s[d1c % 2][:, half * 512:(half + 1) * 512],
                                in_=ps_S[pair + half][:, :], func=Act.Exp,
                                bias=rmax[:, 3:4], scale=1.0,
                                accum_out=rsum[:, half:half + 1]), "sAC")
                    sec("scalar", expg)
                    ac_soft[d1c] = self.cnt["sAC"]
                    def sm2(v, d1c=d1c):
                        w(v, "sAC", ac_soft[d1c])
                        w(v, "sDV")
                        step = self.chain(v)
                        step(v.tensor_tensor(out=rsum[:, 2:3], in0=rsum[:, 0:1],
                                             in1=rsum[:, 1:2], op=Op.add))
                        step(v.reciprocal(out=recip[:, d1c:d1c + 1],
                                          in_=rsum[:, 2:3]))
                    sec("vector", sm2)
                    dv_soft[d1c] = self.cnt["sDV"]
                    if d1c >= 1:
                        emit_tr(d1c - 1)
                emit_tr(DC - 1)
                soft_dv_all = max(dv_soft.values())
                wT_dv_all = dv_wT[(DC - 1, DC - 1)]
                pe_S_all = self.cnt["sPE"]
            self.wq_pump(2)
            qks.close()

            TTa = min(256, TT)
            LWa = L // TTa
            with contextlib.ExitStack() as pes:
                PS = pes.enter_context
                vload = [PS(nc.sbuf_tensor(f"vload{b}{i}", [128, DC, TTa], BF16))
                         for i in range(2)]
                aT = [PS(nc.sbuf_tensor(f"aT{b}{i}", [128, DC, TTa], BF16))
                      for i in range(2)]
                xres = [PS(nc.sbuf_tensor(f"xres{b}{i}", [128, DC, TTa], F32))
                        for i in range(2)]
                xout = [PS(nc.sbuf_tensor(f"xout{b}{i}", [128, DC, TTa], F32))
                        for i in range(2)]
                tproj = PS(nc.sbuf_tensor(f"tproj{b}", [128, TTa], F32))
                tln = PS(nc.sbuf_tensor(f"tln{b}", [128, TTa], F32))
                ps_a = [PS(nc.psum_tensor(f"ps_a{b}{i}", [128, TTa], F32))
                        for i in range(2)]
                ps_p = [PS(nc.psum_tensor(f"ps_p{b}{i}", [128, TTa], F32))
                        for i in range(2)]
                xb3 = PS(nc.sbuf_tensor(f"xb3{b}", [128, DC, TTa], BF16))
                sq3 = PS(nc.sbuf_tensor(f"sq3{b}", [128, DC, TTa], BF16))
                cminW = [PS(nc.sbuf_tensor(f"cminW{b}{i}", [128, TTa], F32))
                         for i in range(2)]
                cmaxW = [PS(nc.sbuf_tensor(f"cmaxW{b}{i}", [128, TTa], F32))
                         for i in range(2)]
                ps_s3 = PS(nc.psum_tensor(f"ps_s3{b}", [128, 2 * LWa], F32))
                ps_q3 = PS(nc.psum_tensor(f"ps_q3{b}", [128, 2 * LWa], F32))
                ps_mmx = [PS(nc.psum_tensor(f"ps_mx{b}{i}", [128, 128], F32))
                          for i in range(2)]
                (ac_s3, pe_s3mm, pe_mmx, dv_mmx, dv_w_list,
                 dv_mnx) = {}, {}, {}, {}, {}, {}
                def ldt0(s):
                    w(s, "sST", v_st_all)
                    w(s, "sDV", v_dv_all)
                    w(s, "sPE", pe_S_all)
                    w(s, "sLD")
                    inc(s.dma_start(out=vload[0][:, :, :], in_=vhr[:, :, 0:TTa]),
                        "sLD", 16)
                    w(s, "sLD")
                    inc(s.dma_start(out=xres[0][:, :, :], in_=xTr[:, :, t0:t0 + TTa]),
                        "sLD", 16)
                sec("sync", ldt0)
                ldt = [self.cnt["sLD"]]
                dv_a, pe_a, dv_p, pe_p, st_x = [], [], [], [], []
                for ti in range(LWa):
                    if ti + 1 < LWa:
                        def ldn(s, ti=ti):
                            if ti >= 1:
                                w(s, "sPE", pe_p[(ti - 1) * DC + DC - 1])
                                w(s, "sDV", dv_p[(ti - 1) * DC + DC - 1])
                            w(s, "sLD")
                            inc(s.dma_start(out=vload[(ti + 1) % 2][:, :, :],
                                            in_=vhr[:, :, (ti + 1) * TTa:(ti + 2) * TTa]),
                                "sLD", 16)
                            w(s, "sLD")
                            inc(s.dma_start(
                                out=xres[(ti + 1) % 2][:, :, :],
                                in_=xTr[:, :, t0 + (ti + 1) * TTa:t0 + (ti + 2) * TTa]),
                                "sLD", 16)
                        sec("sync", ldn)
                        ldt.append(self.cnt["sLD"])
                    self.wq_pump(10)
                    bi = ti % 2
                    for d1c in range(DC):
                        ai = ti * DC + d1c
                        def mma(t, ti=ti, bi=bi, d1c=d1c, ai=ai):
                            w(t, "sLD", ldt[ti])
                            w(t, "sDV", wT_dv_all)
                            if ai >= 2:
                                w(t, "sDV", dv_a[ai - 2])
                            for d2c in range(DC):
                                inc(t.matmul(ps_a[d1c % 2][:, :],
                                             wT[:, d2c, d1c * 128:(d1c + 1) * 128],
                                             vload[bi][:, d2c, :],
                                             start=(d2c == 0),
                                             stop=(d2c == DC - 1)), "sPE")
                        sec("tensor", mma)
                        pe_a.append(self.cnt["sPE"])
                        def eva(v, ti=ti, d1c=d1c, ai=ai):
                            w(v, "sPE", pe_a[ai])
                            w(v, "sDV", soft_dv_all)
                            inc(v.tensor_scalar(out=aT[ti % 2][:, d1c, :],
                                                in0=ps_a[d1c % 2][:, :],
                                                scalar1=recip[:, d1c:d1c + 1],
                                                scalar2=None, op0=Op.mult), "sDV")
                        sec("vector", eva)
                        dv_a.append(self.cnt["sDV"])
                    dv_a_all = dv_a[ti * DC + DC - 1]
                    for oc in range(DC):
                        pi = ti * DC + oc
                        def mmp(t, ti=ti, oc=oc, pi=pi):
                            w(t, "sDV", dv_a_all)
                            if pi >= 2:
                                w(t, "sDV", dv_p[pi - 2])
                            for d1c in range(DC):
                                inc(t.matmul(ps_p[oc % 2][:, :],
                                             self.PW[:, d1c, oc * 128:(oc + 1) * 128],
                                             aT[ti % 2][:, d1c, :],
                                             start=(d1c == 0), stop=False), "sPE")
                            inc(t.matmul(ps_p[oc % 2][:, :],
                                         self.prow_b[:, oc * 128:(oc + 1) * 128],
                                         self.tokpj_b[:, t0 + ti * TTa:t0 + (ti + 1) * TTa],
                                         start=False, stop=True), "sPE")
                        sec("tensor", mmp)
                        pe_p.append(self.cnt["sPE"])
                        def evp(v, ti=ti, bi=bi, oc=oc, pi=pi):
                            w(v, "sPE", pe_p[pi])
                            w(v, "sLD", ldt[ti])
                            if pi >= 2:
                                w(v, "sST", st_x[pi - 2])
                            if ti >= 2 and oc == 0:
                                w(v, "sAC", ac_s3[ti - 2])
                            cb = t0 + ti * TTa
                            w(v, "sDV")
                            step = self.chain(v)
                            step(v.tensor_tensor(out=tproj[:, :],
                                                 in0=ps_p[oc % 2][:, :],
                                                 in1=self.alphab[:, cb:cb + TTa],
                                                 op=Op.mult))
                            step(v.tensor_tensor(out=tln[:, :],
                                                 in0=xres[bi][:, oc, :],
                                                 in1=self.r1pb[:, cb:cb + TTa],
                                                 op=Op.mult))
                            step(v.tensor_tensor(out=xout[ti % 2][:, oc, :],
                                                 in0=tproj[:, :], in1=tln[:, :],
                                                 op=Op.add))
                        sec("vector", evp)
                        dv_p.append(self.cnt["sDV"])
                        def sx(s_, ti=ti, oc=oc, pi=pi):
                            w(s_, "sDV", dv_p[pi])
                            w(s_, "sST")
                            inc(s_.dma_start(
                                out=x1r[:, oc, t0 + ti * TTa:t0 + (ti + 1) * TTa],
                                in_=xout[ti % 2][:, oc, :]), "sST", 16)
                        sec("scalar", sx)
                        st_x.append(self.cnt["sST"])

                    dv_w_list[ti] = dv_p[ti * DC + DC - 1]
                    def mnx3(v, ti=ti):
                        if ti >= 2:
                            w(v, "sPE", pe_mmx[ti - 2])
                        step = self.chain(v)
                        xv = xout[ti % 2][:, :, :].rearrange("p a b -> p b a")
                        step(v.tensor_reduce(out=cminW[ti % 2][:, :], in_=xv,
                                             axis=AX.X, op=Op.min))
                        step(v.tensor_reduce(out=cmaxW[ti % 2][:, :], in_=xv,
                                             axis=AX.X, op=Op.max))
                    sec("vector", mnx3)
                    dv_mnx[ti] = self.cnt["sDV"]
                    def stats_tail(k):
                        def mm3(t, k=k):
                            w(t, "sAC", ac_s3[k])
                            for j in range(TTa // 128):
                                col = k * (TTa // 128) + j
                                for c in range(DC):
                                    inc(t.matmul(ps_s3[:, col:col + 1],
                                                 xb3[:, c, j * 128:(j + 1) * 128],
                                                 self.ones_col_b[:, :],
                                                 start=(c == 0),
                                                 stop=(c == DC - 1)), "sPE")
                                for c in range(DC):
                                    inc(t.matmul(ps_q3[:, col:col + 1],
                                                 sq3[:, c, j * 128:(j + 1) * 128],
                                                 self.ones_col_b[:, :],
                                                 start=(c == 0),
                                                 stop=(c == DC - 1)), "sPE")
                        sec("tensor", mm3)
                        pe_s3mm[k] = self.cnt["sPE"]
                        for which, srcW in enumerate((cminW, cmaxW)):
                            for j in range(TTa // 128):
                                mx_id = 2 * (k * (TTa // 128) + j) + which
                                def trm(t, srcW=srcW, j=j, k=k, mx_id=mx_id):
                                    w(t, "sDV", dv_mnx[k])
                                    if mx_id >= 2:
                                        w(t, "sDV", dv_mmx[mx_id - 2])
                                    inc(t.transpose(
                                        ps_mmx[mx_id % 2][:, :],
                                        srcW[k % 2][:, j * 128:(j + 1) * 128],
                                        self.identf[:, :]), "sPE")
                                sec("tensor", trm)
                                pe_here = self.cnt["sPE"]
                                def rdm(v, which=which, j=j, k=k, mx_id=mx_id,
                                        pe_here=pe_here, b=b):
                                    w(v, "sPE", pe_here)
                                    col = b * (L // 128) + k * (TTa // 128) + j
                                    dst = (self.tm["xmin_tm"] if which == 0
                                           else self.tm["xmax_tm"])
                                    inc(v.tensor_reduce(
                                        out=dst[:, col:col + 1],
                                        in_=ps_mmx[mx_id % 2][:, :], axis=AX.X,
                                        op=(Op.min if which == 0 else Op.max)),
                                        "sDV")
                                sec("vector", rdm)
                                dv_mmx[mx_id] = self.cnt["sDV"]
                        pe_mmx[k] = self.cnt["sPE"]
                    if ti >= 1:
                        stats_tail(ti - 1)
                    def sq3f(s_, ti=ti, bi=bi):
                        w(s_, "sDV", dv_w_list[ti])
                        if ti >= 1:
                            w(s_, "sPE", pe_s3mm[ti - 1])
                        inc(s_.activation(out=xb3[:, :, :],
                                          in_=xout[ti % 2][:, :, :],
                                          func=Act.Copy), "sAC")
                        inc(s_.activation(out=sq3[:, :, :],
                                          in_=xout[ti % 2][:, :, :],
                                          func=Act.Square), "sAC")
                    sec("scalar", sq3f)
                    ac_s3[ti] = self.cnt["sAC"]

                stats_tail(LWa - 1)
                def ev3(v, b=b):
                    w(v, "sPE", pe_s3mm[LWa - 1])
                    c0s = b * (L // 128)
                    inc(v.tensor_copy(out=self.tm["sum3"][:, c0s:c0s + L // 128],
                                      in_=ps_s3[:, :]), "sDV")
                    inc(v.tensor_copy(out=self.tm["ssq3"][:, c0s:c0s + L // 128],
                                      in_=ps_q3[:, :]), "sDV")
                sec("vector", ev3)
            if b + 1 < self.NB:
                snapb = dict(self.cnt)
                def gj(g):
                    for s_ in ALL_SEMS:
                        if snapb[s_] > 0:
                            g.wait_ge(self.sems[s_], snapb[s_])
                sec("gpsimd", gj)
            else:
                self.join_all()
            if b == 0:
                self.wq_finalize()

    def phase2_stats3(self):
        nc, sec, inc, w = self.nc, self.sec, self.inc, self.w
        D, DC, T, NT, TT, NW = self.D, self.DC, self.T, self.NT, self.TT, self.NW
        tm = self.tm
        self.attn_es.close()

        self.ffn_es = contextlib.ExitStack()
        FE = self.ffn_es.enter_context
        self.open_pool(self.ffn_es)
        self.gmaxc = FE(nc.sbuf_tensor("gmaxc", [128, 4 * self.NW], F32))
        self.gminc = FE(nc.sbuf_tensor("gminc", [128, 4 * self.NW], F32))
        self.ab_es = contextlib.ExitStack()
        self.Ab = self.ab_es.enter_context(nc.sbuf_tensor("Ab", [128, T], F32))
        self.Bb = self.ab_es.enter_context(nc.sbuf_tensor("Bb", [128, T], F32))

        self.p3es = contextlib.ExitStack()
        self.w1q = self.p3es.enter_context(nc.sbuf_tensor("w1q", [128, DC, self.H],
                                                          BF16))
        def ldw1(s):
            w(s, "sWS", self.w1q_ready[0])
            w(s, "sWT", self.w1q_ready[1])
            for c in range(0, DC, 2):
                w(s, "sLD")
                inc(s.dma_start(out=self.w1q[:, c:c + 2, :],
                                in_=self.w1qhr[:, c:c + 2, :]), "sLD", 16)
        sec("sync", ldw1)
        self.ld_w1q = self.cnt["sLD"]
        self.x1f = [self.p3es.enter_context(
            nc.sbuf_tensor(f"p3x{i}", [128, DC, TT], F32)) for i in range(2)]

        with contextlib.ExitStack() as ses:
            S = ses.enter_context
            rowA = S(nc.sbuf_tensor("rowA2", [1, T], F32))
            rowB = S(nc.sbuf_tensor("rowB2", [1, T], F32))

            if True:
                rD = 1.0 / self.D
                def vm(v):
                    w(v, "sDV")
                    step = self.chain(v)
                    step(v.tensor_scalar(out=tm["m3"][:, :], in0=tm["sum3"][:, :],
                                         scalar1=rD, scalar2=None, op0=Op.mult))
                    step(v.tensor_scalar(out=tm["tmp1"][:, :], in0=tm["ssq3"][:, :],
                                         scalar1=rD, scalar2=None, op0=Op.mult))
                    step(v.tensor_tensor(out=tm["tmp0"][:, :], in0=tm["m3"][:, :],
                                         in1=tm["m3"][:, :], op=Op.mult))
                    step(v.tensor_tensor(out=tm["tmp1"][:, :], in0=tm["tmp1"][:, :],
                                         in1=tm["tmp0"][:, :], op=Op.subtract))
                    step(v.tensor_scalar(out=tm["tmp1"][:, :], in0=tm["tmp1"][:, :],
                                         scalar1=LN_EPS, scalar2=None, op0=Op.add))
                sec("vector", vm)
                self.sqrt_newton("tmp1", "tmp0")
                def vm2(v):
                    w(v, "sDV")
                    step = self.chain(v)
                    step(v.reciprocal(out=tm["r3"][:, :], in_=tm["tmp0"][:, :]))
                sec("vector", vm2)

                def vq(v):
                    w(v, "sDV")
                    step = self.chain(v)
                    step(v.tensor_tensor(out=tm["qmin_tm"][:, :], in0=tm["xmin_tm"][:, :],
                                         in1=tm["m3"][:, :], op=Op.subtract))
                    step(v.tensor_tensor(out=tm["qmin_tm"][:, :], in0=tm["qmin_tm"][:, :],
                                         in1=tm["r3"][:, :], op=Op.mult))
                    step(v.tensor_tensor(out=tm["qmax_tm"][:, :], in0=tm["xmax_tm"][:, :],
                                         in1=tm["m3"][:, :], op=Op.subtract))
                    step(v.tensor_tensor(out=tm["qmax_tm"][:, :], in0=tm["qmax_tm"][:, :],
                                         in1=tm["r3"][:, :], op=Op.mult))
                    step(v.tensor_reduce(out=tm["tmp1"][:, 0:1], in_=tm["qmin_tm"][:, :],
                                         axis=AX.X, op=Op.min))
                    step(v.tensor_scalar(out=tm["tmp1"][:, 0:1], in0=tm["tmp1"][:, 0:1],
                                         scalar1=-1.0, scalar2=None, op0=Op.mult))
                    step(v.tensor_reduce(out=tm["tmp1"][:, 1:2], in_=tm["qmax_tm"][:, :],
                                         axis=AX.X, op=Op.max))
                sec("vector", vq)
                self.finalize_minmax_cc(tm["tmp1"][:, 0:1], tm["tmp1"][:, 1:2])
                self.allreduce(self.cc_in1, self.cc_out1)
                self.quant_scale_math("s1", "zp1", "lo1", "hi1", "rs1")

            def vab(v):
                w(v, "sDV")
                step = self.chain(v)
                step(v.tensor_scalar(out=tm["tmp1"][:, :], in0=tm["r3"][:, :],
                                     scalar1=self.bc["rs1"][:, 0:1], scalar2=None,
                                     op0=Op.mult))
                step(v.tensor_tensor(out=tm["tmp2"][:, :], in0=tm["m3"][:, :],
                                     in1=tm["tmp1"][:, :], op=Op.mult))
                step(v.tensor_scalar(out=tm["tmp2"][:, :], in0=tm["tmp2"][:, :],
                                     scalar1=-1.0, scalar2=None, op0=Op.mult))
            sec("vector", vab)
            self.row_from_tm("tmp1", rowA, 0, f32=True)
            self.row_from_tm("tmp2", rowB, 0, f32=True)
            self.bcast_from_row(rowA, self.Ab)
            self.bcast_from_row(rowB, self.Bb)
            self.dv_ab = self.cnt["sDV"]

    def phase3_ffn1(self):
        nc, sec, inc, w = self.nc, self.sec, self.inc, self.w
        D, DC, H, HC, T, TT, NW = (self.D, self.DC, self.H, self.HC,
                                   self.T, self.TT, self.NW)
        sc = self.sc
        fes = self.p3es
        if True:
            FS = fes.enter_context
            def vv(v):
                w(v, "sDV")
                w(v, "sGD")
                inc(v.tensor_tensor(out=sc["s1sw1"][:, :], in0=sc["s1"][:, :],
                                    in1=sc["sw1"][:, :], op=Op.mult),
                    "sDV")
            sec("vector", vv)
            self.bcast_scalar("s1sw1")

            NBLK = 4 * NW
            gmaxc = self.gmaxc
            gminc = self.gminc
            x1f = self.x1f
            xqw = [FS(nc.sbuf_tensor(f"p3q{i}", [128, DC, TT], BF16))
                   for i in range(2)]
            qbuf = [FS(nc.sbuf_tensor(f"p3g{i}", [128, 8, TT], F32))
                    for i in range(2)]
            uq = [FS(nc.sbuf_tensor(f"p3u{i}", [128, TT], F32))
                  for i in range(2)]
            uqI = FS(nc.sbuf_tensor("p3uI", [128, TT], I32))
            b1prb = FS(nc.sbuf_tensor("b1prb", [1, H], BF16))
            onesb3 = FS(nc.sbuf_tensor("onesb3", [1, TT], BF16))
            b1pr = FS(nc.sbuf_tensor("b1pr", [1, H // 2], F32))
            ld_b1h = {}
            for h2 in range(2):
                def ldb1(s, h2=h2):
                    if h2 == 0:
                        w(s, "sDV", self.dv_ab)
                    else:
                        w(s, "sDV")
                    w(s, "sLD")
                    inc(s.dma_start(out=b1pr[:, :],
                                    in_=self.b1r_h[0:1,
                                                   h2 * (H // 2):(h2 + 1) * (H // 2)]),
                        "sLD", 16)
                sec("sync", ldb1)
                ld_b1h[h2] = self.cnt["sLD"]
                def vb1(v, h2=h2):
                    w(v, "sLD", ld_b1h[h2])
                    w(v, "sDV")
                    step = self.chain(v)
                    if h2 == 0:
                        step(v.tensor_tensor(out=sc["t0"][:, :],
                                             in0=sc["rs1"][:, :],
                                             in1=sc["rsw1"][:, :], op=Op.mult))
                    step(v.tensor_scalar(out=b1pr[:, :], in0=b1pr[:, :],
                                         scalar1=sc["t0"][:, 0:1], scalar2=None,
                                         op0=Op.mult))
                    step(v.tensor_copy(
                        out=b1prb[0:1, h2 * (H // 2):(h2 + 1) * (H // 2)],
                        in_=b1pr[:, :]))
                    if h2 == 0:
                        inc(v.memset(onesb3[:, :], 1.0), "sDV")
                sec("vector", vb1)
            dv_b1 = self.cnt["sDV"]
            ps_h = [FS(nc.psum_tensor(f"ps_h{i}", [128, TT], F32))
                    for i in range(4)]
            ld_w, dv_q, pe_h, pe_w, ac_h, gp_q, st_q = {}, {}, {}, {}, {}, {}, {}
            def ld0(s):
                w(s, "sST")
                w(s, "sLD")
                inc(s.dma_start(out=x1f[0][:, :, :], in_=self.x1r[:, :, 0:TT]),
                    "sLD", 16)
            sec("sync", ld0)
            ld_w[0] = self.cnt["sLD"]
            def emit_qz(tt):
                bi = tt % 2
                cb = tt * TT
                def qz(v, tt=tt, bi=bi, cb=cb):
                    w(v, "sLD", ld_w[tt])
                    if tt >= 2:
                        w(v, "sPE", pe_w[tt - 2])
                    step = self.chain(v)
                    for c in range(DC):
                        step(v.tensor_tensor(out=uq[0][:, :], in0=x1f[bi][:, c, :],
                                             in1=self.Ab[:, cb:cb + TT],
                                             op=Op.mult))
                        step(v.tensor_tensor(out=uq[1][:, :], in0=uq[0][:, :],
                                             in1=self.Bb[:, cb:cb + TT],
                                             op=Op.add))
                        step(v.tensor_copy(out=uqI[:, :], in_=uq[1][:, :]))
                        step(v.tensor_scalar(out=xqw[bi][:, c, :], in0=uqI[:, :],
                                             scalar1=self.bc["lo1"][:, 0:1],
                                             scalar2=self.bc["hi1"][:, 0:1],
                                             op0=Op.max, op1=Op.min))
                sec("vector", qz)
                dv_q[tt] = self.cnt["sDV"]

            emit_qz(0)
            for tt in range(NW):
                if tt + 1 < NW:
                    def ldn(s, tt=tt):
                        if tt >= 1:
                            w(s, "sDV", dv_q[tt - 1])
                        w(s, "sLD")
                        inc(s.dma_start(
                            out=x1f[(tt + 1) % 2][:, :, :],
                            in_=self.x1r[:, :, (tt + 1) * TT:(tt + 2) * TT]),
                            "sLD", 16)
                    sec("sync", ldn)
                    ld_w[tt + 1] = self.cnt["sLD"]
                    emit_qz(tt + 1)
                bi = tt % 2
                for hc in range(HC):
                    gidx = tt * HC + hc
                    bank = gidx % 4
                    q = tt * 4 + hc // 8
                    def mmh(t, tt=tt, bi=bi, hc=hc, bank=bank, gidx=gidx):
                        if gidx == 0:
                            w(t, "sLD", self.ld_w1q)
                            w(t, "sDV", dv_b1)
                        w(t, "sDV", dv_q[tt])
                        if gidx >= 4:
                            w(t, "sAC", ac_h[gidx - 4])
                        for c in range(DC):
                            inc(t.matmul(ps_h[bank][:, :],
                                         self.w1q[:, c, hc * 128:(hc + 1) * 128],
                                         xqw[bi][:, c, :],
                                         start=(c == 0), stop=False), "sPE")
                        inc(t.matmul(ps_h[bank][:, :],
                                     b1prb[:, hc * 128:(hc + 1) * 128],
                                     onesb3[:, :],
                                     start=False, stop=True), "sPE")
                    sec("tensor", mmh)
                    pe_h[gidx] = self.cnt["sPE"]
                    def evh(s_, hc=hc, bank=bank, gidx=gidx, q=q):
                        w(s_, "sPE", pe_h[gidx])
                        if q >= 2 and hc % 8 == 0:
                            w(s_, "sGD", st_q[q - 2])
                            w(s_, "sDV", gp_q[q - 2])
                        inc(s_.activation(out=qbuf[q % 2][:, hc % 8, :],
                                          in_=ps_h[bank][:, :], func=Act.Gelu,
                                          bias=0.0,
                                          scale=self.bc["s1sw1"][:, 0:1]), "sAC")
                    sec("scalar", evh)
                    ac_h[gidx] = self.cnt["sAC"]
                    if hc % 8 == 7:
                        def mnx(v, gidx=gidx, q=q):
                            w(v, "sAC", ac_h[gidx])
                            sl = qbuf[q % 2][:, :, :]
                            inc(v.tensor_reduce(out=gmaxc[:, q:q + 1],
                                                in_=sl, axis=AX.XY, op=Op.max),
                                "sDV")
                            v.wait_ge(self.sems["sDV"], self.cnt["sDV"])
                            inc(v.tensor_reduce(out=gminc[:, q:q + 1],
                                                in_=sl, axis=AX.XY, op=Op.min),
                                "sDV")
                        sec("vector", mnx)
                        gp_q[q] = self.cnt["sDV"]
                        def sth(g, tt=tt, hc=hc, gidx=gidx, q=q):
                            w(g, "sAC", ac_h[gidx])
                            w(g, "sGD")
                            inc(g.dma_start(
                                out=self.gh[:, tt, hc - 7:hc + 1, :],
                                in_=qbuf[q % 2][:, :, :]), "sGD", 16)
                        sec("gpsimd", sth)
                        st_q[q] = self.cnt["sGD"]
                pe_w[tt] = self.cnt["sPE"]

            def gred(v):
                w(v, "sDV")
                step = self.chain(v)
                step(v.tensor_reduce(out=self.tm["tmp1"][:, 0:1],
                                     in_=gminc[:, 0:NBLK], axis=AX.X, op=Op.min))
                step(v.tensor_scalar(out=self.tm["tmp1"][:, 0:1],
                                     in0=self.tm["tmp1"][:, 0:1],
                                     scalar1=-1.0, scalar2=None, op0=Op.mult))
                step(v.tensor_reduce(out=self.tm["tmp1"][:, 1:2],
                                     in_=gmaxc[:, 0:NBLK], axis=AX.X, op=Op.max))
            sec("vector", gred)
            self.finalize_minmax_cc(self.tm["tmp1"][:, 0:1], self.tm["tmp1"][:, 1:2])
            snap = dict(self.cnt)
            self.allreduce(self.cc_in2, self.cc_out2)
            def mkj():
                def f(e):
                    for s_ in ("sLD", "sST", "sPE", "sDV", "sAC", "sGD", "sGP"):
                        if snap[s_] > 0:
                            e.wait_ge(self.sems[s_], snap[s_])
                return f
            for e in ("sync", "scalar", "vector", "tensor"):
                sec(e, mkj())
        fes.close()
        self.ab_es.close()

    def phase4_ffn2(self):
        nc, sec, inc, w = self.nc, self.sec, self.inc, self.w
        D, DC, H, HC, T, TT, NW = (self.D, self.DC, self.H, self.HC,
                                   self.T, self.TT, self.NW)
        sc = self.sc

        with contextlib.ExitStack() as fes:
            FS = fes.enter_context
            w2q = FS(nc.sbuf_tensor("w2q", [128, HC, D], BF16))
            b2row = FS(nc.sbuf_tensor("b2row4", [1, D], F32))
            b2p_rb = FS(nc.sbuf_tensor("b2p_rb", [1, D], BF16))
            orow = FS(nc.sbuf_tensor("orowb", [1, 512], BF16))

            def ldw2(s):
                w(s, "sWS", self.w2q_ready[0])
                w(s, "sWT", self.w2q_ready[1])
                for c in range(0, HC, 8):
                    w(s, "sLD")
                    inc(s.dma_start(out=w2q[:, c:c + 8, :],
                                    in_=self.w2qhw[:, c:c + 8, :]), "sLD", 16)
            sec("sync", ldw2)
            ld_w2q = self.cnt["sLD"]

            self.quant_scale_math("s2", "zp2", "lo2", "hi2", "rs2")

            def ldb2(s):
                w(s, "sLD")
                inc(s.dma_start(out=b2row[:, :], in_=self.b2r_h), "sLD", 16)
            sec("sync", ldb2)
            ldb = self.cnt["sLD"]
            def vv(v):
                w(v, "sDV")
                w(v, "sGD")
                w(v, "sLD", ldb)
                step = self.chain(v)
                step(v.tensor_tensor(out=sc["s2sw2"][:, :], in0=sc["s2"][:, :],
                                     in1=sc["sw2"][:, :], op=Op.mult))
                step(v.tensor_tensor(out=sc["t0"][:, :], in0=sc["rs2"][:, :],
                                     in1=sc["rsw2"][:, :], op=Op.mult))
                step(v.tensor_scalar(out=b2row[:, :], in0=b2row[:, :],
                                     scalar1=sc["t0"][:, 0:1], scalar2=None,
                                     op0=Op.mult))
                step(v.tensor_copy(out=b2p_rb[:, :], in_=b2row[:, :]))
                inc(v.memset(orow[:, :], 1.0), "sDV")
            sec("vector", vv)
            self.bcast_scalar("s2sw2")

            HH = HC // 2
            TTs = 256
            gf = [FS(nc.sbuf_tensor(f"gf{i}", [128, HH, TT], F32))
                  for i in range(2)]
            hq = [FS(nc.sbuf_tensor(f"hq{i}", [128, HC, TTs], BF16))
                  for i in range(2)]
            xres = [FS(nc.sbuf_tensor(f"x2res{i}", [128, DC, TTs], F32))
                    for i in range(2)]
            oout = FS(nc.sbuf_tensor("oout", [128, DC, TTs], F32))
            gfI = FS(nc.sbuf_tensor("gfI", [128, 8, TTs], I32))
            ps_o = [FS(nc.psum_tensor(f"ps_o{i}", [128, TTs], F32))
                    for i in range(2)]

            ld_half, ac_half, dv_clast, dv_csub = {}, {}, {}, {}
            ld_x, pe_o, dv_o, st_o, pe_sub = {}, [], [], [], {}
            dv_rs2 = self.cnt["sDV"]

            def emit_loads(tt):
                for half in range(2):
                    hid = tt * 2 + half
                    def ldh(s, tt=tt, half=half, hid=hid):
                        w(s, "sLD")
                        if hid >= 2:
                            w(s, "sDV", dv_clast[hid - 2])
                        inc(s.dma_start(
                            out=gf[hid % 2][:, :, :],
                            in_=self.gh[:, tt, half * HH:(half + 1) * HH, :]),
                            "sLD", 16)
                    sec("sync", ldh)
                    ld_half[hid] = self.cnt["sLD"]
                for half in range(2):
                    hid = tt * 2 + half
                    def rnd(s_, hid=hid):
                        w(s_, "sLD", ld_half[hid])
                        w(s_, "sDV", dv_rs2)
                        inc(s_.activation(out=gf[hid % 2][:, :, :],
                                          in_=gf[hid % 2][:, :, :], func=Act.Copy,
                                          bias=0.0,
                                          scale=self.bc["rs2"][:, 0:1]), "sAC")
                    sec("scalar", rnd)
                    ac_half[hid] = self.cnt["sAC"]

            def emit_clamp(tt, sub):
                s_id = tt * 2 + sub
                for half in range(2):
                    hid = tt * 2 + half
                    def clp(v, sub=sub, half=half, hid=hid, s_id=s_id):
                        w(v, "sAC", ac_half[hid])
                        if s_id >= 2 and half == 0:
                            w(v, "sPE", pe_sub[s_id - 2])
                        step = self.chain(v)
                        for p in range(2):
                            sl = gf[hid % 2][:, p * 8:(p + 1) * 8,
                                             sub * TTs:(sub + 1) * TTs]
                            hsl = hq[sub][:, half * HH + p * 8:
                                          half * HH + (p + 1) * 8, :]
                            step(v.tensor_copy(out=gfI[:, :, :], in_=sl))
                            step(v.tensor_scalar(out=hsl, in0=gfI[:, :, :],
                                                 scalar1=self.bc["lo2"][:, 0:1],
                                                 scalar2=self.bc["hi2"][:, 0:1],
                                                 op0=Op.max, op1=Op.min))
                    sec("vector", clp)
                    dv_csub[(tt, sub)] = self.cnt["sDV"]
                    if sub == 1:
                        dv_clast[tt * 2 + half] = self.cnt["sDV"]

            emit_loads(0)
            emit_clamp(0, 0)
            emit_clamp(0, 1)
            for tt in range(NW):
                for sub in range(2):
                    s_id = tt * 2 + sub
                    def ldx(s, s_id=s_id):
                        w(s, "sLD")
                        if s_id >= 2:
                            w(s, "sDV", dv_o[(s_id - 2) * DC + DC - 1])
                        inc(s.dma_start(
                            out=xres[s_id % 2][:, :, :],
                            in_=self.x1r[:, :, s_id * TTs:(s_id + 1) * TTs]),
                            "sLD", 16)
                    sec("sync", ldx)
                    ld_x[s_id] = self.cnt["sLD"]
                    for dc in range(DC):
                        oi = s_id * DC + dc
                        def mmo(t, tt=tt, sub=sub, dc=dc, oi=oi):
                            if oi == 0:
                                w(t, "sLD", ld_w2q)
                            w(t, "sDV", dv_csub[(tt, sub)])
                            if oi >= 2:
                                w(t, "sDV", dv_o[oi - 2])
                            for hc in range(HC):
                                inc(t.matmul(ps_o[dc % 2][:, :],
                                             w2q[:, hc, dc * 128:(dc + 1) * 128],
                                             hq[sub][:, hc, :],
                                             start=(hc == 0), stop=False), "sPE")
                            inc(t.matmul(ps_o[dc % 2][:, :],
                                         b2p_rb[:, dc * 128:(dc + 1) * 128],
                                         orow[:, 0:TTs],
                                         start=False, stop=True), "sPE")
                        sec("tensor", mmo)
                        pe_o.append(self.cnt["sPE"])
                        def evo(v, s_id=s_id, dc=dc, oi=oi):
                            w(v, "sPE", pe_o[oi])
                            w(v, "sLD", ld_x[s_id])
                            if oi >= 8:
                                w(v, "sGD", st_o[oi - 8])
                            inc(v.scalar_tensor_tensor(
                                out=oout[:, dc, :],
                                in0=ps_o[dc % 2][:, :],
                                scalar=self.bc["s2sw2"][:, 0:1],
                                in1=xres[s_id % 2][:, dc, :],
                                op0=Op.mult, op1=Op.add), "sDV")
                        sec("vector", evo)
                        dv_o.append(self.cnt["sDV"])
                        def sto(g, s_id=s_id, dc=dc, oi=oi):
                            w(g, "sDV", dv_o[oi])
                            w(g, "sGD")
                            inc(g.dma_start(
                                out=self.outr[:, dc, s_id * TTs:(s_id + 1) * TTs],
                                in_=oout[:, dc, :]), "sGD", 16)
                        sec("gpsimd", sto)
                        st_o.append(self.cnt["sGD"])
                    pe_sub[s_id] = self.cnt["sPE"]
                    if sub == 0 and tt + 1 < NW:
                        emit_loads(tt + 1)
                        emit_clamp(tt + 1, 0)
                    if sub == 1 and tt + 1 < NW:
                        emit_clamp(tt + 1, 1)
            self.join_all()
        self.ffn_es.close()

    def finish(self):
        sec, w = self.sec, self.w
        def fw(s_):
            w(s_, "sST")
            w(s_, "sLD")
            w(s_, "sGD")
        sec("scalar", fw)


def host_pack(inputs, L, NB, n_cores):
    x = np.asarray(inputs["L0_attn_x"], dtype=np.float32)
    B = x.shape[0]
    assert B == NB * n_cores
    ident_b = np.eye(128, dtype=ml_dtypes.bfloat16)
    ident_f = np.eye(128, dtype=np.float32)
    shared = {
        "qkvwT": np.ascontiguousarray(np.asarray(inputs["L0_attn_qkv_w"], np.float32).T),
        "projwT": np.ascontiguousarray(np.asarray(inputs["L0_attn_proj_w"], np.float32).T),
        "w1T": np.ascontiguousarray(np.asarray(inputs["L0_ffn_W1"], np.float32).T),
        "w2T": np.ascontiguousarray(np.asarray(inputs["L0_ffn_W2"], np.float32).T),
        "nwr": np.asarray(inputs["L0_attn_norm_w"], np.float32).reshape(1, -1),
        "nbr": np.asarray(inputs["L0_attn_norm_b"], np.float32).reshape(1, -1),
        "qkvbr": np.asarray(inputs["L0_attn_qkv_b"], np.float32).reshape(1, -1),
        "projbr": np.asarray(inputs["L0_attn_proj_b"], np.float32).reshape(1, -1),
        "b1r": np.asarray(inputs["L0_ffn_b1"], np.float32).reshape(1, -1),
        "b2r": np.asarray(inputs["L0_ffn_b2"], np.float32).reshape(1, -1),
        "identb": ident_b,
        "identf": ident_f,
    }
    in_maps = []
    for c in range(n_cores):
        xc = x[c * NB:(c + 1) * NB].reshape(NB * L, x.shape[2])
        in_maps.append(dict(shared, xT=np.ascontiguousarray(xc.T)))
    return in_maps


def host_unpack(results, L, NB, n_cores, D=1024):
    outs = []
    for c in range(n_cores):
        o = np.asarray(results[c]["outT"])
        outs.append(np.ascontiguousarray(o.T).reshape(NB, L, D))
    return np.concatenate(outs, axis=0)


def kernel(**inputs):
    from concourse.bass_utils import run_bass_kernel_spmd
    x = np.asarray(inputs["L0_attn_x"])
    B, L, D = x.shape
    assert (B, L, D) == (16, 2048, 1024), (B, L, D)
    n_cores = 8
    NB = B // n_cores
    bld = Builder(L, NB, TT=512, n_cores=n_cores)
    nc = bld.build()
    in_maps = host_pack(inputs, L, NB, n_cores)
    res = run_bass_kernel_spmd(nc, in_maps, core_ids=list(range(n_cores)))
    return host_unpack(res.results, L, NB, n_cores, D)


# version 21 (final)
# speedup vs baseline: 1.1014x; 1.0252x over previous
import contextlib
import math

import numpy as np
import ml_dtypes

import concourse.bass as bass
import concourse.mybir as mybir
from concourse.bass_isa import ReduceOp

F32 = mybir.dt.float32
BF16 = mybir.dt.bfloat16
I32 = mybir.dt.int32
Act = mybir.ActivationFunctionType
Op = mybir.AluOpType
AX = mybir.AxisListType

LN_EPS = 1e-5
ALL_SEMS = ("sLD", "sST", "sGD", "sPE", "sDV", "sAC", "sGP", "sCC", "sWQ",
            "sWR", "sWS", "sWT")


class Builder:
    def __init__(self, L, NB, TT=512, n_cores=8):
        self.L, self.NB, self.TT, self.n_cores = L, NB, TT, n_cores
        self.D, self.DC, self.H, self.HC = 1024, 8, 4096, 32
        self.T = NB * L
        self.NT = self.T // 128
        self.NW = self.T // TT
        self.LC = L // 128
        self.LW = L // TT
        self.TC = TT // 128
        assert L % TT == 0 and TT % 128 == 0 and TT <= 512

    def sem_new(self, *names):
        for n in names:
            self.sems[n] = self.es.enter_context(self.nc.semaphore(n))
            self.cnt[n] = 0

    def inc(self, instr, s, k=1):
        instr.then_inc(self.sems[s], k)
        self.cnt[s] += k
        return self.cnt[s]

    def w(self, eng, s, val=None):
        if val is None:
            val = self.cnt[s]
        assert val <= self.cnt[s], (s, val, self.cnt[s])
        if val > 0:
            eng.wait_ge(self.sems[s], val)

    def sec(self, eng_name, fn):
        getattr(self.block, eng_name)(fn)

    def join_all(self):
        snap = dict(self.cnt)
        def mk():
            def f(e):
                for s in ALL_SEMS:
                    if snap[s] > 0:
                        e.wait_ge(self.sems[s], snap[s])
            return f
        for e in ("sync", "scalar", "vector", "tensor", "gpsimd"):
            self.sec(e, mk())

    def chain(self, v):
        def step(instr):
            self.inc(instr, "sDV")
            v.wait_ge(self.sems["sDV"], self.cnt["sDV"])
        return step

    def build(self):
        L, NB, TT, D, DC, H, HC, T, NT = (self.L, self.NB, self.TT, self.D,
                                          self.DC, self.H, self.HC, self.T,
                                          self.NT)
        nc = bass.Bass("TRN2", target_bir_lowering=False)
        self.nc = nc
        dt = nc.dram_tensor
        self.NWq = self.NW
        self.xT = dt("xT", [D, T], F32, kind="ExternalInput").ap()
        self.qkvwT = dt("qkvwT", [D, 3 * D], F32, kind="ExternalInput").ap()
        self.projwT = dt("projwT", [D, D], F32, kind="ExternalInput").ap()
        self.w1T = dt("w1T", [D, H], F32, kind="ExternalInput").ap()
        self.w2T = dt("w2T", [H, D], F32, kind="ExternalInput").ap()
        self.nwr_h = dt("nwr", [1, D], F32, kind="ExternalInput").ap()
        self.nbr_h = dt("nbr", [1, D], F32, kind="ExternalInput").ap()
        self.qkvbr_h = dt("qkvbr", [1, 3 * D], F32, kind="ExternalInput").ap()
        self.projbr_h = dt("projbr", [1, D], F32, kind="ExternalInput").ap()
        self.b1r_h = dt("b1r", [1, H], F32, kind="ExternalInput").ap()
        self.b2r_h = dt("b2r", [1, D], F32, kind="ExternalInput").ap()
        self.identb_h = dt("identb", [128, 128], BF16, kind="ExternalInput").ap()
        self.identf_h = dt("identf", [128, 128], F32, kind="ExternalInput").ap()
        self.outT = dt("outT", [D, T], F32, kind="ExternalOutput").ap()
        self.x1h = dt("x1h", [D, T], F32).ap()
        self.gh = dt("gh", [128, self.NW, self.HC, self.TT], F32).ap()
        self.vh = dt("vh", [D, L], BF16).ap()
        self.w1qh = dt("w1qh", [D, H], BF16).ap()
        self.w2qh = dt("w2qh", [H, D], BF16).ap()
        self.cc_in1 = dt("cc_in1", [1, 2], F32)
        self.cc_out1 = dt("cc_out1", [1, 2], F32)
        self.cc_in2 = dt("cc_in2", [1, 2], F32)
        self.cc_out2 = dt("cc_out2", [1, 2], F32)

        self.xTr = self.xT.rearrange("(c p) t -> p c t", p=128)
        self.x1r = self.x1h.rearrange("(c p) t -> p c t", p=128)
        self.vhr = self.vh.rearrange("(c p) t -> p c t", p=128)
        self.outr = self.outT.rearrange("(c p) t -> p c t", p=128)
        self.w1Tq = self.w1T.rearrange("(c p) (x h) -> p c x h", p=128, h=512)
        self.w2Tq = self.w2T.rearrange("(c p) (x d) -> p c x d", p=128, d=512)
        self.w1qhw = self.w1qh.rearrange("(c p) (x h) -> p c x h", p=128, h=512)
        self.w2qhw = self.w2qh.rearrange("(c p) (x d) -> p c x d", p=128, d=512)
        self.w1qhr = self.w1qh.rearrange("(c p) h -> p c h", p=128)

        es = contextlib.ExitStack()
        self.es = es
        with es:
            E = es.enter_context
            self.sems, self.cnt = {}, {}
            self.sem_new(*ALL_SEMS)

            self.identb = E(nc.sbuf_tensor("identb_s", [128, 128], BF16))
            self.identf = E(nc.sbuf_tensor("identf_s", [128, 128], F32))
            self.stage = E(nc.sbuf_tensor("stage", [128, 128], F32))
            self.stage2 = E(nc.sbuf_tensor("stage2", [128, 128], BF16))
            self.onesrowf = E(nc.sbuf_tensor("onesrowf", [1, 128], F32))
            self.e1row = E(nc.sbuf_tensor("e1row", [1, 2], F32))
            self.nwnb_colf = E(nc.sbuf_tensor("nwnb_colf", [128, 2 * DC], F32))
            self.b1cols = E(nc.sbuf_tensor("b1cols", [128, HC], F32))
            self.b2cols = E(nc.sbuf_tensor("b2cols", [128, DC], F32))
            tm = {}
            for name in ("sum1", "ssq1", "m1", "v1", "alpha", "invalpha",
                         "r1", "kscale", "tmp0", "tmp1", "tmp2",
                         "sum3", "ssq3", "m3", "r3", "xmin_tm", "xmax_tm",
                         "qmin_tm", "qmax_tm"):
                tm[name] = E(nc.sbuf_tensor("tm_" + name, [128, NT], F32))
            self.tm = tm
            self.onesNT = E(nc.sbuf_tensor("onesNT", [128, NT], F32))
            self.ones_col_b = E(nc.sbuf_tensor("ones_col_b", [128, 1], BF16))
            sc = {}
            for name in ("s1", "zp1", "lo1", "hi1", "rs1", "s1sw1",
                         "s2", "zp2", "lo2", "hi2", "rs2", "s2sw2",
                         "sw1", "rsw1", "sw2", "rsw2", "t0", "t1", "t2",
                         "onef"):
                sc[name] = E(nc.sbuf_tensor("sc_" + name, [1, 1], F32))
            sc["ti"] = E(nc.sbuf_tensor("sc_ti", [1, 1], I32))
            self.sc = sc
            bc = {}
            for name in ("lo1", "hi1", "lo2", "hi2", "s1sw1", "s2sw2",
                         "rsw1", "rsw2", "rs1", "rs2"):
                bc[name] = E(nc.sbuf_tensor("bc_" + name, [128, 1], F32))
            self.bc = bc
            self.wqrow = E(nc.sbuf_tensor("wqrow", [1, 2], F32))
            self.ccs = E(nc.sbuf_tensor("ccs", [1, 2], F32))
            self.cc2 = E(nc.sbuf_tensor("cc2", [2, 1], F32))

            self.block = E(nc.Block())

            self.phase0_prep()
            for b in range(NB):
                self.phase1_attn(b)
            self.phase2_stats3()
            self.phase3_ffn1()
            self.phase4_ffn2()
            self.finish()
        return nc

    def open_pool(self, stack):
        S = stack.enter_context
        self.ps_small = S(self.nc.psum_tensor(f"ps_sm{self.nc.next_id()}",
                                              [128, 128], F32))
        self.ps_bk = [S(self.nc.psum_tensor(f"ps_bk{self.nc.next_id()}",
                                            [128, 512], F32)) for _ in range(2)]

    def row_from_tm(self, name, dst, part, scale=None, f32=False):
        nc, sec, inc, w, NT = self.nc, self.sec, self.inc, self.w, self.NT
        src = self.tm[name]
        if scale is not None:
            def pre(v):
                w(v, "sDV")
                inc(v.tensor_scalar(out=self.tm["tmp0"][:, :], in0=src[:, :],
                                    scalar1=scale, scalar2=None, op0=Op.mult),
                    "sDV")
            sec("vector", pre)
            src = self.tm["tmp0"]
        def tr(t):
            w(t, "sDV")
            w(t, "sLD")
            inc(t.transpose(self.ps_small[0:NT, :], src[:, 0:NT],
                            self.identf[:, :]), "sPE")
        sec("tensor", tr)
        def ev(v):
            w(v, "sPE")
            w(v, "sLD")
            w(v, "sDV")
            inc(v.tensor_copy(out=self.stage[0:NT, :],
                              in_=self.ps_small[0:NT, :]), "sDV")
            if not f32:
                v.wait_ge(self.sems["sDV"], self.cnt["sDV"])
                inc(v.tensor_copy(out=self.stage2[0:NT, :],
                                  in_=self.stage[0:NT, :]), "sDV")
        sec("vector", ev)
        def pk(s):
            w(s, "sDV")
            srcb = self.stage[0:NT, :] if f32 else self.stage2[0:NT, :]
            w(s, "sLD")
            inc(s.dma_start(out=dst[part:part + 1, :], in_=srcb), "sLD", 16)
        sec("sync", pk)

    def bcast_from_row(self, row, dst):
        nc, sec, inc, w, T = self.nc, self.sec, self.inc, self.w, self.T
        for i in range(T // 512):
            def mm(t, i=i):
                w(t, "sLD")
                w(t, "sDV")
                inc(t.matmul(self.ps_bk[i % 2][:, :], self.onesrowf[:, :],
                             row[:, i * 512:(i + 1) * 512],
                             start=True, stop=True), "sPE")
            sec("tensor", mm)
            def ev(v, i=i):
                w(v, "sPE")
                inc(v.tensor_copy(out=dst[:, i * 512:(i + 1) * 512],
                                  in_=self.ps_bk[i % 2][:, :]), "sDV")
            sec("vector", ev)

    def bcast_scalar(self, name):
        nc, sec, inc, w = self.nc, self.sec, self.inc, self.w
        def mm(t):
            w(t, "sDV")
            inc(t.matmul(self.ps_small[:, 0:1], self.onesrowf[:, :],
                         self.sc[name][:, :], start=True, stop=True), "sPE")
        sec("tensor", mm)
        def ev(v):
            w(v, "sPE")
            inc(v.tensor_copy(out=self.bc[name][:, :],
                              in_=self.ps_small[:, 0:1]), "sDV")
        sec("vector", ev)

    def allreduce(self, cc_in, cc_out):
        sec, inc, w = self.sec, self.inc, self.w
        def cc(g):
            w(g, "sDV")
            w(g, "sLD")
            w(g, "sGD")
            inc(g.dma_start(out=cc_in.ap(), in_=self.ccs[:, :]), "sGD", 16)
            g.wait_ge(self.sems["sGD"], self.cnt["sGD"])
            inc(g.collective_compute(
                "AllReduce", Op.max,
                replica_groups=[list(range(self.n_cores))],
                ins=[cc_in.ap().opt()], outs=[cc_out.ap().opt()]), "sCC")
            g.wait_ge(self.sems["sCC"], self.cnt["sCC"])
            w(g, "sGD")
            inc(g.dma_start(out=self.ccs[:, :], in_=cc_out.ap()), "sGD", 16)
        sec("gpsimd", cc)

    def quant_scale_math(self, s_, zp_, lo_, hi_, rs_):
        sec, inc, w, sc = self.sec, self.inc, self.w, self.sc
        def vs(v):
            w(v, "sGD")
            w(v, "sDV")
            step = self.chain(v)
            step(v.tensor_tensor(out=sc["t0"][:, :], in0=self.ccs[:, 0:1],
                                 in1=self.ccs[:, 1:2], op=Op.add))
            step(v.tensor_scalar(out=sc[s_][:, :], in0=sc["t0"][:, :],
                                 scalar1=float(np.float32(1.0) / np.float32(255.0)),
                                 scalar2=1e-8, op0=Op.mult, op1=Op.max))
            step(v.reciprocal(out=sc[rs_][:, :], in_=sc[s_][:, :]))
            step(v.tensor_tensor(out=sc["t1"][:, :], in0=self.ccs[:, 0:1],
                                 in1=sc[rs_][:, :], op=Op.mult))
            step(v.tensor_copy(out=sc["ti"][:, :], in_=sc["t1"][:, :]))
            step(v.tensor_copy(out=sc["t1"][:, :], in_=sc["ti"][:, :]))
            step(v.tensor_scalar(out=sc[zp_][:, :], in0=sc["t1"][:, :],
                                 scalar1=-128.0, scalar2=-128.0,
                                 op0=Op.add, op1=Op.max))
            step(v.tensor_scalar(out=sc[zp_][:, :], in0=sc[zp_][:, :],
                                 scalar1=127.0, scalar2=None, op0=Op.min))
            step(v.tensor_scalar(out=sc[lo_][:, :], in0=sc[zp_][:, :],
                                 scalar1=-1.0, scalar2=-128.0,
                                 op0=Op.mult, op1=Op.add))
            step(v.tensor_scalar(out=sc[hi_][:, :], in0=sc[zp_][:, :],
                                 scalar1=-1.0, scalar2=127.0,
                                 op0=Op.mult, op1=Op.add))
        sec("vector", vs)
        for n in (lo_, hi_, rs_):
            self.bcast_scalar(n)

    def sqrt_newton(self, v_in, out_sqrt):
        sec, inc, w, tm = self.sec, self.inc, self.w, self.tm
        def sq(s):
            w(s, "sDV")
            inc(s.activation(out=tm["tmp2"][:, :], in_=tm[v_in][:, :],
                             func=Act.Sqrt), "sAC")
        sec("scalar", sq)
        def vm(v):
            w(v, "sAC")
            w(v, "sDV")
            step = self.chain(v)
            step(v.reciprocal(out=tm[out_sqrt][:, :], in_=tm["tmp2"][:, :]))
            step(v.tensor_tensor(out=tm[out_sqrt][:, :], in0=tm[out_sqrt][:, :],
                                 in1=tm[v_in][:, :], op=Op.mult))
            step(v.tensor_tensor(out=tm[out_sqrt][:, :], in0=tm[out_sqrt][:, :],
                                 in1=tm["tmp2"][:, :], op=Op.add))
            step(v.tensor_scalar(out=tm[out_sqrt][:, :], in0=tm[out_sqrt][:, :],
                                 scalar1=0.5, scalar2=None, op0=Op.mult))
        sec("vector", vm)

    def stats_pass(self, src_r, sum_name, ssq_name, minmax=None):
        nc, sec, inc, w = self.nc, self.sec, self.inc, self.w
        DC, TT, NW, TC, NT = self.DC, self.TT, self.NW, self.TC, self.NT
        with contextlib.ExitStack() as ses:
            S = ses.enter_context
            xf = [S(nc.sbuf_tensor(f"st_xf{nc.next_id()}", [128, DC, TT], F32))
                  for _ in range(2)]
            xb = [S(nc.sbuf_tensor(f"st_xb{nc.next_id()}", [128, DC, TT], BF16))
                  for _ in range(2)]
            sq = [S(nc.sbuf_tensor(f"st_sq{nc.next_id()}", [128, DC, TT], BF16))
                  for _ in range(2)]
            ps_sum = S(nc.psum_tensor(f"ps_su{nc.next_id()}", [128, NT], F32))
            ps_ssq = S(nc.psum_tensor(f"ps_sq{nc.next_id()}", [128, NT], F32))

            def ld0(s):
                w(s, "sPE")
                w(s, "sDV")
                w(s, "sGD")
                inc(s.dma_start(out=xf[0][:, :, :], in_=src_r[:, :, 0:TT]),
                    "sGD", 16)
            sec("gpsimd", ld0)
            ld_w = [self.cnt["sGD"]]
            dv_win, ac_win, pe_win, dv_cast = [], [], [], []
            for wi in range(NW):
                if wi + 1 < NW:
                    def ldn(s, wi=wi):
                        if wi >= 1:
                            w(s, "sDV", dv_win[wi - 1])
                            w(s, "sAC", ac_win[wi - 1])
                        w(s, "sGD")
                        inc(s.dma_start(out=xf[(wi + 1) % 2][:, :, :],
                                        in_=src_r[:, :, (wi + 1) * TT:(wi + 2) * TT]),
                            "sGD", 16)
                    sec("gpsimd", ldn)
                    ld_w.append(self.cnt["sGD"])
                bi = wi % 2
                def proc(v, wi=wi, bi=bi):
                    w(v, "sGD", ld_w[wi])
                    if wi >= 2:
                        w(v, "sPE", pe_win[wi - 2])
                    step = self.chain(v)
                    inc(v.tensor_copy(out=xb[bi][:, :, :], in_=xf[bi][:, :, :]),
                        "sDV")
                    dv_cast.append(self.cnt["sDV"])
                    if minmax is not None:
                        cmin, cmax = minmax
                        sl = slice(wi * TT, (wi + 1) * TT)
                        step(v.tensor_tensor(out=cmax[:, sl], in0=xf[bi][:, 0, :],
                                             in1=xf[bi][:, 1, :], op=Op.max))
                        for c in range(2, DC):
                            step(v.tensor_tensor(out=cmax[:, sl], in0=cmax[:, sl],
                                                 in1=xf[bi][:, c, :], op=Op.max))
                        step(v.tensor_tensor(out=cmin[:, sl], in0=xf[bi][:, 0, :],
                                             in1=xf[bi][:, 1, :], op=Op.min))
                        for c in range(2, DC):
                            step(v.tensor_tensor(out=cmin[:, sl], in0=cmin[:, sl],
                                                 in1=xf[bi][:, c, :], op=Op.min))
                sec("vector", proc)
                dv_win.append(self.cnt["sDV"])
                def sqr(s_, wi=wi, bi=bi):
                    w(s_, "sDV", dv_cast[wi])
                    inc(s_.activation(out=sq[bi][:, :, :], in_=xb[bi][:, :, :],
                                      func=Act.Square), "sAC")
                sec("scalar", sqr)
                ac_win.append(self.cnt["sAC"])
                def mm(t, wi=wi, bi=bi):
                    w(t, "sDV", dv_cast[wi])
                    w(t, "sAC", ac_win[wi])
                    for j in range(TC):
                        col = wi * TC + j
                        for c in range(DC):
                            inc(t.matmul(ps_sum[:, col:col + 1],
                                         xb[bi][:, c, j * 128:(j + 1) * 128],
                                         self.ones_col_b[:, :],
                                         start=(c == 0), stop=(c == DC - 1)),
                                "sPE")
                        for c in range(DC):
                            inc(t.matmul(ps_ssq[:, col:col + 1],
                                         sq[bi][:, c, j * 128:(j + 1) * 128],
                                         self.ones_col_b[:, :],
                                         start=(c == 0), stop=(c == DC - 1)),
                                "sPE")
                sec("tensor", mm)
                pe_win.append(self.cnt["sPE"])
            def evs(v):
                w(v, "sPE", pe_win[-1])
                inc(v.tensor_copy(out=self.tm[sum_name][:, :], in_=ps_sum[:, :]),
                    "sDV")
                inc(v.tensor_copy(out=self.tm[ssq_name][:, :], in_=ps_ssq[:, :]),
                    "sDV")
            sec("vector", evs)

    def finalize_minmax_cc(self, negmin_col, max_col):
        nc, sec, inc, w = self.nc, self.sec, self.inc, self.w
        def cp(v):
            w(v, "sDV")
            step = self.chain(v)
            step(v.tensor_copy(out=self.tm["tmp0"][:, 0:1], in_=negmin_col))
            step(v.tensor_copy(out=self.tm["tmp0"][:, 1:2], in_=max_col))
        sec("vector", cp)
        def tr(t):
            w(t, "sDV")
            inc(t.transpose(self.ps_small[0:2, :], self.tm["tmp0"][:, 0:2],
                            self.identf[:, :]), "sPE")
        sec("tensor", tr)
        def rd(v):
            w(v, "sPE")
            inc(v.tensor_reduce(out=self.cc2[:, 0:1], in_=self.ps_small[0:2, :],
                                axis=AX.X, op=Op.max), "sDV")
        sec("vector", rd)
        def pk(s):
            w(s, "sDV")
            w(s, "sLD")
            inc(s.dma_start(out=self.ccs[0:1, :], in_=self.cc2[0:2, 0:1]),
                "sLD", 16)
        sec("sync", pk)

    def wq_pump(self, k, quant=True):
        nc, sec, inc, w = self.nc, self.sec, self.inc, self.w
        k *= max(1, 2048 // self.L)
        for _ in range(k):
            lim = 256 if self.wq_finalized else 128
            i = self.wq_step
            if i >= lim:
                return
            if i >= 128 and not quant:
                return
            pi = i % 128
            src, dst, rsw = self._wq_order[pi]
            if i in (0, 128):
                def ld01(s, i=i):
                    for j in (i, i + 1):
                        sem = "sWQ" if j % 2 == 0 else "sWR"
                        if j >= 2:
                            w(s, "sDV", self._wq_dv.get(j - 2, 0))
                        inc(s.dma_start(out=self._wabs[j % 2][:, :],
                                        in_=self._wq_order[j % 128][0]),
                            sem, 16)
                        self._wq_ld[j] = (sem, self.cnt[sem])
                sec("sync", ld01)
            if i < 128:
                def vab(v, i=i, pi=pi):
                    w(v, self._wq_ld[i][0], self._wq_ld[i][1])
                    am = self._am[0 if pi < 64 else 1]
                    col = pi % 64
                    inc(v.reduce_max(out=am[:, col:col + 1],
                                     in_=self._wabs[i % 2][:, :], axis=AX.X,
                                     apply_absolute_value=True), "sDV")
                sec("vector", vab)
                self._wq_dv[i] = self.cnt["sDV"]
                gate = ("sDV", self._wq_dv[i])
            else:
                def vrnd(v, i=i, rsw=rsw):
                    w(v, self._wq_ld[i][0], self._wq_ld[i][1])
                    if i >= 130:
                        sem_p, cnt_p = self._wq_st[i - 2]
                        w(v, sem_p, cnt_p)
                    buf = self._wabs[i % 2]
                    step = self.chain(v)
                    step(v.tensor_scalar(out=buf[:, :], in0=buf[:, :],
                                         scalar1=self.bc[rsw][:, 0:1],
                                         scalar2=None, op0=Op.mult))
                    step(v.tensor_copy(out=self._wqI[:, :], in_=buf[:, :]))
                    step(v.tensor_copy(out=self._wqst[i % 2][:, :],
                                       in_=self._wqI[:, :]))
                sec("vector", vrnd)
                self._wq_dv[i] = self.cnt["sDV"]
                gate = ("sDV", self._wq_dv[i])
                st_sem = "sWS" if i % 2 == 0 else "sWT"
                def stp(s, i=i, dst=dst, st_sem=st_sem):
                    w(s, "sDV", self._wq_dv[i])
                    inc(s.dma_start(out=dst, in_=self._wqst[i % 2][:, :]),
                        st_sem, 16)
                sec("sync", stp)
                self._wq_st[i] = (st_sem, self.cnt[st_sem])
                if pi == 63:
                    self.w1q_ready = (self.cnt["sWS"], self.cnt["sWT"])
                if pi == 127:
                    self.w2q_ready = (self.cnt["sWS"], self.cnt["sWT"])
            if i + 2 < lim:
                def ldn(s, i=i, gate=gate):
                    sem = "sWQ" if i % 2 == 0 else "sWR"
                    w(s, gate[0], gate[1])
                    inc(s.dma_start(out=self._wabs[(i + 2) % 2][:, :],
                                    in_=self._wq_order[(i + 2) % 128][0]),
                        sem, 16)
                    self._wq_ld[i + 2] = (sem, self.cnt[sem])
                sec("sync", ldn)
            self.wq_step = i + 1

    def wq_finalize(self):
        nc, sec, inc, w = self.nc, self.sec, self.inc, self.w
        sc, tm = self.sc, self.tm
        assert self.wq_step == 128, self.wq_step
        with nc.psum_tensor("ps_wqf", [128, 512], F32) as ps_f:
            def vr(v):
                w(v, "sDV")
                step = self.chain(v)
                step(v.tensor_reduce(out=tm["tmp0"][:, 0:1],
                                     in_=self._am[0][:, :], axis=AX.X,
                                     op=Op.max))
                step(v.tensor_reduce(out=tm["tmp0"][:, 1:2],
                                     in_=self._am[1][:, :], axis=AX.X,
                                     op=Op.max))
            sec("vector", vr)
            def tr(t):
                w(t, "sDV")
                inc(t.transpose(ps_f[0:2, 0:128], tm["tmp0"][:, 0:2],
                                self.identf[:, :]), "sPE")
            sec("tensor", tr)
            def fin(v):
                w(v, "sPE")
                step = self.chain(v)
                step(v.tensor_reduce(out=self.cc2[:, 0:1], in_=ps_f[0:2, 0:128],
                                     axis=AX.X, op=Op.max))
            sec("vector", fin)
            def pk(s):
                w(s, "sDV")
                inc(s.dma_start(out=self.wqrow[0:1, :], in_=self.cc2[0:2, 0:1]),
                    "sGP", 16)
            sec("sync", pk)
            def fin2(v):
                w(v, "sGP")
                step = self.chain(v)
                for col, sw, rsw in ((0, "sw1", "rsw1"), (1, "sw2", "rsw2")):
                    step(v.tensor_scalar(
                        out=sc[sw][:, :], in0=self.wqrow[:, col:col + 1],
                        scalar1=float(np.float32(1.0) / np.float32(127.0)),
                        scalar2=1e-8, op0=Op.mult, op1=Op.max))
                    step(v.reciprocal(out=sc[rsw][:, :], in_=sc[sw][:, :]))
            sec("vector", fin2)
            for rsw in ("rsw1", "rsw2"):
                def mm(t, rsw=rsw):
                    w(t, "sDV")
                    inc(t.matmul(ps_f[:, 0:1], self.onesrowf[:, :],
                                 sc[rsw][:, :], start=True, stop=True), "sPE")
                sec("tensor", mm)
                def ev(v, rsw=rsw):
                    w(v, "sPE")
                    inc(v.tensor_copy(out=self.bc[rsw][:, :],
                                      in_=ps_f[:, 0:1]), "sDV")
                sec("vector", ev)
        self.wq_finalized = True

    def phase0_prep(self):
        nc, sec, inc, w = self.nc, self.sec, self.inc, self.w
        D, DC, T, NT, TT, NW, TC = (self.D, self.DC, self.T, self.NT,
                                    self.TT, self.NW, self.TC)
        tm = self.tm

        self.attn_es = contextlib.ExitStack()
        AE = self.attn_es.enter_context
        self._wq_order = (
            [(self.w1Tq[:, i // 8, i % 8, :],
              self.w1qhw[:, i // 8, i % 8, :], "rsw1") for i in range(64)] +
            [(self.w2Tq[:, i // 2, i % 2, :],
              self.w2qhw[:, i // 2, i % 2, :], "rsw2") for i in range(64)])
        self._wq_ld, self._wq_st, self._wq_dv, self._wq_ac = {}, {}, {}, {}
        self.wq_step = 0
        self.wq_finalized = False
        self.Wqkv = AE(nc.sbuf_tensor("Wqkv", [128, DC, 3 * D], BF16))
        self.PW = AE(nc.sbuf_tensor("PW", [128, DC, D], BF16))
        self.alphab = AE(nc.sbuf_tensor("alphab", [128, T], BF16))
        self.r1pb = AE(nc.sbuf_tensor("r1pb", [128, T], BF16))
        self.crow_b = AE(nc.sbuf_tensor("crow_b", [2, 3 * D], BF16))
        self.prow_b = AE(nc.sbuf_tensor("prow_b", [2, D], BF16))
        self.tokqk_b = AE(nc.sbuf_tensor("tokqk_b", [2, T], BF16))
        self.tokpj_b = AE(nc.sbuf_tensor("tokpj_b", [2, T], BF16))

        with contextlib.ExitStack() as p0a:
            self.open_pool(p0a)
            stg = contextlib.ExitStack()
            PA = stg.enter_context
            self.nwnb = PA(nc.sbuf_tensor("nwnb", [2, D], F32))
            self.qkvbr = PA(nc.sbuf_tensor("qkvbr_s", [1, 3 * D], F32))
            self.projbr = PA(nc.sbuf_tensor("projbr_s", [1, D], F32))
            self.b1row = PA(nc.sbuf_tensor("b1row", [1, self.H], F32))
            self.b2row = PA(nc.sbuf_tensor("b2row", [1, D], F32))
            self.crow_f = PA(nc.sbuf_tensor("crow_f", [2, 3 * D], F32))
            self.onesrowb = PA(nc.sbuf_tensor("onesrowb", [1, D], BF16))

            def ld(s):
                for out, in_ in ((self.identb[:, :], self.identb_h),
                                 (self.identf[:, :], self.identf_h),
                                 (self.nwnb[0:1, :], self.nwr_h),
                                 (self.nwnb[1:2, :], self.nbr_h),
                                 (self.qkvbr[:, :], self.qkvbr_h),
                                 (self.projbr[:, :], self.projbr_h),
                                 (self.b1row[:, :], self.b1r_h),
                                 (self.b2row[:, :], self.b2r_h)):
                    w(s, "sLD")
                    inc(s.dma_start(out=out, in_=in_), "sLD", 16)
            sec("sync", ld)
            n_small = self.cnt["sLD"]

            def ms(v):
                w(v, "sLD", n_small)
                inc(v.memset(self.onesNT[:, :], 1.0), "sDV")
                inc(v.memset(self.ones_col_b[:, :], 1.0), "sDV")
                inc(v.memset(self.sc["onef"][:, :], 1.0), "sDV")
                inc(v.memset(self.onesrowf[:, :], 1.0), "sDV")
                inc(v.memset(self.e1row[:, 0:1], 0.0), "sDV")
                inc(v.memset(self.e1row[:, 1:2], 1.0), "sDV")
                inc(v.memset(self.onesrowb[:, :], 1.0), "sDV")
            sec("vector", ms)

            tr_jobs = (
                [(self.nwnb[:, c * 128:(c + 1) * 128], 2,
                  self.nwnb_colf[:, 2 * c:2 * c + 2]) for c in range(DC)] +
                [(self.b1row[:, c * 128:(c + 1) * 128], 1,
                  self.b1cols[:, c:c + 1]) for c in range(self.HC)] +
                [(self.b2row[:, c * 128:(c + 1) * 128], 1,
                  self.b2cols[:, c:c + 1]) for c in range(DC)])
            dv_tr = {}
            for ji, (src_sl, nr, dst_sl) in enumerate(tr_jobs):
                def trc(t, ji=ji, src_sl=src_sl, nr=nr):
                    if ji == 0:
                        w(t, "sLD", n_small)
                        w(t, "sDV")
                    if ji >= 2:
                        w(t, "sDV", dv_tr[ji - 2])
                    inc(t.transpose(self.ps_bk[ji % 2][0:128, 0:nr],
                                    src_sl, self.identf[0:nr, 0:nr]), "sPE")
                sec("tensor", trc)
                pe_tr_ = self.cnt["sPE"]
                def evc(v, ji=ji, nr=nr, dst_sl=dst_sl, pe_tr_=pe_tr_):
                    w(v, "sPE", pe_tr_)
                    inc(v.tensor_copy(out=dst_sl,
                                      in_=self.ps_bk[ji % 2][0:128, 0:nr]),
                        "sDV")
                sec("vector", evc)
                dv_tr[ji] = self.cnt["sDV"]
            dv_nwnb = dv_tr[DC - 1]

            def prow(v):
                w(v, "sLD", n_small)
                inc(v.tensor_copy(out=self.prow_b[0:1, :], in_=self.projbr[:, :]),
                    "sDV")
            sec("vector", prow)
            def prow2(s):
                w(s, "sDV")
                w(s, "sLD")
                inc(s.dma_start(out=self.prow_b[1:2, :], in_=self.onesrowb[:, :]),
                    "sLD", 16)
            sec("sync", prow2)

            with contextlib.ExitStack() as wes:
                WS = wes.enter_context
                wraw = [WS(nc.sbuf_tensor(f"wraw{i}", [128, 3 * D], F32))
                        for i in range(2)]
                praw = [WS(nc.sbuf_tensor(f"praw{i}", [128, D], F32))
                        for i in range(2)]
                ps_c12 = [WS(nc.psum_tensor(f"ps_c12_{i}", [2, 512], F32))
                          for i in range(5)]
                dv_done, pe_done, ld_done = {}, {}, {}
                for half in range(2):
                    sdv_half = self.cnt["sDV"]
                    for c in range(DC):
                        ci = half * DC + c
                        def ldc(s, c=c, ci=ci, half=half):
                            if ci >= 2:
                                w(s, "sDV", dv_done[ci - 2])
                                w(s, "sPE", pe_done[ci - 2])
                            w(s, "sLD")
                            inc(s.dma_start(out=wraw[ci % 2][:, :],
                                            in_=self.qkvwT[c * 128:(c + 1) * 128, :]),
                                "sLD", 16)
                            if half == 0:
                                w(s, "sLD")
                                inc(s.dma_start(out=praw[ci % 2][:, :],
                                                in_=self.projwT[c * 128:(c + 1) * 128, :]),
                                    "sLD", 16)
                        sec("sync", ldc)
                        ld_done[ci] = self.cnt["sLD"]
                        def mmc(t, c=c, ci=ci, half=half, sdv_half=sdv_half):
                            w(t, "sLD", ld_done[ci])
                            if c == 0:
                                w(t, "sDV", sdv_half)
                            for ot in range(3):
                                inc(t.matmul(ps_c12[ot][:, :],
                                             self.nwnb_colf[:, 2 * c:2 * c + 2],
                                             wraw[ci % 2][:, (half * 3 + ot) * 512:
                                                          (half * 3 + ot + 1) * 512],
                                             start=(c == 0), stop=False), "sPE")
                        sec("tensor", mmc)
                        pe_done[ci] = self.cnt["sPE"]
                        if half == 0:
                            def dvc(v, c=c, ci=ci):
                                w(v, "sLD", ld_done[ci])
                                w(v, "sDV", dv_nwnb)
                                inc(v.tensor_scalar(
                                    out=self.Wqkv[:, c, :], in0=wraw[ci % 2][:, :],
                                    scalar1=self.nwnb_colf[:, 2 * c:2 * c + 1],
                                    scalar2=None, op0=Op.mult), "sDV")
                                inc(v.tensor_copy(out=self.PW[:, c, :],
                                                  in_=praw[ci % 2][:, :]), "sDV")
                            sec("vector", dvc)
                        dv_done[ci] = self.cnt["sDV"]
                    def mmb2(t, half=half):
                        for ot in range(3):
                            o0 = (half * 3 + ot) * 512
                            inc(t.matmul(ps_c12[ot][:, :], self.e1row[:, :],
                                         self.qkvbr[:, o0:o0 + 512],
                                         start=False, stop=True), "sPE")
                    sec("tensor", mmb2)
                    pe_c12 = self.cnt["sPE"]
                    def evc(v, half=half):
                        w(v, "sPE", pe_c12)
                        w(v, "sLD", n_small)
                        for ot in range(3):
                            o0 = (half * 3 + ot) * 512
                            inc(v.tensor_copy(out=self.crow_f[:, o0:o0 + 512],
                                              in_=ps_c12[ot][:, :]), "sDV")
                    sec("vector", evc)
                def evb(v):
                    w(v, "sDV")
                    inc(v.tensor_copy(out=self.crow_b[:, :],
                                      in_=self.crow_f[:, :]), "sDV")
                sec("vector", evb)

            stg.close()
            AE = self.attn_es.enter_context
            self._wabs = [AE(nc.sbuf_tensor(f"wabs{i}", [128, 512], F32))
                          for i in range(2)]
            self._am = [AE(nc.sbuf_tensor(f"wam{i}", [128, 64], F32))
                        for i in range(2)]
            self._wqst = [AE(nc.sbuf_tensor(f"wqst{i}", [128, 512], BF16))
                          for i in range(2)]
            self._wqI = AE(nc.sbuf_tensor("wqI", [128, 512], I32))
            self.join_all()

            self.stats_pass(self.xTr, "sum1", "ssq1")
            self.rowA = p0a.enter_context(nc.sbuf_tensor("rowA", [1, T], F32))
            self.rowB = p0a.enter_context(nc.sbuf_tensor("rowB", [1, T], F32))

            rD = 1.0 / self.D
            def vm(v):
                w(v, "sDV")
                w(v, "sAC")
                step = self.chain(v)
                step(v.tensor_scalar(out=tm["m1"][:, :], in0=tm["sum1"][:, :],
                                     scalar1=rD, scalar2=None, op0=Op.mult))
                step(v.tensor_scalar(out=tm["tmp0"][:, :], in0=tm["ssq1"][:, :],
                                     scalar1=rD, scalar2=None, op0=Op.mult))
                step(v.tensor_tensor(out=tm["tmp1"][:, :], in0=tm["m1"][:, :],
                                     in1=tm["m1"][:, :], op=Op.mult))
                step(v.tensor_tensor(out=tm["v1"][:, :], in0=tm["tmp0"][:, :],
                                     in1=tm["tmp1"][:, :], op=Op.subtract))
                step(v.tensor_scalar(out=tm["tmp1"][:, :], in0=tm["v1"][:, :],
                                     scalar1=1.0 + LN_EPS, scalar2=LN_EPS * LN_EPS,
                                     op0=Op.mult, op1=Op.add))
            sec("vector", vm)
            self.sqrt_newton("tmp1", "invalpha")
            def vm2(v):
                w(v, "sDV")
                step = self.chain(v)
                step(v.reciprocal(out=tm["alpha"][:, :], in_=tm["invalpha"][:, :]))
                step(v.tensor_scalar(out=tm["tmp1"][:, :], in0=tm["v1"][:, :],
                                     scalar1=LN_EPS, scalar2=None, op0=Op.add))
            sec("vector", vm2)
            self.sqrt_newton("tmp1", "tmp0")
            def vm3(v):
                w(v, "sDV")
                step = self.chain(v)
                step(v.reciprocal(out=tm["r1"][:, :], in_=tm["tmp0"][:, :]))
                step(v.tensor_scalar(out=tm["kscale"][:, :], in0=tm["alpha"][:, :],
                                     scalar1=1.0 / math.sqrt(self.L),
                                     scalar2=None, op0=Op.mult))
                step(v.tensor_scalar(out=tm["tmp1"][:, :], in0=tm["r1"][:, :],
                                     scalar1=1.0, scalar2=None, op0=Op.add))
                step(v.tensor_tensor(out=tm["tmp2"][:, :], in0=tm["m1"][:, :],
                                     in1=tm["r1"][:, :], op=Op.mult))
                step(v.tensor_tensor(out=tm["tmp2"][:, :], in0=tm["tmp2"][:, :],
                                     in1=tm["invalpha"][:, :], op=Op.mult))
                step(v.tensor_scalar(out=tm["tmp2"][:, :], in0=tm["tmp2"][:, :],
                                     scalar1=-1.0, scalar2=None, op0=Op.mult))
            sec("vector", vm3)

            self.row_from_tm("invalpha", self.tokqk_b, 1)
            self.row_from_tm("invalpha", self.tokpj_b, 0)
            self.row_from_tm("tmp2", self.tokpj_b, 1)
            self.row_from_tm("alpha", self.rowA, 0, f32=True)
            self.row_from_tm("tmp1", self.rowB, 0, f32=True)
            self.row_from_tm("m1", self.tokqk_b, 0, scale=-1.0)
            self.bcast_from_row(self.rowA, self.alphab)
            self.bcast_from_row(self.rowB, self.r1pb)
        self.join_all()

    def phase1_attn(self, b):
        nc, sec, inc, w = self.nc, self.sec, self.inc, self.w
        D, DC, L, TT, LC, LW, TC = (self.D, self.DC, self.L, self.TT,
                                    self.LC, self.LW, self.TC)
        tm = self.tm
        t0 = b * L
        c0 = t0 // 128
        xTr, x1r, vhr = self.xTr, self.x1r, self.vhr

        with contextlib.ExitStack() as aes:
            A = aes.enter_context
            TTq = min(256, TT)
            LWq = L // TTq
            TCq = TTq // 128
            wT = A(nc.sbuf_tensor(f"wT{b}", [128, DC, D], BF16))
            recip = A(nc.sbuf_tensor(f"recip{b}", [128, DC], F32))
            qks = contextlib.ExitStack()
            QK = qks.enter_context
            q_s = QK(nc.sbuf_tensor(f"q_s{b}", [128, LC, D], BF16))
            k_s = QK(nc.sbuf_tensor(f"k_s{b}", [128, LC, D], BF16))
            xes = contextlib.ExitStack()
            XE = xes.enter_context
            xb = [XE(nc.sbuf_tensor(f"a_xb{b}{i}", [128, DC, TTq], BF16))
                  for i in range(2)]

            with contextlib.ExitStack() as qes:
                Q = qes.enter_context
                ps_qk = [Q(nc.psum_tensor(f"ps_qk{b}{i}", [128, 512], F32))
                         for i in range(8)]
                def ld0(g):
                    w(g, "sPE")
                    w(g, "sDV")
                    w(g, "sGD")
                    inc(g.dma_start(out=xb[0][:, :, :], in_=xTr[:, :, t0:t0 + TTq]),
                        "sGD", 16)
                sec("gpsimd", ld0)
                ld_w = [self.cnt["sGD"]]
                evac_dv, evac_ac, pe_qk = {}, {}, {}
                for wi in range(LWq):
                    if wi + 1 < LWq:
                        def ldn(g, wi=wi):
                            if wi >= 1:
                                w(g, "sPE", pe_qk[(wi - 1, TCq - 1)])
                            w(g, "sGD")
                            inc(g.dma_start(
                                out=xb[(wi + 1) % 2][:, :, :],
                                in_=xTr[:, :, t0 + (wi + 1) * TTq:t0 + (wi + 2) * TTq]),
                                "sGD", 16)
                        sec("gpsimd", ldn)
                        ld_w.append(self.cnt["sGD"])
                    self.wq_pump(8, quant=False)
                    bi = wi % 2
                    for j in range(TCq):
                        tau = wi * TCq + j
                        bank0 = 4 * (tau % 2)
                        def mm(t, wi=wi, bi=bi, j=j, tau=tau, bank0=bank0):
                            w(t, "sGD", ld_w[wi])
                            if tau >= 2:
                                w(t, "sDV", evac_dv[tau - 2])
                            for ot in range(4):
                                for c in range(DC):
                                    inc(t.matmul(
                                        ps_qk[bank0 + ot][:, :],
                                        xb[bi][:, c, j * 128:(j + 1) * 128],
                                        self.Wqkv[:, c, ot * 512:(ot + 1) * 512],
                                        start=(c == 0), stop=False), "sPE")
                                inc(t.matmul(
                                    ps_qk[bank0 + ot][:, :],
                                    self.tokqk_b[:, t0 + tau * 128:t0 + (tau + 1) * 128],
                                    self.crow_b[:, ot * 512:(ot + 1) * 512],
                                    start=False, stop=True), "sPE")
                        sec("tensor", mm)
                        pe_qk[(wi, j)] = self.cnt["sPE"]
                        def evq(v, wi=wi, j=j, tau=tau, bank0=bank0):
                            w(v, "sPE", pe_qk[(wi, j)])
                            for ot in range(2):
                                inc(v.tensor_scalar(
                                    out=q_s[:, tau, ot * 512:(ot + 1) * 512],
                                    in0=ps_qk[bank0 + ot][:, :],
                                    scalar1=tm["alpha"][:, c0 + tau:c0 + tau + 1],
                                    scalar2=None, op0=Op.mult), "sDV")
                            for ot in range(2):
                                inc(v.tensor_scalar(
                                    out=k_s[:, tau, ot * 512:(ot + 1) * 512],
                                    in0=ps_qk[bank0 + 2 + ot][:, :],
                                    scalar1=tm["kscale"][:, c0 + tau:c0 + tau + 1],
                                    scalar2=None, op0=Op.mult), "sDV")
                        sec("vector", evq)
                        evac_dv[tau] = self.cnt["sDV"]
                qk_dv_all = evac_dv[LC - 1]
                qk_ac_all = self.cnt["sAC"]
            self.wq_pump(2)

            with contextlib.ExitStack() as ves:
                V = ves.enter_context
                vwin = [V(nc.sbuf_tensor(f"vwin{b}", [128, TTq], BF16))] * 2
                ps_v = [V(nc.psum_tensor(f"ps_v{b}{i}", [128, TTq], F32))
                        for i in range(2)]
                def ldv0(g):
                    w(g, "sPE")
                    w(g, "sDV")
                    w(g, "sGD")
                    inc(g.dma_start(out=xb[0][:, :, :], in_=xTr[:, :, t0:t0 + TTq]),
                        "sGD", 16)
                sec("gpsimd", ldv0)
                ldv_w = [self.cnt["sGD"]]
                pev, dvv, stv = {}, {}, {}
                for wi in range(LWq):
                    if wi + 1 < LWq:
                        def ldn(g, wi=wi):
                            if wi >= 1:
                                w(g, "sPE", pev[(wi - 1, DC - 1)])
                            w(g, "sGD")
                            inc(g.dma_start(
                                out=xb[(wi + 1) % 2][:, :, :],
                                in_=xTr[:, :, t0 + (wi + 1) * TTq:t0 + (wi + 2) * TTq]),
                                "sGD", 16)
                        sec("gpsimd", ldn)
                        ldv_w.append(self.cnt["sGD"])
                    self.wq_pump(6)
                    bi = wi % 2
                    for oc in range(DC):
                        vidx = wi * DC + oc
                        bank = vidx % 2
                        def mmv(t, wi=wi, bi=bi, oc=oc, bank=bank, vidx=vidx):
                            w(t, "sGD", ldv_w[wi])
                            if vidx >= 2:
                                w(t, "sDV", dvv[vidx - 2])
                            for c in range(DC):
                                inc(t.matmul(
                                    ps_v[bank][:, :],
                                    self.Wqkv[:, c, 2048 + oc * 128:2048 + (oc + 1) * 128],
                                    xb[bi][:, c, :],
                                    start=(c == 0), stop=False), "sPE")
                            inc(t.matmul(
                                ps_v[bank][:, :],
                                self.crow_b[:, 2048 + oc * 128:2048 + (oc + 1) * 128],
                                self.tokqk_b[:, t0 + wi * TTq:t0 + (wi + 1) * TTq],
                                start=False, stop=True), "sPE")
                        sec("tensor", mmv)
                        pev[(wi, oc)] = self.cnt["sPE"]
                        def evv(v, wi=wi, oc=oc, bank=bank, vidx=vidx):
                            w(v, "sPE", pev[(wi, oc)])
                            if vidx >= 1:
                                w(v, "sST", stv[vidx - 1])
                            inc(v.tensor_copy(out=vwin[vidx % 2][:, :],
                                              in_=ps_v[bank][:, :]), "sDV")
                        sec("vector", evv)
                        dvv[vidx] = self.cnt["sDV"]
                        def sv(s_, wi=wi, oc=oc, vidx=vidx):
                            w(s_, "sDV", dvv[vidx])
                            w(s_, "sST")
                            inc(s_.dma_start(out=vhr[:, oc, wi * TTq:(wi + 1) * TTq],
                                             in_=vwin[vidx % 2][:, :]), "sST", 16)
                        sec("scalar", sv)
                        stv[vidx] = self.cnt["sST"]
                v_st_all = stv[LWq * DC - 1]
                v_dv_all = dvv[LWq * DC - 1]
            xes.close()
            self.wq_pump(2)
            with contextlib.ExitStack() as ses:
                S = ses.enter_context
                w_s = [S(nc.sbuf_tensor(f"w_s{b}{i}", [128, D], BF16))
                       for i in range(2)]
                rmax = S(nc.sbuf_tensor(f"rmax{b}", [128, 4], F32))
                rsum = S(nc.sbuf_tensor(f"rsum{b}", [128, 4], F32))
                ps_S = [S(nc.psum_tensor(f"ps_S{b}{i}", [128, 512], F32))
                        for i in range(4)]
                ps_w = [S(nc.psum_tensor(f"ps_w{b}{i}", [128, 128], BF16))
                        for i in range(2)]
                pe_S, dv_soft, ac_soft, pe_tr, dv_wT = {}, {}, {}, {}, {}

                def emit_tr(d1c):
                    for d2c in range(DC):
                        if d2c >= 2:
                            prev = (d1c, d2c - 2)
                        elif d1c >= 1:
                            prev = (d1c - 1, DC - 2 + d2c)
                        else:
                            prev = None
                        def trw(t, d1c=d1c, d2c=d2c, prev=prev):
                            w(t, "sAC", ac_soft[d1c])
                            if prev is not None:
                                w(t, "sDV", dv_wT[prev])
                            inc(t.transpose(ps_w[d2c % 2][:, :],
                                            w_s[d1c % 2][:, d2c * 128:(d2c + 1) * 128],
                                            self.identb[:, :]), "sPE")
                        sec("tensor", trw)
                        pe_tr[(d1c, d2c)] = self.cnt["sPE"]
                        def evw(v, d1c=d1c, d2c=d2c):
                            w(v, "sPE", pe_tr[(d1c, d2c)])
                            inc(v.tensor_copy(
                                out=wT[:, d2c, d1c * 128:(d1c + 1) * 128],
                                in_=ps_w[d2c % 2][:, :]), "sDV")
                        sec("vector", evw)
                        dv_wT[(d1c, d2c)] = self.cnt["sDV"]

                for d1c in range(DC):
                    pair = 2 * (d1c % 2)
                    def mmS(t, d1c=d1c, pair=pair):
                        w(t, "sAC", qk_ac_all)
                        w(t, "sDV", qk_dv_all)
                        if d1c == 0:
                            w(t, "sDV", v_dv_all)
                        if d1c >= 2:
                            w(t, "sAC", ac_soft[d1c - 2])
                        for half in range(2):
                            for tau in range(LC):
                                inc(t.matmul(
                                    ps_S[pair + half][:, :],
                                    q_s[:, tau, d1c * 128:(d1c + 1) * 128],
                                    k_s[:, tau, half * 512:(half + 1) * 512],
                                    start=(tau == 0), stop=(tau == LC - 1)), "sPE")
                    sec("tensor", mmS)
                    pe_S[d1c] = self.cnt["sPE"]
                    def soft(v, d1c=d1c, pair=pair):
                        w(v, "sPE", pe_S[d1c])
                        if d1c == 0:
                            w(v, "sST", v_st_all)
                        w(v, "sDV")
                        step = self.chain(v)
                        step(v.reduce_max(out=rmax[:, 0:1], in_=ps_S[pair][:, :],
                                          axis=AX.X))
                        step(v.reduce_max(out=rmax[:, 1:2], in_=ps_S[pair + 1][:, :],
                                          axis=AX.X))
                        step(v.tensor_tensor(out=rmax[:, 2:3], in0=rmax[:, 0:1],
                                             in1=rmax[:, 1:2], op=Op.max))
                        step(v.tensor_scalar(out=rmax[:, 3:4], in0=rmax[:, 2:3],
                                             scalar1=-1.0, scalar2=None,
                                             op0=Op.mult))
                    sec("vector", soft)
                    dv_max = self.cnt["sDV"]
                    def expg(s_, d1c=d1c, pair=pair):
                        w(s_, "sDV", dv_max)
                        if d1c >= 2:
                            w(s_, "sPE", pe_tr[(d1c - 2, DC - 1)])
                        for half in range(2):
                            inc(s_.activation(
                                out=w_s[d1c % 2][:, half * 512:(half + 1) * 512],
                                in_=ps_S[pair + half][:, :], func=Act.Exp,
                                bias=rmax[:, 3:4], scale=1.0,
                                accum_out=rsum[:, half:half + 1]), "sAC")
                    sec("scalar", expg)
                    ac_soft[d1c] = self.cnt["sAC"]
                    def sm2(v, d1c=d1c):
                        w(v, "sAC", ac_soft[d1c])
                        w(v, "sDV")
                        step = self.chain(v)
                        step(v.tensor_tensor(out=rsum[:, 2:3], in0=rsum[:, 0:1],
                                             in1=rsum[:, 1:2], op=Op.add))
                        step(v.reciprocal(out=recip[:, d1c:d1c + 1],
                                          in_=rsum[:, 2:3]))
                    sec("vector", sm2)
                    dv_soft[d1c] = self.cnt["sDV"]
                    if d1c >= 1:
                        emit_tr(d1c - 1)
                emit_tr(DC - 1)
                soft_dv_all = max(dv_soft.values())
                wT_dv_all = dv_wT[(DC - 1, DC - 1)]
                pe_S_all = self.cnt["sPE"]
            self.wq_pump(2)
            qks.close()

            TTa = min(256, TT)
            LWa = L // TTa
            with contextlib.ExitStack() as pes:
                PS = pes.enter_context
                vload = [PS(nc.sbuf_tensor(f"vload{b}{i}", [128, DC, TTa], BF16))
                         for i in range(2)]
                aT = [PS(nc.sbuf_tensor(f"aT{b}{i}", [128, DC, TTa], BF16))
                      for i in range(2)]
                xres = [PS(nc.sbuf_tensor(f"xres{b}{i}", [128, DC, TTa], F32))
                        for i in range(2)]
                xout = [PS(nc.sbuf_tensor(f"xout{b}{i}", [128, DC, TTa], F32))
                        for i in range(2)]
                tproj = PS(nc.sbuf_tensor(f"tproj{b}", [128, TTa], F32))
                tln = PS(nc.sbuf_tensor(f"tln{b}", [128, TTa], F32))
                ps_a = [PS(nc.psum_tensor(f"ps_a{b}{i}", [128, TTa], F32))
                        for i in range(2)]
                ps_p = [PS(nc.psum_tensor(f"ps_p{b}{i}", [128, TTa], F32))
                        for i in range(2)]
                xb3 = PS(nc.sbuf_tensor(f"xb3{b}", [128, DC, TTa], BF16))
                sq3 = PS(nc.sbuf_tensor(f"sq3{b}", [128, DC, TTa], BF16))
                cminW = [PS(nc.sbuf_tensor(f"cminW{b}{i}", [128, TTa], F32))
                         for i in range(2)]
                cmaxW = [PS(nc.sbuf_tensor(f"cmaxW{b}{i}", [128, TTa], F32))
                         for i in range(2)]
                ps_s3 = PS(nc.psum_tensor(f"ps_s3{b}", [128, 2 * LWa], F32))
                ps_q3 = PS(nc.psum_tensor(f"ps_q3{b}", [128, 2 * LWa], F32))
                ps_mmx = [PS(nc.psum_tensor(f"ps_mx{b}{i}", [128, 128], F32))
                          for i in range(2)]
                (ac_s3, pe_s3mm, pe_mmx, dv_mmx, dv_w_list,
                 dv_mnx) = {}, {}, {}, {}, {}, {}
                def ldt0(s):
                    w(s, "sST", v_st_all)
                    w(s, "sDV", v_dv_all)
                    w(s, "sPE", pe_S_all)
                    w(s, "sLD")
                    inc(s.dma_start(out=vload[0][:, :, :], in_=vhr[:, :, 0:TTa]),
                        "sLD", 16)
                    w(s, "sLD")
                    inc(s.dma_start(out=xres[0][:, :, :], in_=xTr[:, :, t0:t0 + TTa]),
                        "sLD", 16)
                sec("sync", ldt0)
                ldt = [self.cnt["sLD"]]
                dv_a, pe_a, dv_p, pe_p, st_x = [], [], [], [], []
                for ti in range(LWa):
                    if ti + 1 < LWa:
                        def ldn(s, ti=ti):
                            if ti >= 1:
                                w(s, "sPE", pe_p[(ti - 1) * DC + DC - 1])
                                w(s, "sDV", dv_p[(ti - 1) * DC + DC - 1])
                            w(s, "sLD")
                            inc(s.dma_start(out=vload[(ti + 1) % 2][:, :, :],
                                            in_=vhr[:, :, (ti + 1) * TTa:(ti + 2) * TTa]),
                                "sLD", 16)
                            w(s, "sLD")
                            inc(s.dma_start(
                                out=xres[(ti + 1) % 2][:, :, :],
                                in_=xTr[:, :, t0 + (ti + 1) * TTa:t0 + (ti + 2) * TTa]),
                                "sLD", 16)
                        sec("sync", ldn)
                        ldt.append(self.cnt["sLD"])
                    self.wq_pump(10)
                    bi = ti % 2
                    for d1c in range(DC):
                        ai = ti * DC + d1c
                        def mma(t, ti=ti, bi=bi, d1c=d1c, ai=ai):
                            w(t, "sLD", ldt[ti])
                            w(t, "sDV", wT_dv_all)
                            if ai >= 2:
                                w(t, "sDV", dv_a[ai - 2])
                            for d2c in range(DC):
                                inc(t.matmul(ps_a[d1c % 2][:, :],
                                             wT[:, d2c, d1c * 128:(d1c + 1) * 128],
                                             vload[bi][:, d2c, :],
                                             start=(d2c == 0),
                                             stop=(d2c == DC - 1)), "sPE")
                        sec("tensor", mma)
                        pe_a.append(self.cnt["sPE"])
                        def eva(v, ti=ti, d1c=d1c, ai=ai):
                            w(v, "sPE", pe_a[ai])
                            w(v, "sDV", soft_dv_all)
                            inc(v.tensor_scalar(out=aT[ti % 2][:, d1c, :],
                                                in0=ps_a[d1c % 2][:, :],
                                                scalar1=recip[:, d1c:d1c + 1],
                                                scalar2=None, op0=Op.mult), "sDV")
                        sec("vector", eva)
                        dv_a.append(self.cnt["sDV"])
                    dv_a_all = dv_a[ti * DC + DC - 1]
                    for oc in range(DC):
                        pi = ti * DC + oc
                        def mmp(t, ti=ti, oc=oc, pi=pi):
                            w(t, "sDV", dv_a_all)
                            if pi >= 2:
                                w(t, "sDV", dv_p[pi - 2])
                            for d1c in range(DC):
                                inc(t.matmul(ps_p[oc % 2][:, :],
                                             self.PW[:, d1c, oc * 128:(oc + 1) * 128],
                                             aT[ti % 2][:, d1c, :],
                                             start=(d1c == 0), stop=False), "sPE")
                            inc(t.matmul(ps_p[oc % 2][:, :],
                                         self.prow_b[:, oc * 128:(oc + 1) * 128],
                                         self.tokpj_b[:, t0 + ti * TTa:t0 + (ti + 1) * TTa],
                                         start=False, stop=True), "sPE")
                        sec("tensor", mmp)
                        pe_p.append(self.cnt["sPE"])
                        def evp(v, ti=ti, bi=bi, oc=oc, pi=pi):
                            w(v, "sPE", pe_p[pi])
                            w(v, "sLD", ldt[ti])
                            if pi >= 2:
                                w(v, "sST", st_x[pi - 2])
                            if ti >= 2 and oc == 0:
                                w(v, "sAC", ac_s3[ti - 2])
                            cb = t0 + ti * TTa
                            w(v, "sDV")
                            step = self.chain(v)
                            step(v.tensor_tensor(out=tproj[:, :],
                                                 in0=ps_p[oc % 2][:, :],
                                                 in1=self.alphab[:, cb:cb + TTa],
                                                 op=Op.mult))
                            step(v.tensor_tensor(out=tln[:, :],
                                                 in0=xres[bi][:, oc, :],
                                                 in1=self.r1pb[:, cb:cb + TTa],
                                                 op=Op.mult))
                            step(v.tensor_tensor(out=xout[ti % 2][:, oc, :],
                                                 in0=tproj[:, :], in1=tln[:, :],
                                                 op=Op.add))
                        sec("vector", evp)
                        dv_p.append(self.cnt["sDV"])
                        def sx(s_, ti=ti, oc=oc, pi=pi):
                            w(s_, "sDV", dv_p[pi])
                            w(s_, "sST")
                            inc(s_.dma_start(
                                out=x1r[:, oc, t0 + ti * TTa:t0 + (ti + 1) * TTa],
                                in_=xout[ti % 2][:, oc, :]), "sST", 16)
                        sec("scalar", sx)
                        st_x.append(self.cnt["sST"])

                    dv_w_list[ti] = dv_p[ti * DC + DC - 1]
                    def mnx3(v, ti=ti):
                        if ti >= 2:
                            w(v, "sPE", pe_mmx[ti - 2])
                        step = self.chain(v)
                        xv = xout[ti % 2][:, :, :].rearrange("p a b -> p b a")
                        step(v.tensor_reduce(out=cminW[ti % 2][:, :], in_=xv,
                                             axis=AX.X, op=Op.min))
                        step(v.tensor_reduce(out=cmaxW[ti % 2][:, :], in_=xv,
                                             axis=AX.X, op=Op.max))
                    sec("vector", mnx3)
                    dv_mnx[ti] = self.cnt["sDV"]
                    def stats_tail(k):
                        def mm3(t, k=k):
                            w(t, "sAC", ac_s3[k])
                            for j in range(TTa // 128):
                                col = k * (TTa // 128) + j
                                for c in range(DC):
                                    inc(t.matmul(ps_s3[:, col:col + 1],
                                                 xb3[:, c, j * 128:(j + 1) * 128],
                                                 self.ones_col_b[:, :],
                                                 start=(c == 0),
                                                 stop=(c == DC - 1)), "sPE")
                                for c in range(DC):
                                    inc(t.matmul(ps_q3[:, col:col + 1],
                                                 sq3[:, c, j * 128:(j + 1) * 128],
                                                 self.ones_col_b[:, :],
                                                 start=(c == 0),
                                                 stop=(c == DC - 1)), "sPE")
                        sec("tensor", mm3)
                        pe_s3mm[k] = self.cnt["sPE"]
                        for which, srcW in enumerate((cminW, cmaxW)):
                            for j in range(TTa // 128):
                                mx_id = 2 * (k * (TTa // 128) + j) + which
                                def trm(t, srcW=srcW, j=j, k=k, mx_id=mx_id):
                                    w(t, "sDV", dv_mnx[k])
                                    if mx_id >= 2:
                                        w(t, "sDV", dv_mmx[mx_id - 2])
                                    inc(t.transpose(
                                        ps_mmx[mx_id % 2][:, :],
                                        srcW[k % 2][:, j * 128:(j + 1) * 128],
                                        self.identf[:, :]), "sPE")
                                sec("tensor", trm)
                                pe_here = self.cnt["sPE"]
                                def rdm(v, which=which, j=j, k=k, mx_id=mx_id,
                                        pe_here=pe_here, b=b):
                                    w(v, "sPE", pe_here)
                                    col = b * (L // 128) + k * (TTa // 128) + j
                                    dst = (self.tm["xmin_tm"] if which == 0
                                           else self.tm["xmax_tm"])
                                    inc(v.tensor_reduce(
                                        out=dst[:, col:col + 1],
                                        in_=ps_mmx[mx_id % 2][:, :], axis=AX.X,
                                        op=(Op.min if which == 0 else Op.max)),
                                        "sDV")
                                sec("vector", rdm)
                                dv_mmx[mx_id] = self.cnt["sDV"]
                        pe_mmx[k] = self.cnt["sPE"]
                    if ti >= 1:
                        stats_tail(ti - 1)
                    def sq3f(s_, ti=ti, bi=bi):
                        w(s_, "sDV", dv_w_list[ti])
                        if ti >= 1:
                            w(s_, "sPE", pe_s3mm[ti - 1])
                        inc(s_.activation(out=xb3[:, :, :],
                                          in_=xout[ti % 2][:, :, :],
                                          func=Act.Copy), "sAC")
                        inc(s_.activation(out=sq3[:, :, :],
                                          in_=xout[ti % 2][:, :, :],
                                          func=Act.Square), "sAC")
                    sec("scalar", sq3f)
                    ac_s3[ti] = self.cnt["sAC"]

                stats_tail(LWa - 1)
                def ev3(v, b=b):
                    w(v, "sPE", pe_s3mm[LWa - 1])
                    c0s = b * (L // 128)
                    inc(v.tensor_copy(out=self.tm["sum3"][:, c0s:c0s + L // 128],
                                      in_=ps_s3[:, :]), "sDV")
                    inc(v.tensor_copy(out=self.tm["ssq3"][:, c0s:c0s + L // 128],
                                      in_=ps_q3[:, :]), "sDV")
                sec("vector", ev3)
            if b + 1 < self.NB:
                snapb = dict(self.cnt)
                def gj(g):
                    for s_ in ALL_SEMS:
                        if snapb[s_] > 0:
                            g.wait_ge(self.sems[s_], snapb[s_])
                sec("gpsimd", gj)
            else:
                self.join_all()
            if b == 0:
                self.wq_finalize()

    def phase2_stats3(self):
        nc, sec, inc, w = self.nc, self.sec, self.inc, self.w
        D, DC, T, NT, TT, NW = self.D, self.DC, self.T, self.NT, self.TT, self.NW
        tm = self.tm
        self.attn_es.close()

        self.ffn_es = contextlib.ExitStack()
        FE = self.ffn_es.enter_context
        self.open_pool(self.ffn_es)
        self.gmaxc = FE(nc.sbuf_tensor("gmaxc", [128, 4 * self.NW], F32))
        self.gminc = FE(nc.sbuf_tensor("gminc", [128, 4 * self.NW], F32))
        self.ab_es = contextlib.ExitStack()
        self.Ab = self.ab_es.enter_context(nc.sbuf_tensor("Ab", [128, T], F32))
        self.Bb = self.ab_es.enter_context(nc.sbuf_tensor("Bb", [128, T], F32))

        self.p3es = contextlib.ExitStack()
        self.w1q = self.p3es.enter_context(nc.sbuf_tensor("w1q", [128, DC, self.H],
                                                          BF16))
        def ldw1(s):
            w(s, "sWS", self.w1q_ready[0])
            w(s, "sWT", self.w1q_ready[1])
            for c in range(0, DC, 2):
                w(s, "sLD")
                inc(s.dma_start(out=self.w1q[:, c:c + 2, :],
                                in_=self.w1qhr[:, c:c + 2, :]), "sLD", 16)
        sec("sync", ldw1)
        self.ld_w1q = self.cnt["sLD"]
        self.x1f = [self.p3es.enter_context(
            nc.sbuf_tensor(f"p3x{i}", [128, DC, TT], F32)) for i in range(2)]

        with contextlib.ExitStack() as ses:
            S = ses.enter_context
            rowA = S(nc.sbuf_tensor("rowA2", [1, T], F32))
            rowB = S(nc.sbuf_tensor("rowB2", [1, T], F32))

            if True:
                rD = 1.0 / self.D
                def vm(v):
                    w(v, "sDV")
                    step = self.chain(v)
                    step(v.tensor_scalar(out=tm["m3"][:, :], in0=tm["sum3"][:, :],
                                         scalar1=rD, scalar2=None, op0=Op.mult))
                    step(v.tensor_scalar(out=tm["tmp1"][:, :], in0=tm["ssq3"][:, :],
                                         scalar1=rD, scalar2=None, op0=Op.mult))
                    step(v.tensor_tensor(out=tm["tmp0"][:, :], in0=tm["m3"][:, :],
                                         in1=tm["m3"][:, :], op=Op.mult))
                    step(v.tensor_tensor(out=tm["tmp1"][:, :], in0=tm["tmp1"][:, :],
                                         in1=tm["tmp0"][:, :], op=Op.subtract))
                    step(v.tensor_scalar(out=tm["tmp1"][:, :], in0=tm["tmp1"][:, :],
                                         scalar1=LN_EPS, scalar2=None, op0=Op.add))
                sec("vector", vm)
                self.sqrt_newton("tmp1", "tmp0")
                def vm2(v):
                    w(v, "sDV")
                    step = self.chain(v)
                    step(v.reciprocal(out=tm["r3"][:, :], in_=tm["tmp0"][:, :]))
                sec("vector", vm2)

                def vq(v):
                    w(v, "sDV")
                    step = self.chain(v)
                    step(v.tensor_tensor(out=tm["qmin_tm"][:, :], in0=tm["xmin_tm"][:, :],
                                         in1=tm["m3"][:, :], op=Op.subtract))
                    step(v.tensor_tensor(out=tm["qmin_tm"][:, :], in0=tm["qmin_tm"][:, :],
                                         in1=tm["r3"][:, :], op=Op.mult))
                    step(v.tensor_tensor(out=tm["qmax_tm"][:, :], in0=tm["xmax_tm"][:, :],
                                         in1=tm["m3"][:, :], op=Op.subtract))
                    step(v.tensor_tensor(out=tm["qmax_tm"][:, :], in0=tm["qmax_tm"][:, :],
                                         in1=tm["r3"][:, :], op=Op.mult))
                    step(v.tensor_reduce(out=tm["tmp1"][:, 0:1], in_=tm["qmin_tm"][:, :],
                                         axis=AX.X, op=Op.min))
                    step(v.tensor_scalar(out=tm["tmp1"][:, 0:1], in0=tm["tmp1"][:, 0:1],
                                         scalar1=-1.0, scalar2=None, op0=Op.mult))
                    step(v.tensor_reduce(out=tm["tmp1"][:, 1:2], in_=tm["qmax_tm"][:, :],
                                         axis=AX.X, op=Op.max))
                sec("vector", vq)
                self.finalize_minmax_cc(tm["tmp1"][:, 0:1], tm["tmp1"][:, 1:2])
                self.allreduce(self.cc_in1, self.cc_out1)
                self.quant_scale_math("s1", "zp1", "lo1", "hi1", "rs1")

            def vab(v):
                w(v, "sDV")
                step = self.chain(v)
                step(v.tensor_scalar(out=tm["tmp1"][:, :], in0=tm["r3"][:, :],
                                     scalar1=self.bc["rs1"][:, 0:1], scalar2=None,
                                     op0=Op.mult))
                step(v.tensor_tensor(out=tm["tmp2"][:, :], in0=tm["m3"][:, :],
                                     in1=tm["tmp1"][:, :], op=Op.mult))
                step(v.tensor_scalar(out=tm["tmp2"][:, :], in0=tm["tmp2"][:, :],
                                     scalar1=-1.0, scalar2=None, op0=Op.mult))
            sec("vector", vab)
            self.row_from_tm("tmp1", rowA, 0, f32=True)
            self.row_from_tm("tmp2", rowB, 0, f32=True)
            self.bcast_from_row(rowA, self.Ab)
            self.bcast_from_row(rowB, self.Bb)
            self.dv_ab = self.cnt["sDV"]

    def phase3_ffn1(self):
        nc, sec, inc, w = self.nc, self.sec, self.inc, self.w
        D, DC, H, HC, T, TT, NW = (self.D, self.DC, self.H, self.HC,
                                   self.T, self.TT, self.NW)
        sc = self.sc
        fes = self.p3es
        if True:
            FS = fes.enter_context
            def vv(v):
                w(v, "sDV")
                w(v, "sGD")
                inc(v.tensor_tensor(out=sc["s1sw1"][:, :], in0=sc["s1"][:, :],
                                    in1=sc["sw1"][:, :], op=Op.mult),
                    "sDV")
            sec("vector", vv)
            self.bcast_scalar("s1sw1")

            NBLK = 4 * NW
            gmaxc = self.gmaxc
            gminc = self.gminc
            x1f = self.x1f
            xqw = [FS(nc.sbuf_tensor(f"p3q{i}", [128, DC, TT], BF16))
                   for i in range(2)]
            qbuf = [FS(nc.sbuf_tensor(f"p3g{i}", [128, 8, TT], F32))
                    for i in range(2)]
            uq = [FS(nc.sbuf_tensor(f"p3u{i}", [128, TT], F32))
                  for i in range(2)]
            uqI = FS(nc.sbuf_tensor("p3uI", [128, TT], I32))
            ps_h = [FS(nc.psum_tensor(f"ps_h{i}", [128, TT], F32))
                    for i in range(4)]
            ld_w, dv_q, pe_h, pe_w, ac_h, gp_q, st_q = {}, {}, {}, {}, {}, {}, {}
            def ld0(s):
                w(s, "sST")
                w(s, "sLD")
                inc(s.dma_start(out=x1f[0][:, :, :], in_=self.x1r[:, :, 0:TT]),
                    "sLD", 16)
            sec("sync", ld0)
            ld_w[0] = self.cnt["sLD"]
            def emit_qz(tt):
                bi = tt % 2
                cb = tt * TT
                def qz(v, tt=tt, bi=bi, cb=cb):
                    w(v, "sLD", ld_w[tt])
                    if tt >= 2:
                        w(v, "sPE", pe_w[tt - 2])
                    step = self.chain(v)
                    for c in range(DC):
                        step(v.tensor_tensor(out=uq[0][:, :], in0=x1f[bi][:, c, :],
                                             in1=self.Ab[:, cb:cb + TT],
                                             op=Op.mult))
                        step(v.tensor_tensor(out=uq[1][:, :], in0=uq[0][:, :],
                                             in1=self.Bb[:, cb:cb + TT],
                                             op=Op.add))
                        step(v.tensor_copy(out=uqI[:, :], in_=uq[1][:, :]))
                        step(v.tensor_scalar(out=xqw[bi][:, c, :], in0=uqI[:, :],
                                             scalar1=self.bc["lo1"][:, 0:1],
                                             scalar2=self.bc["hi1"][:, 0:1],
                                             op0=Op.max, op1=Op.min))
                sec("vector", qz)
                dv_q[tt] = self.cnt["sDV"]

            emit_qz(0)
            for tt in range(NW):
                if tt + 1 < NW:
                    def ldn(s, tt=tt):
                        if tt >= 1:
                            w(s, "sDV", dv_q[tt - 1])
                        w(s, "sLD")
                        inc(s.dma_start(
                            out=x1f[(tt + 1) % 2][:, :, :],
                            in_=self.x1r[:, :, (tt + 1) * TT:(tt + 2) * TT]),
                            "sLD", 16)
                    sec("sync", ldn)
                    ld_w[tt + 1] = self.cnt["sLD"]
                    emit_qz(tt + 1)
                bi = tt % 2
                for hc in range(HC):
                    gidx = tt * HC + hc
                    bank = gidx % 4
                    q = tt * 4 + hc // 8
                    def mmh(t, tt=tt, bi=bi, hc=hc, bank=bank, gidx=gidx):
                        if gidx == 0:
                            w(t, "sLD", self.ld_w1q)
                        w(t, "sDV", dv_q[tt])
                        if gidx >= 4:
                            w(t, "sAC", ac_h[gidx - 4])
                        for c in range(DC):
                            inc(t.matmul(ps_h[bank][:, :],
                                         self.w1q[:, c, hc * 128:(hc + 1) * 128],
                                         xqw[bi][:, c, :],
                                         start=(c == 0), stop=(c == DC - 1)),
                                "sPE")
                    sec("tensor", mmh)
                    pe_h[gidx] = self.cnt["sPE"]
                    def evh(s_, hc=hc, bank=bank, gidx=gidx, q=q):
                        w(s_, "sPE", pe_h[gidx])
                        if q >= 2 and hc % 8 == 0:
                            w(s_, "sGD", st_q[q - 2])
                            w(s_, "sDV", gp_q[q - 2])
                        inc(s_.activation(out=qbuf[q % 2][:, hc % 8, :],
                                          in_=ps_h[bank][:, :], func=Act.Gelu,
                                          bias=self.b1cols[:, hc:hc + 1],
                                          scale=self.bc["s1sw1"][:, 0:1]), "sAC")
                    sec("scalar", evh)
                    ac_h[gidx] = self.cnt["sAC"]
                    if hc % 8 == 7:
                        def mnx(v, gidx=gidx, q=q):
                            w(v, "sAC", ac_h[gidx])
                            sl = qbuf[q % 2][:, :, :]
                            inc(v.tensor_reduce(out=gmaxc[:, q:q + 1],
                                                in_=sl, axis=AX.XY, op=Op.max),
                                "sDV")
                            v.wait_ge(self.sems["sDV"], self.cnt["sDV"])
                            inc(v.tensor_reduce(out=gminc[:, q:q + 1],
                                                in_=sl, axis=AX.XY, op=Op.min),
                                "sDV")
                        sec("vector", mnx)
                        gp_q[q] = self.cnt["sDV"]
                        def sth(g, tt=tt, hc=hc, gidx=gidx, q=q):
                            w(g, "sAC", ac_h[gidx])
                            w(g, "sGD")
                            inc(g.dma_start(
                                out=self.gh[:, tt, hc - 7:hc + 1, :],
                                in_=qbuf[q % 2][:, :, :]), "sGD", 16)
                        sec("gpsimd", sth)
                        st_q[q] = self.cnt["sGD"]
                pe_w[tt] = self.cnt["sPE"]

            def gred(v):
                w(v, "sDV")
                step = self.chain(v)
                step(v.tensor_reduce(out=self.tm["tmp1"][:, 0:1],
                                     in_=gminc[:, 0:NBLK], axis=AX.X, op=Op.min))
                step(v.tensor_scalar(out=self.tm["tmp1"][:, 0:1],
                                     in0=self.tm["tmp1"][:, 0:1],
                                     scalar1=-1.0, scalar2=None, op0=Op.mult))
                step(v.tensor_reduce(out=self.tm["tmp1"][:, 1:2],
                                     in_=gmaxc[:, 0:NBLK], axis=AX.X, op=Op.max))
            sec("vector", gred)
            self.finalize_minmax_cc(self.tm["tmp1"][:, 0:1], self.tm["tmp1"][:, 1:2])
            snap = dict(self.cnt)
            self.allreduce(self.cc_in2, self.cc_out2)
            def mkj():
                def f(e):
                    for s_ in ("sLD", "sST", "sPE", "sDV", "sAC", "sGD", "sGP"):
                        if snap[s_] > 0:
                            e.wait_ge(self.sems[s_], snap[s_])
                return f
            for e in ("sync", "scalar", "vector", "tensor"):
                sec(e, mkj())
        fes.close()
        self.ab_es.close()

    def phase4_ffn2(self):
        nc, sec, inc, w = self.nc, self.sec, self.inc, self.w
        D, DC, H, HC, T, TT, NW = (self.D, self.DC, self.H, self.HC,
                                   self.T, self.TT, self.NW)
        sc = self.sc

        with contextlib.ExitStack() as fes:
            FS = fes.enter_context
            w2q = FS(nc.sbuf_tensor("w2q", [128, HC, D], BF16))
            b2row = FS(nc.sbuf_tensor("b2row4", [1, D], F32))
            b2p_rb = FS(nc.sbuf_tensor("b2p_rb", [1, D], BF16))
            orow = FS(nc.sbuf_tensor("orowb", [1, 512], BF16))

            def ldw2(s):
                w(s, "sWS", self.w2q_ready[0])
                w(s, "sWT", self.w2q_ready[1])
                for c in range(0, HC, 8):
                    w(s, "sLD")
                    inc(s.dma_start(out=w2q[:, c:c + 8, :],
                                    in_=self.w2qhw[:, c:c + 8, :]), "sLD", 16)
            sec("sync", ldw2)
            ld_w2q = self.cnt["sLD"]

            self.quant_scale_math("s2", "zp2", "lo2", "hi2", "rs2")

            def ldb2(s):
                w(s, "sLD")
                inc(s.dma_start(out=b2row[:, :], in_=self.b2r_h), "sLD", 16)
            sec("sync", ldb2)
            ldb = self.cnt["sLD"]
            def vv(v):
                w(v, "sDV")
                w(v, "sGD")
                w(v, "sLD", ldb)
                step = self.chain(v)
                step(v.tensor_tensor(out=sc["s2sw2"][:, :], in0=sc["s2"][:, :],
                                     in1=sc["sw2"][:, :], op=Op.mult))
                step(v.tensor_tensor(out=sc["t0"][:, :], in0=sc["rs2"][:, :],
                                     in1=sc["rsw2"][:, :], op=Op.mult))
                step(v.tensor_scalar(out=b2row[:, :], in0=b2row[:, :],
                                     scalar1=sc["t0"][:, 0:1], scalar2=None,
                                     op0=Op.mult))
                step(v.tensor_copy(out=b2p_rb[:, :], in_=b2row[:, :]))
                inc(v.memset(orow[:, :], 1.0), "sDV")
            sec("vector", vv)
            self.bcast_scalar("s2sw2")

            HH = HC // 2
            TTs = 256
            gf = [FS(nc.sbuf_tensor(f"gf{i}", [128, HH, TT], F32))
                  for i in range(2)]
            hq = [FS(nc.sbuf_tensor(f"hq{i}", [128, HC, TTs], BF16))
                  for i in range(2)]
            xres = [FS(nc.sbuf_tensor(f"x2res{i}", [128, DC, TTs], F32))
                    for i in range(2)]
            oout = FS(nc.sbuf_tensor("oout", [128, DC, TTs], F32))
            gfI = FS(nc.sbuf_tensor("gfI", [128, 8, TTs], I32))
            ps_o = [FS(nc.psum_tensor(f"ps_o{i}", [128, TTs], F32))
                    for i in range(2)]

            ld_half, ac_half, dv_clast, dv_csub = {}, {}, {}, {}
            ld_x, pe_o, dv_o, st_o, pe_sub = {}, [], [], [], {}
            dv_rs2 = self.cnt["sDV"]

            def emit_loads(tt):
                for half in range(2):
                    hid = tt * 2 + half
                    def ldh(s, tt=tt, half=half, hid=hid):
                        w(s, "sLD")
                        if hid >= 2:
                            w(s, "sDV", dv_clast[hid - 2])
                        inc(s.dma_start(
                            out=gf[hid % 2][:, :, :],
                            in_=self.gh[:, tt, half * HH:(half + 1) * HH, :]),
                            "sLD", 16)
                    sec("sync", ldh)
                    ld_half[hid] = self.cnt["sLD"]
                for half in range(2):
                    hid = tt * 2 + half
                    def rnd(s_, hid=hid):
                        w(s_, "sLD", ld_half[hid])
                        w(s_, "sDV", dv_rs2)
                        inc(s_.activation(out=gf[hid % 2][:, :, :],
                                          in_=gf[hid % 2][:, :, :], func=Act.Copy,
                                          bias=0.0,
                                          scale=self.bc["rs2"][:, 0:1]), "sAC")
                    sec("scalar", rnd)
                    ac_half[hid] = self.cnt["sAC"]

            def emit_clamp(tt, sub):
                s_id = tt * 2 + sub
                for half in range(2):
                    hid = tt * 2 + half
                    def clp(v, sub=sub, half=half, hid=hid, s_id=s_id):
                        w(v, "sAC", ac_half[hid])
                        if s_id >= 2 and half == 0:
                            w(v, "sPE", pe_sub[s_id - 2])
                        step = self.chain(v)
                        for p in range(2):
                            sl = gf[hid % 2][:, p * 8:(p + 1) * 8,
                                             sub * TTs:(sub + 1) * TTs]
                            hsl = hq[sub][:, half * HH + p * 8:
                                          half * HH + (p + 1) * 8, :]
                            step(v.tensor_copy(out=gfI[:, :, :], in_=sl))
                            step(v.tensor_scalar(out=hsl, in0=gfI[:, :, :],
                                                 scalar1=self.bc["lo2"][:, 0:1],
                                                 scalar2=self.bc["hi2"][:, 0:1],
                                                 op0=Op.max, op1=Op.min))
                    sec("vector", clp)
                    dv_csub[(tt, sub)] = self.cnt["sDV"]
                    if sub == 1:
                        dv_clast[tt * 2 + half] = self.cnt["sDV"]

            emit_loads(0)
            emit_clamp(0, 0)
            emit_clamp(0, 1)
            for tt in range(NW):
                for sub in range(2):
                    s_id = tt * 2 + sub
                    def ldx(s, s_id=s_id):
                        w(s, "sLD")
                        if s_id >= 2:
                            w(s, "sDV", dv_o[(s_id - 2) * DC + DC - 1])
                        inc(s.dma_start(
                            out=xres[s_id % 2][:, :, :],
                            in_=self.x1r[:, :, s_id * TTs:(s_id + 1) * TTs]),
                            "sLD", 16)
                    sec("sync", ldx)
                    ld_x[s_id] = self.cnt["sLD"]
                    for dc in range(DC):
                        oi = s_id * DC + dc
                        def mmo(t, tt=tt, sub=sub, dc=dc, oi=oi):
                            if oi == 0:
                                w(t, "sLD", ld_w2q)
                            w(t, "sDV", dv_csub[(tt, sub)])
                            if oi >= 2:
                                w(t, "sDV", dv_o[oi - 2])
                            for hc in range(HC):
                                inc(t.matmul(ps_o[dc % 2][:, :],
                                             w2q[:, hc, dc * 128:(dc + 1) * 128],
                                             hq[sub][:, hc, :],
                                             start=(hc == 0), stop=False), "sPE")
                            inc(t.matmul(ps_o[dc % 2][:, :],
                                         b2p_rb[:, dc * 128:(dc + 1) * 128],
                                         orow[:, 0:TTs],
                                         start=False, stop=True), "sPE")
                        sec("tensor", mmo)
                        pe_o.append(self.cnt["sPE"])
                        def evo(v, s_id=s_id, dc=dc, oi=oi):
                            w(v, "sPE", pe_o[oi])
                            w(v, "sLD", ld_x[s_id])
                            if oi >= 8:
                                w(v, "sGD", st_o[oi - 8])
                            inc(v.scalar_tensor_tensor(
                                out=oout[:, dc, :],
                                in0=ps_o[dc % 2][:, :],
                                scalar=self.bc["s2sw2"][:, 0:1],
                                in1=xres[s_id % 2][:, dc, :],
                                op0=Op.mult, op1=Op.add), "sDV")
                        sec("vector", evo)
                        dv_o.append(self.cnt["sDV"])
                        def sto(g, s_id=s_id, dc=dc, oi=oi):
                            w(g, "sDV", dv_o[oi])
                            w(g, "sGD")
                            inc(g.dma_start(
                                out=self.outr[:, dc, s_id * TTs:(s_id + 1) * TTs],
                                in_=oout[:, dc, :]), "sGD", 16)
                        sec("gpsimd", sto)
                        st_o.append(self.cnt["sGD"])
                    pe_sub[s_id] = self.cnt["sPE"]
                    if sub == 0 and tt + 1 < NW:
                        emit_loads(tt + 1)
                        emit_clamp(tt + 1, 0)
                    if sub == 1 and tt + 1 < NW:
                        emit_clamp(tt + 1, 1)
            self.join_all()
        self.ffn_es.close()

    def finish(self):
        sec, w = self.sec, self.w
        def fw(s_):
            w(s_, "sST")
            w(s_, "sLD")
            w(s_, "sGD")
        sec("scalar", fw)


def host_pack(inputs, L, NB, n_cores):
    x = np.asarray(inputs["L0_attn_x"], dtype=np.float32)
    B = x.shape[0]
    assert B == NB * n_cores
    ident_b = np.eye(128, dtype=ml_dtypes.bfloat16)
    ident_f = np.eye(128, dtype=np.float32)
    shared = {
        "qkvwT": np.ascontiguousarray(np.asarray(inputs["L0_attn_qkv_w"], np.float32).T),
        "projwT": np.ascontiguousarray(np.asarray(inputs["L0_attn_proj_w"], np.float32).T),
        "w1T": np.ascontiguousarray(np.asarray(inputs["L0_ffn_W1"], np.float32).T),
        "w2T": np.ascontiguousarray(np.asarray(inputs["L0_ffn_W2"], np.float32).T),
        "nwr": np.asarray(inputs["L0_attn_norm_w"], np.float32).reshape(1, -1),
        "nbr": np.asarray(inputs["L0_attn_norm_b"], np.float32).reshape(1, -1),
        "qkvbr": np.asarray(inputs["L0_attn_qkv_b"], np.float32).reshape(1, -1),
        "projbr": np.asarray(inputs["L0_attn_proj_b"], np.float32).reshape(1, -1),
        "b1r": np.asarray(inputs["L0_ffn_b1"], np.float32).reshape(1, -1),
        "b2r": np.asarray(inputs["L0_ffn_b2"], np.float32).reshape(1, -1),
        "identb": ident_b,
        "identf": ident_f,
    }
    in_maps = []
    for c in range(n_cores):
        xc = x[c * NB:(c + 1) * NB].reshape(NB * L, x.shape[2])
        in_maps.append(dict(shared, xT=np.ascontiguousarray(xc.T)))
    return in_maps


def host_unpack(results, L, NB, n_cores, D=1024):
    outs = []
    for c in range(n_cores):
        o = np.asarray(results[c]["outT"])
        outs.append(np.ascontiguousarray(o.T).reshape(NB, L, D))
    return np.concatenate(outs, axis=0)


def kernel(**inputs):
    from concourse.bass_utils import run_bass_kernel_spmd
    x = np.asarray(inputs["L0_attn_x"])
    B, L, D = x.shape
    assert (B, L, D) == (16, 2048, 1024), (B, L, D)
    n_cores = 8
    NB = B // n_cores
    bld = Builder(L, NB, TT=512, n_cores=n_cores)
    nc = bld.build()
    in_maps = host_pack(inputs, L, NB, n_cores)
    res = run_bass_kernel_spmd(nc, in_maps, core_ids=list(range(n_cores)))
    return host_unpack(res.results, L, NB, n_cores, D)
